# version 01
# baseline (speedup 1.0000x reference)
import math
from contextlib import ExitStack

import numpy as np

import concourse.bass as bass
import concourse.mybir as mybir
from concourse.bass_utils import run_bass_kernel_spmd

F32 = mybir.dt.float32
BF16 = mybir.dt.bfloat16
AF = mybir.ActivationFunctionType
ALU = mybir.AluOpType
AX = mybir.AxisListType

H = 2048
NH = 16
HD = 128
S = 2048
NS = 2
T = NS * S
INTER = 5632
RQ = 1536
RKV = 512
KC = H // 128
SCALE = 1.0 / math.sqrt(HD)
ROT = 32
EPS = 1e-5
NEG = -1.0e30
_BASE = 2.0 ** (-2.0 ** (-(math.log2(NH) - 3)))
SLOPES = [_BASE ** i for i in range(1, NH + 1)]

SUBLAYERS = 4
DEBUG = False
HEADS = 16
NO_H1_LOADS = False
H1_STAGE = 9
NO_H1_PREP = False

ENGS = ("tensor", "vector", "scalar", "gpsimd", "sync")


class PB:
    def __init__(self):
        self.q = {e: [] for e in ENGS}
        self.cnt = {}

    def _sem(self, s):
        if s not in self.cnt:
            self.cnt[s] = 0

    def op(self, eng, fn, inc=None, n=1):
        if inc is not None:
            self._sem(inc)
            self.cnt[inc] += n
            self.q[eng].append(("op", fn, inc, n))
            return (inc, self.cnt[inc])
        self.q[eng].append(("op", fn, None, 0))
        return None

    def wait(self, eng, token):
        if token is None:
            return
        s, v = token
        if v > 0:
            self.q[eng].append(("wait", s, v))

    def tok(self, sem):
        self._sem(sem)
        return (sem, self.cnt[sem])

    def waitc(self, eng, sem):
        self.wait(eng, self.tok(sem))

    def barrier(self):
        toks = {}
        for e in ENGS:
            toks[e] = self.op(e, lambda en: en.nop(), inc="bar_" + e)
        for e in ENGS:
            for e2 in ENGS:
                if e2 != e:
                    self.wait(e, toks[e2])

    def dma(self, eng, dst, src, sem):
        return self.op(eng, lambda e, d=dst, s=src: e.dma_start(d, s), inc=sem, n=16)


def _pack_lhsT(W):
    Kt, Mt = W.shape
    nk, nm = Kt // 128, Mt // 128
    w = W.reshape(nk, 128, nm, 128)
    return np.ascontiguousarray(w.transpose(2, 1, 0, 3).reshape(nm, 128, nk * 128))


def _host_inputs(inputs, core, cache={}):
    f32 = np.float32
    if not cache:
        W_qkv = np.asarray(inputs["L0_attn_a_W_qkv"], f32)
        b_qkv = np.asarray(inputs["L0_attn_a_b_qkv"], f32)
        W_d = np.asarray(inputs["L0_attn_a_W_dense"], f32)
        b_d = np.asarray(inputs["L0_attn_a_b_dense"], f32)
        qcols = np.concatenate([np.arange(h * 384, h * 384 + 128) for h in range(NH)])
        kcols = qcols + 128
        vcols = qcols + 256
        c = cache
        c["wq"] = _pack_lhsT(W_qkv[:, qcols])
        c["wk"] = _pack_lhsT(W_qkv[:, kcols])
        c["wv"] = np.ascontiguousarray(W_qkv[:, vcols].reshape(KC, 128, H))
        c["bq"] = np.ascontiguousarray(b_qkv[qcols].reshape(NH, 128).T)
        c["bk"] = np.ascontiguousarray(b_qkv[kcols].reshape(NH, 128).T)
        c["bv"] = np.ascontiguousarray(b_qkv[vcols].reshape(1, H))
        c["wd"] = _pack_lhsT(W_d)
        c["bd"] = np.ascontiguousarray(b_d.reshape(NH, 128).T)
        Wqd = np.asarray(inputs["L0_attn_b_Wq_down"], f32)
        Wqu = np.asarray(inputs["L0_attn_b_Wq_up"], f32)
        Wkd = np.asarray(inputs["L0_attn_b_Wkv_down"], f32)
        kup = np.asarray(inputs["L0_attn_b_k_up"], f32)
        vup = np.asarray(inputs["L0_attn_b_v_up"], f32)
        Wo = np.asarray(inputs["L0_attn_b_Wo"], f32)
        c["wqd"] = _pack_lhsT(np.ascontiguousarray(Wqd.T))
        c["wqu"] = _pack_lhsT(np.ascontiguousarray(Wqu.T))
        c["wkd"] = _pack_lhsT(np.ascontiguousarray(Wkd.T))
        c["kup"] = np.ascontiguousarray(
            np.stack([_pack_lhsT(kup[h])[0] for h in range(NH)]))
        c["vup"] = np.ascontiguousarray(
            vup.transpose(1, 0, 2).reshape(4, 128, NH * HD))
        c["wo"] = _pack_lhsT(np.ascontiguousarray(Wo.T))
        gw = np.asarray(inputs["L0_ffn_gate_w"], f32)
        uw = np.asarray(inputs["L0_ffn_up_w"], f32)
        dwn = np.asarray(inputs["L0_ffn_down_w"], f32)
        c["gw"] = _pack_lhsT(np.ascontiguousarray(gw.T))
        c["uw"] = _pack_lhsT(np.ascontiguousarray(uw.T))
        c["dw"] = _pack_lhsT(np.ascontiguousarray(dwn.T))
        c["gs"] = np.ascontiguousarray(
            np.asarray(inputs["L0_ffn_gate_s"], f32).reshape(44, 128).T)
        c["us"] = np.ascontiguousarray(
            np.asarray(inputs["L0_ffn_up_s"], f32).reshape(44, 128).T)
        c["ds"] = np.ascontiguousarray(
            np.asarray(inputs["L0_ffn_down_s"], f32).reshape(16, 128).T)
        c["ident"] = np.eye(128, dtype=f32)
        c["epsc"] = np.full((128, 1), EPS, f32)
        c["onesk1"] = np.ones((1, 128), f32)
        c["onesm"] = np.full((128, 1), 1.0 / H, f32)
        iu = np.arange(128)
        c["upmask"] = np.where(iu[None, :] > iu[:, None], np.float32(NEG), 0.0).astype(f32)
        c["lowmask"] = np.where(iu[:, None] > iu[None, :], np.float32(NEG), 0.0).astype(f32)
        c["ramp"] = np.ascontiguousarray(
            np.broadcast_to(np.arange(S, dtype=f32), (128, S)))
        c["qpos"] = (np.arange(16)[None, :] * 128 + iu[:, None]).astype(f32)
        c["dconst"] = (iu[:, None] - np.arange(512)[None, :]).astype(f32)
        c["dn16"] = np.broadcast_to((-128.0 * np.arange(16, dtype=f32))[None, :],
                                    (128, 16)).copy()
        R = np.zeros((128, 128), f32)
        for i in range(16):
            R[i, i + 16] = -1.0
            R[i + 16, i] = 1.0
        c["rt"] = np.ascontiguousarray(R.T)
        inv = 1.0 / (10000.0 ** (np.arange(0, ROT, 2, dtype=f32) / ROT))
        tv = np.arange(S, dtype=f32)
        fr = np.outer(tv, inv)
        emb = np.concatenate([fr, fr], axis=1).T
        cos = np.cos(emb).astype(f32)
        sin = np.sin(emb).astype(f32)
        cq = np.full((128, S), SCALE, f32); cq[:ROT] = cos * SCALE
        sq = np.zeros((128, S), f32);       sq[:ROT] = sin * SCALE
        ck = np.ones((128, S), f32);        ck[:ROT] = cos
        sk = np.zeros((128, S), f32);       sk[:ROT] = sin
        c["tbl_cq"], c["tbl_sq"], c["tbl_ck"], c["tbl_sk"] = cq, sq, ck, sk
    m = dict(cache)
    hs = np.asarray(inputs["L0_attn_a_hidden_states"], np.float32)
    hc = hs[2 * core: 2 * core + 2]
    m["x0"] = np.ascontiguousarray(hc.transpose(2, 0, 1).reshape(KC, 128, T))
    return m


def build_program():
    nc = bass.Bass("TRN2", target_bir_lowering=False)
    D = {}

    def din(name, shape, dt=F32):
        D[name] = nc.dram_tensor(name, list(shape), dt, kind="ExternalInput").ap()

    din("x0", (KC, 128, T))
    din("wq", (NH, 128, H)); din("wk", (NH, 128, H)); din("wv", (KC, 128, H))
    din("bq", (128, NH)); din("bk", (128, NH)); din("bv", (1, H))
    din("wd", (KC, 128, H)); din("bd", (128, NH))
    din("wqd", (12, 128, H)); din("wqu", (NH, 128, RQ)); din("wkd", (4, 128, H))
    din("kup", (NH, 128, RKV)); din("vup", (4, 128, H)); din("wo", (KC, 128, H))
    din("gw", (44, 128, H)); din("uw", (44, 128, H)); din("dw", (KC, 128, INTER))
    din("gs", (128, 44)); din("us", (128, 44)); din("ds", (128, KC))
    din("ident", (128, 128)); din("onesk1", (1, 128)); din("onesm", (128, 1))
    din("epsc", (128, 1))
    din("upmask", (128, 128)); din("lowmask", (128, 128))
    din("ramp", (128, S)); din("qpos", (128, 16)); din("rt", (128, 128))
    din("dconst", (128, 512)); din("dn16", (128, 16))
    for n in ("tbl_cq", "tbl_sq", "tbl_ck", "tbl_sk"):
        din(n, (128, S))

    D["out"] = nc.dram_tensor("out", [KC, 128, T], F32, kind="ExternalOutput").ap()
    if DEBUG:
        D["dbg_xln"] = nc.dram_tensor("dbg_xln", [KC, 128, S], BF16, kind="ExternalOutput").ap()
        D["dbg_qt"] = nc.dram_tensor("dbg_qt", [128, S], BF16, kind="ExternalOutput").ap()
        D["dbg_kt"] = nc.dram_tensor("dbg_kt", [128, S], BF16, kind="ExternalOutput").ap()
        D["dbg_ao"] = nc.dram_tensor("dbg_ao", [NH, 128, S], BF16, kind="ExternalOutput").ap()
        D["dbg_st"] = nc.dram_tensor("dbg_st", [4, 128, 16], F32, kind="ExternalOutput").ap()
        D["dbg_nmg"] = nc.dram_tensor("dbg_nmg", [128, 1], F32, kind="ExternalOutput").ap()
        D["dbg_fbc"] = nc.dram_tensor("dbg_fbc", [128, S], F32, kind="ExternalOutput").ap()
        D["dbg_ao0"] = nc.dram_tensor("dbg_ao0", [128, S], BF16, kind="ExternalOutput").ap()
    for n, dt in (("xA", F32), ("xB", F32), ("lnA", BF16), ("lnB", BF16)):
        D[n] = nc.dram_tensor(n, [KC, 128, T], dt, kind="Internal").ap()
    D["ao"] = nc.dram_tensor("ao", [NS, NH, 128, S], BF16, kind="Internal").ap()
    D["gwb"] = nc.dram_tensor("gwb", [44, 128, H], BF16, kind="Internal").ap()
    D["uwb"] = nc.dram_tensor("uwb", [44, 128, H], BF16, kind="Internal").ap()
    D["dwb"] = nc.dram_tensor("dwb", [KC, 128, INTER], BF16, kind="Internal").ap()

    pb = PB()

    def sb(name, shape, dt):
        return nc.alloc_sbuf_tensor(name, list(shape), dt)

    IDENT = sb("IDENT", (128, 128), F32)
    ONESK1 = sb("ONESK1", (1, 128), F32)
    ONESM = sb("ONESM", (128, 1), F32)
    EPSC = sb("EPSC", (128, 1), F32)
    UPM = sb("UPM", (128, 128), F32)
    LOWM = sb("LOWM", (128, 128), F32)
    QPOS = sb("QPOS", (128, 16), F32)
    DN16 = sb("DN16", (128, 16), F32)
    RT = sb("RT", (128, 128), F32)
    BQ = sb("BQ", (128, 16), F32)
    BK = sb("BK", (128, 16), F32)
    BD = sb("BD", (128, 16), F32)
    BVROW = sb("BVROW", (1, H), F32)
    GS = sb("GS", (128, 44), F32)
    US = sb("US", (128, 44), F32)
    DS = sb("DS", (128, 16), F32)
    NEGM = sb("NEGM", (128, 16), F32)
    SACC = sb("SACC", (128, 16), F32)
    LNSB = sb("LNSB", (128, 16), F32)
    GTB = sb("GTB", (128, 16), F32)
    FQT = sb("FQT", (128, 16), F32)
    KB = sb("KB", (128, 16), F32)
    QPS = sb("QPS", (128, 16), F32)
    NM1 = sb("NM1", (128, 1), F32)
    NMG = sb("NMG", (128, 1), F32)
    TROW = sb("TROW", (1, 128), F32)
    MGR = sb("MGR", (1, 1), F32)
    FROW = sb("FROW", (1, S), F32)
    SROW = sb("SROW", (1, 512), F32)
    SROW2 = sb("SROW2", (1, 512), F32)
    ATN = tuple(sb(f"ATN{i}", (128, 512), BF16) for i in range(2))
    PREP = tuple(sb(f"PREP{i}", (128, H), BF16) for i in range(2))

    PS_A = nc.alloc_psum_tensor("PS_A", [128, 2048], F32)
    PS_B = nc.alloc_psum_tensor("PS_B", [128, 2048], F32)

    class Scope:
        def __init__(self):
            self.base = nc.sbuf_base
        def sb(self, name, shape, dt):
            return nc.alloc_sbuf_tensor(name, list(shape), dt)
        def close(self):
            nc.sbuf_base = self.base

    for dst, src in [(IDENT, "ident"), (ONESK1, "onesk1"), (ONESM, "onesm"),
                     (UPM, "upmask"), (LOWM, "lowmask"), (EPSC, "epsc"),
                     (QPOS, "qpos"), (RT, "rt"), (DN16, "dn16"),
                     (BQ, "bq"), (BK, "bk"),
                     (BD, "bd"), (BVROW, "bv"), (GS, "gs"), (US, "us"), (DS, "ds")]:
        pb.dma("sync", dst[:, :], D[src][:, :], "d_const")
    tok_const = pb.tok("d_const")
    for e in ("tensor", "vector", "scalar"):
        pb.wait(e, tok_const)

    prep_jobs = []
    for m in range(44):
        prep_jobs.append((D["gw"][m], D["gwb"][m], H))
        prep_jobs.append((D["uw"][m], D["uwb"][m], H))
    for f in range(KC):
        for c0 in range(0, INTER, 2048):
            w = min(2048, INTER - c0)
            prep_jobs.append((D["dw"][f, :, c0:c0 + w], D["dwb"][f, :, c0:c0 + w], w))
    prep_state = {"i": 0, "tok": [None, None]}

    def prep_step(k=2):
        for _ in range(k):
            i = prep_state["i"]
            if i >= len(prep_jobs):
                return
            slot = i % 2
            src, dst, w = prep_jobs[i]
            pb.wait("gpsimd", prep_state["tok"][slot])
            tl = pb.dma("gpsimd", PREP[slot][:, 0:w], src, f"d_prep{slot}")
            pb.wait("sync", tl)
            ts = pb.dma("sync", dst, PREP[slot][:, 0:w], f"d_preps{slot}")
            prep_state["tok"][slot] = ts
            prep_state["i"] = i + 1

    def prep_finish():
        while prep_state["i"] < len(prep_jobs):
            prep_step(8)

    def mm_loop(eng_specs, final_inc="pe"):
        tok = None
        nsp = len(eng_specs)
        for i, (o, l, r, st, sp) in enumerate(eng_specs):
            inc = final_inc if i == nsp - 1 else None
            tok = pb.op("tensor", lambda e, o=o, l=l, r=r, st=st, sp=sp:
                        e.matmul(o, l, r, start=st, stop=sp, skip_group_check=True),
                        inc=inc)
        return tok

    STORE_SEMS = ("d_dxn0", "d_dxn1", "d_dxl", "d_fxn0", "d_fxn1", "d_fxl",
                  "d_ao", "d_fx20", "d_fx21", "d_dx20", "d_dx21")

    def hbm_fence(eng="sync"):
        for s_ in STORE_SEMS:
            pb.waitc(eng, s_)

    def ln_finalize(ps_sum, ps_sq, Mbc, Rbc, MRt, width, tok_pe_stats, psA_off):
        pb.wait("scalar", tok_pe_stats)
        pb.waitc("scalar", "pe")
        pb.op("scalar", lambda e, o=SROW[0:1, 0:width], i=ps_sum:
              e.activation(o, i, AF.Copy), inc="act")
        t2 = pb.op("scalar", lambda e, o=SROW2[0:1, 0:width], i=ps_sq:
                   e.activation(o, i, AF.Copy), inc="act")
        pb.wait("tensor", t2)
        pb.waitc("tensor", "dve")
        bm = pb.op("tensor", lambda e, o=PS_A[:, psA_off:psA_off + width],
                   l=ONESK1[:, :], r=SROW[0:1, 0:width]:
                   e.matmul(o, l, r, start=True, stop=True, skip_group_check=True),
                   inc="pe")
        bq_ = pb.op("tensor", lambda e, o=PS_A[:, psA_off + width:psA_off + 2 * width],
                    l=ONESK1[:, :], r=SROW2[0:1, 0:width]:
                    e.matmul(o, l, r, start=True, stop=True, skip_group_check=True),
                    inc="pe")
        pb.wait("scalar", bq_)
        pb.op("scalar", lambda e, o=Mbc[:, 0:width], i=PS_A[:, psA_off:psA_off + width]:
              e.activation(o, i, AF.Copy), inc="act")
        tsq = pb.op("scalar", lambda e, o=MRt[:, 0:width],
                    i=PS_A[:, psA_off:psA_off + width]:
                    e.activation(o, i, AF.Square), inc="act")
        pb.wait("vector", tsq)
        tv = pb.op("vector", lambda e, o=Rbc[:, 0:width],
                   a=PS_A[:, psA_off + width:psA_off + 2 * width], b=MRt[:, 0:width]:
                   e.tensor_tensor(o, a, b, ALU.subtract), inc="dve")
        pb.wait("scalar", tv)
        pb.op("scalar", lambda e, o=Rbc[:, 0:width], i=Rbc[:, 0:width]:
              e.activation(o, i, AF.Ln, bias=EPSC[:, :]), inc="act")
        te = pb.op("scalar", lambda e, o=Rbc[:, 0:width], i=Rbc[:, 0:width]:
                   e.activation(o, i, AF.Exp, scale=-0.5), inc="act")
        pb.wait("vector", te)
        tmr = pb.op("vector", lambda e, o=MRt[:, 0:width], a=Mbc[:, 0:width],
                    b=Rbc[:, 0:width]:
                    e.tensor_tensor(o, a, b, ALU.mult), inc="dve")
        return tmr

    def ln_prep(sc, pfx, s, XLN, scr, fbc):
        XPV = [scr[:, 0:512], scr[:, 512:1024]]
        SQT = scr[:, 1024:1536]
        MBC = fbc[:, 0:512]
        RBC = fbc[:, 512:1024]
        MRT = fbc[:, 1024:1536]
        T1 = fbc[:, 1536:2048]
        xv_tok = [None, None]
        for tau in range(4):
            g0 = s * S + tau * 512
            sq_tok = None
            pe_sq = None
            for kc in range(KC):
                slot = kc % 2
                pb.wait("sync", xv_tok[slot])
                tl = pb.dma("sync", XPV[slot], D["x0"][kc, :, g0:g0 + 512],
                            f"d_lpx{slot}")
                pb.wait("tensor", tl)
                pb.op("tensor", lambda e, o=PS_B[0:1, 0:512], l=ONESM[:, :],
                      r=XPV[slot], st=(kc == 0), sp=(kc == KC - 1):
                      e.matmul(o, l, r, start=st, stop=sp, skip_group_check=True))
                pb.wait("scalar", tl)
                pb.wait("scalar", sq_tok)
                ta = pb.op("scalar", lambda e, o=SQT, i=XPV[slot]:
                           e.activation(o, i, AF.Square), inc="act")
                pb.wait("tensor", ta)
                pe_sq = pb.op("tensor", lambda e, o=PS_B[0:1, 512:1024], l=ONESM[:, :],
                              r=SQT, st=(kc == 0), sp=(kc == KC - 1):
                              e.matmul(o, l, r, start=st, stop=sp,
                                       skip_group_check=True), inc="pe")
                sq_tok = pe_sq
                xv_tok[slot] = pe_sq
            tfin = ln_finalize(PS_B[0:1, 0:512], PS_B[0:1, 512:1024],
                               MBC, RBC, MRT, 512, pe_sq, 0)
            for kc in range(KC):
                slot = kc % 2
                pb.wait("sync", xv_tok[slot])
                tl = pb.dma("sync", XPV[slot], D["x0"][kc, :, g0:g0 + 512],
                            f"d_lpx{slot}")
                pb.wait("vector", tl)
                pb.wait("vector", tfin)
                pb.op("vector", lambda e, o=T1, a=XPV[slot], b=RBC:
                      e.tensor_tensor(o, a, b, ALU.mult), inc="dve")
                t2 = pb.op("vector", lambda e,
                           o=XLN[:, kc * S + tau * 512: kc * S + tau * 512 + 512],
                           a=T1, b=MRT:
                           e.tensor_tensor(o, a, b, ALU.subtract), inc="dve")
                xv_tok[slot] = t2

    def attn_heads(lay, s, XLN, bufs):
        QT = bufs["QT"]; KT = bufs["KT"]; VSB = bufs["VSB"]
        TRASH = bufs["TRASH"]; FBC = bufs["FBC"]; AO = bufs["AO"]
        tok_ao_store = None
        tok_wq_free = [None, None]
        tok_wk_free = [None, None]
        tok_wv_free = None
        tok_exp_c1_last = None
        tok_fmult = [None, None, None, None]
        tok_attnv = {}
        tok_exp_c2 = [None, None]
        for h in range(HEADS):
            slot = h % 2
            if NO_H1_LOADS and h > 0:
                twq = twk = None
            elif lay == "a":
                pb.wait("gpsimd", tok_wq_free[slot])
                twq = pb.dma("gpsimd", bufs["WQ"][slot][:, :], D["wq"][h], f"d_wq{slot}")
                pb.wait("gpsimd", tok_wk_free[slot])
                twk = pb.dma("gpsimd", bufs["WK"][slot][:, :], D["wk"][h], f"d_wk{slot}")
            else:
                pb.wait("gpsimd", tok_wq_free[slot])
                twq = pb.dma("gpsimd", bufs["WQ"][slot][:, :], D["wqu"][h], f"d_wq{slot}")
                pb.wait("gpsimd", tok_wk_free[slot])
                twk = pb.dma("gpsimd", bufs["WK"][slot][:, :], D["kup"][h], f"d_wk{slot}")
            if h % 2 == 0:
                pb.wait("gpsimd", tok_wv_free)
                nkv = KC if lay == "a" else 4
                wsrc3 = D["wv"] if lay == "a" else D["vup"]
                for kc in range(nkv):
                    twv = pb.dma("gpsimd", bufs["WV"][:, kc * 256:(kc + 1) * 256],
                                 wsrc3[kc, :, h * 128:(h + 2) * 128], "d_wv")
            if not (NO_H1_PREP and h > 0):
                prep_step(3)

            if h % 2 == 0:
                tok_ve = [None, None]
                for tm in range(KC):
                    pv = PS_A[:, (tm % 2) * 512:(tm % 2) * 512 + 256]
                    pb.wait("tensor", twv)
                    pb.wait("tensor", tok_exp_c1_last)
                    pb.wait("tensor", tok_ve[tm % 2])
                    specs = []
                    for kc in range(nkv):
                        if lay == "a":
                            l = XLN[:, kc * S + tm * 128: kc * S + tm * 128 + 128]
                        else:
                            l = bufs["CC"][:, kc * S + tm * 128: kc * S + tm * 128 + 128]
                        r = bufs["WV"][:, kc * 256:(kc + 1) * 256]
                        specs.append((pv, l, r, kc == 0, kc == nkv - 1))
                    tv = mm_loop(specs)
                    ov = VSB[:, tm * 256:(tm + 1) * 256]
                    if lay == "a":
                        pb.wait("vector", tv)
                        te = pb.op("vector", lambda e, o=ov, i=pv,
                                   b=bufs["BVBC"][:, h * 128:h * 128 + 256]:
                                   e.scalar_tensor_tensor(o, i, 1.0, b, op0=ALU.mult,
                                                          op1=ALU.add), inc="dve")
                    else:
                        pb.wait("scalar", tv)
                        te = pb.op("scalar", lambda e, o=ov, i=pv:
                                   e.activation(o, i, AF.Copy), inc="act")
                    tok_ve[tm % 2] = te
                tok_wv_free = tok_ve[1]
                tok_v_done = (tok_ve[0], tok_ve[1])

            if h > 0 and H1_STAGE < 1:
                continue
            nq_kc = KC if lay == "a" else 12
            qsrc = XLN if lay == "a" else bufs["CQ"]
            pb.wait("tensor", twq)
            pb.wait("tensor", tok_exp_c1_last)
            for t_ in tok_v_done:
                pb.wait("tensor", t_)
            specs = []
            for n in range(4):
                for kc in range(nq_kc):
                    specs.append((PS_A[:, n * 512:(n + 1) * 512],
                                  bufs["WQ"][slot][:, kc * 128:(kc + 1) * 128],
                                  qsrc[:, kc * S + n * 512: kc * S + (n + 1) * 512],
                                  kc == 0, kc == nq_kc - 1))
            tq = mm_loop(specs)
            tok_wq_free[slot] = tq
            if lay == "a":
                QF = bufs["QF"]; SCR = bufs["SCR"]
                pb.wait("scalar", tq)
                pb.waitc("scalar", "dve")
                tqf = pb.op("scalar", lambda e, o=QF[:, :], i=PS_A[:, :],
                            bb=BQ[:, h:h + 1]:
                            e.activation(o, i, AF.Identity, bias=bb),
                            inc="act")
                pb.wait("tensor", tqf)
                pb.waitc("tensor", "dve")
                specs = [(PS_B[:, n * 512:(n + 1) * 512], RT[:, :],
                          QF[:, n * 512:(n + 1) * 512], True, True) for n in range(4)]
                trq = mm_loop(specs)
                pb.wait("vector", tqf)
                pb.op("vector", lambda e, o=SCR[:, :], a=QF[:, :], b=bufs["TCQ"][:, :]:
                      e.tensor_tensor(o, a, b, ALU.mult), inc="dve")
                pb.wait("vector", trq)
                pb.op("vector", lambda e, o=QF[:, :], a=PS_B[:, :], b=bufs["TSQ"][:, :]:
                      e.tensor_tensor(o, a, b, ALU.mult), inc="dve")
                tqt = pb.op("vector", lambda e, o=QT[:, :], a=SCR[:, :], b=QF[:, :]:
                            e.tensor_tensor(o, a, b, ALU.add), inc="dve")
            else:
                pb.wait("scalar", tq)
                tqt = pb.op("scalar", lambda e, o=QT[:, :], i=PS_A[:, :]:
                            e.activation(o, i, AF.Copy, scale=float(SCALE)), inc="act")

            if h > 0 and H1_STAGE < 2:
                continue
            nk_kc = KC if lay == "a" else 4
            ksrc = XLN if lay == "a" else bufs["CC"]
            pb.wait("tensor", twk)
            if lay == "a":
                pb.wait("tensor", tqf)
            else:
                pb.wait("tensor", tqt)
            specs = []
            for n in range(4):
                for kc in range(nk_kc):
                    specs.append((PS_A[:, n * 512:(n + 1) * 512],
                                  bufs["WK"][slot][:, kc * 128:(kc + 1) * 128],
                                  ksrc[:, kc * S + n * 512: kc * S + (n + 1) * 512],
                                  kc == 0, kc == nk_kc - 1))
            tk = mm_loop(specs)
            tok_wk_free[slot] = tk
            if lay == "a":
                pb.wait("scalar", tk)
                pb.wait("scalar", tqt)
                tkf = pb.op("scalar", lambda e, o=QF[:, :], i=PS_A[:, :],
                            bb=BK[:, h:h + 1]:
                            e.activation(o, i, AF.Identity, bias=bb),
                            inc="act")
                pb.wait("tensor", tkf)
                pb.wait("tensor", tqt)
                specs = [(PS_B[:, n * 512:(n + 1) * 512], RT[:, :],
                          QF[:, n * 512:(n + 1) * 512], True, True) for n in range(4)]
                trk = mm_loop(specs)
                pb.wait("vector", tkf)
                pb.op("vector", lambda e, o=SCR[:, :], a=QF[:, :], b=bufs["TCK"][:, :]:
                      e.tensor_tensor(o, a, b, ALU.mult), inc="dve")
                pb.wait("vector", trk)
                pb.op("vector", lambda e, o=QF[:, :], a=PS_B[:, :], b=bufs["TSK"][:, :]:
                      e.tensor_tensor(o, a, b, ALU.mult), inc="dve")
                tkt = pb.op("vector", lambda e, o=KT[:, :], a=SCR[:, :], b=QF[:, :]:
                            e.tensor_tensor(o, a, b, ALU.add), inc="dve")
                tok_qkt = tkt
            else:
                pb.wait("scalar", tk)
                tkt = pb.op("scalar", lambda e, o=KT[:, :], i=PS_A[:, :]:
                            e.activation(o, i, AF.Copy), inc="act")
                tok_qkt = tkt

            if DEBUG and lay == "a" and s == 0 and h == 0:
                pb.wait("sync", tok_qkt)
                pb.dma("sync", D["dbg_qt"][:, :], QT[:, :], "d_dbg")
                pb.dma("sync", D["dbg_kt"][:, :], KT[:, :], "d_dbg")
            if lay == "b":
                trh = pb.op("vector", lambda e, o=bufs["RH"][:, :], i=bufs["RAMP"][:, :],
                            sl=float(SLOPES[h]):
                            e.tensor_scalar(o, i, sl, None, op0=ALU.mult), inc="dve")
                tqps = pb.op("vector", lambda e, o=QPS[:, :], i=QPOS[:, :],
                             sl=-float(SLOPES[h]):
                             e.tensor_scalar(o, i, sl, None, op0=ALU.mult), inc="dve")
                talb = pb.op("vector", lambda e, o=bufs["ALB"][:, :], i=bufs["DCONST"][:, :],
                             sl=float(SLOPES[h]):
                             e.tensor_scalar(o, i, sl, None, op0=ALU.mult), inc="dve")

            if h > 0 and H1_STAGE < 3:
                continue
            tok_exp_qi = None
            for qi in range(16):
                klen = (qi + 1) * 128
                if lay == "a":
                    pb.wait("tensor", tok_qkt)
                    pb.wait("tensor", tkf)
                else:
                    pb.wait("tensor", tok_qkt)
                pb.wait("tensor", tok_exp_qi)
                specs = []
                for n in range(0, klen, 512):
                    w = min(512, klen - n)
                    specs.append((PS_A[:, n:n + w], QT[:, qi * 128:(qi + 1) * 128],
                                  KT[:, n:n + w], True, True))
                tsc = mm_loop(specs)
                pb.wait("vector", tsc)
                if lay == "b":
                    pb.wait("vector", trh)
                    pb.wait("vector", tqps)
                    pb.op("vector", lambda e, o=PS_A[:, 0:klen], i=PS_A[:, 0:klen],
                          s1=QPS[:, qi:qi + 1], b=bufs["RH"][:, 0:klen]:
                          e.scalar_tensor_tensor(o, i, s1, b, op0=ALU.add, op1=ALU.add),
                          inc="dve")
                pb.op("vector", lambda e, o=PS_A[:, klen - 128:klen],
                      a=PS_A[:, klen - 128:klen], b=UPM[:, :]:
                      e.tensor_tensor(o, a, b, ALU.add), inc="dve")
                tmx = pb.op("vector", lambda e, o=NEGM[:, qi:qi + 1], i=PS_A[:, 0:klen]:
                            e.reduce_max(o, i, axis=AX.X, negate=True), inc="dve")
                pb.wait("scalar", tmx)
                tok_exp_qi = pb.op("scalar", lambda e, o=TRASH[:, 0:klen],
                                   i=PS_A[:, 0:klen], b=NEGM[:, qi:qi + 1],
                                   a=SACC[:, qi:qi + 1]:
                                   e.activation(o, i, AF.Exp, bias=b, accum_out=a),
                                   inc="act")
            tok_exp_c1_last = tok_exp_qi

            if h > 0 and H1_STAGE < 4:
                continue
            pb.wait("scalar", tok_exp_c1_last)
            pb.op("scalar", lambda e, o=LNSB[:, :], i=SACC[:, :]:
                  e.activation(o, i, AF.Ln), inc="act")
            pb.wait("vector", tok_exp_c1_last)
            tnm = pb.op("vector", lambda e, o=NM1[:, :], i=NEGM[:, :]:
                        e.tensor_reduce(o, i, axis=AX.X, op=ALU.min), inc="dve")
            pb.wait("tensor", tnm)
            pb.waitc("tensor", "dve")
            pb.waitc("tensor", "act")
            ttr = pb.op("tensor", lambda e, o=PS_B[0:1, 0:128], l=NM1[:, :],
                        r=IDENT[:, :]:
                        e.matmul(o, l, r, start=True, stop=True,
                                 skip_group_check=True), inc="pe")
            pb.wait("scalar", ttr)
            ttrc = pb.op("scalar", lambda e, o=TROW[:, :], i=PS_B[0:1, 0:128]:
                         e.activation(o, i, AF.Copy), inc="act")
            pb.wait("vector", ttrc)
            tmg = pb.op("vector", lambda e, o=MGR[:, :], i=TROW[:, :]:
                        e.tensor_reduce(o, i, axis=AX.X, op=ALU.min), inc="dve")
            pb.wait("tensor", tmg)
            tbg = pb.op("tensor", lambda e, o=PS_B[:, 512:513], l=ONESK1[:, :],
                        r=MGR[:, :]:
                        e.matmul(o, l, r, start=True, stop=True,
                                 skip_group_check=True), inc="pe")
            pb.wait("scalar", tbg)
            tnmg = pb.op("scalar", lambda e, o=NMG[:, :], i=PS_B[:, 512:513]:
                         e.activation(o, i, AF.Copy), inc="act")
            pb.wait("vector", tnmg)
            tg = pb.op("vector", lambda e, o=GTB[:, :], i=NEGM[:, :], s1=NMG[:, :],
                       b=LNSB[:, :]:
                       e.scalar_tensor_tensor(o, i, s1, b, op0=ALU.subtract,
                                              op1=ALU.subtract), inc="dve")
            if lay == "b":
                tkb = pb.op("vector", lambda e, o=KB[:, :], i=DN16[:, :],
                            sl=float(SLOPES[h]), s2=NMG[:, :]:
                            e.tensor_scalar(o, i, sl, s2, op0=ALU.mult, op1=ALU.add),
                            inc="dve")
            pb.wait("scalar", tg)
            tf = pb.op("scalar", lambda e, o=FQT[:, :], i=GTB[:, :]:
                       e.activation(o, i, AF.Exp), inc="act")
            tfr = None
            trc_toks = [None, None, None, None]
            for qi in range(16):
                pb.wait("tensor", tf)
                pb.wait("tensor", trc_toks[qi % 4])
                ttr2 = pb.op("tensor", lambda e,
                             o=PS_B[0:1, (qi % 4) * 512:(qi % 4) * 512 + 128],
                             l=FQT[:, qi:qi + 1], r=IDENT[:, :]:
                             e.matmul(o, l, r, start=True, stop=True,
                                      skip_group_check=True), inc="pe")
                pb.wait("scalar", ttr2)
                tfr = pb.op("scalar", lambda e, o=FROW[0:1, qi * 128:(qi + 1) * 128],
                            i=PS_B[0:1, (qi % 4) * 512:(qi % 4) * 512 + 128]:
                            e.activation(o, i, AF.Copy), inc="act")
                trc_toks[qi % 4] = tfr
            tfb = None
            for n in range(4):
                pb.wait("tensor", tfr)
                tb_ = pb.op("tensor", lambda e, o=PS_B[:, n * 512:(n + 1) * 512],
                            l=ONESK1[:, :], r=FROW[0:1, n * 512:(n + 1) * 512]:
                            e.matmul(o, l, r, start=True, stop=True,
                                     skip_group_check=True), inc="pe")
                pb.wait("scalar", tb_)
                tfb = pb.op("scalar", lambda e, o=FBC[:, n * 512:(n + 1) * 512],
                            i=PS_B[:, n * 512:(n + 1) * 512]:
                            e.activation(o, i, AF.Copy), inc="act")

            if DEBUG and lay == "a" and s == 0 and h == 0:
                pb.barrier()
                pb.dma("sync", D["dbg_st"][0], NEGM[:, :], "d_dbg")
                pb.dma("sync", D["dbg_st"][1], SACC[:, :], "d_dbg")
                pb.dma("sync", D["dbg_st"][2], FQT[:, :], "d_dbg")
                pb.dma("sync", D["dbg_nmg"][:, :], NMG[:, :], "d_dbg")
                pb.dma("sync", D["dbg_fbc"][:, :], FBC[:, :], "d_dbg")
                pb.barrier()
            if h > 0 and H1_STAGE < 5:
                continue
            if lay == "b":
                pb.wait("scalar", tkb)
            for j in range(4):
                ov = PS_B[:, 1024 + (j % 2) * 512: 1024 + (j % 2) * 512 + 512]
                tattnv = None
                for kc in range(4 * j + 4):
                    jjs = max(0, kc - 4 * j)
                    w = 512 - jjs * 128
                    q0 = j * 512 + jjs * 128
                    par = kc % 2
                    pv = PS_B[:, par * 512 + jjs * 128: par * 512 + 512]
                    pb.wait("tensor", tok_exp_c2[par])
                    if kc == 0:
                        pb.wait("tensor", tfb)
                        pb.wait("vector", tfb)
                        pb.wait("tensor", tok_fmult[(j + 2) % 4])
                    tst = pb.op("tensor", lambda e, o=pv,
                                l=KT[:, kc * 128:(kc + 1) * 128],
                                r=QT[:, q0: (j + 1) * 512]:
                                e.matmul(o, l, r, start=True, stop=True,
                                         skip_group_check=True), inc="pe")
                    tmk = None
                    if lay == "b":
                        pb.wait("vector", tst)
                        pb.wait("vector", talb)
                        tmk = pb.op("vector", lambda e, o=pv, a=pv, b=bufs["ALB"][:, 0:w]:
                                    e.tensor_tensor(o, a, b, ALU.add), inc="dve")
                    if kc >= 4 * j:
                        dg = PS_B[:, par * 512 + jjs * 128: par * 512 + jjs * 128 + 128]
                        pb.wait("vector", tst)
                        tmk = pb.op("vector", lambda e, o=dg, a=dg, b=LOWM[:, :]:
                                    e.tensor_tensor(o, a, b, ALU.add), inc="dve")
                    pb.wait("scalar", tst)
                    pb.wait("scalar", tmk)
                    pb.wait("scalar", tok_attnv.get(par))
                    delta = 4 * j + jjs - kc
                    bias_ap = NMG[:, :] if lay == "a" else KB[:, delta:delta + 1]
                    te2 = pb.op("scalar", lambda e, o=ATN[par][:, 0:w], i=pv, b=bias_ap:
                                e.activation(o, i, AF.Exp, bias=b), inc="act")
                    tok_exp_c2[par] = te2
                    pb.wait("tensor", te2)
                    vl = VSB[:, kc * 256 + (h % 2) * 128: kc * 256 + (h % 2) * 128 + 128]
                    tattnv = pb.op("tensor", lambda e, o=ov[:, jjs * 128:512], l=vl,
                                   r=ATN[par][:, 0:w], st=(kc == 0), sp=(kc == 4 * j + 3):
                                   e.matmul(o, l, r, start=st, stop=sp,
                                            skip_group_check=True), inc="pe")
                    tok_attnv[par] = tattnv
                pb.wait("vector", tattnv)
                pb.wait("vector", tok_ao_store)
                tfm = pb.op("vector", lambda e, o=AO[:, j * 512:(j + 1) * 512], a=ov,
                            b=FBC[:, j * 512:(j + 1) * 512]:
                            e.tensor_tensor(o, a, b, ALU.mult), inc="dve")
                tok_fmult[j] = tfm
            if DEBUG and lay == "a" and s == 0 and h == 0:
                pb.barrier()
                pb.dma("sync", D["dbg_ao0"][:, :], AO[:, :], "d_dbg")
                pb.barrier()
            pb.wait("sync", tok_fmult[3])
            tok_ao_store = pb.dma("sync", D["ao"][s, h], AO[:, :], "d_ao")
        pb.wait("sync", tok_ao_store)

    def dense_epilogue(sc, pfx, s, wsrc_name, bias, xprev, xnext, xlnout):
        AOB = sc.sb(f"{pfx}_aob", (128, KC * 512), BF16)
        WDB = [sc.sb(f"{pfx}_wdb{i}", (128, H), BF16) for i in range(2)]
        XPV = [sc.sb(f"{pfx}_xpv{i}", (128, 512), F32) for i in range(2)]
        XNW = [sc.sb(f"{pfx}_xnw{i}", (128, 512), F32) for i in range(2)]
        SQT = sc.sb(f"{pfx}_sqt", (128, 512), F32)
        MBC = sc.sb(f"{pfx}_mbc", (128, 512), F32)
        RBC = sc.sb(f"{pfx}_rbc", (128, 512), F32)
        MRT = sc.sb(f"{pfx}_mrt", (128, 512), F32)
        T1 = sc.sb(f"{pfx}_t1", (128, 512), F32)
        XLT = sc.sb(f"{pfx}_xlt", (128, 512), BF16)
        wsrc = D[wsrc_name]
        hbm_fence()
        tok_w = [None, None]
        tok_aob = None
        tok_ev = [None, None]
        tok_store = [None, None]
        tok_stats = [None, None]
        tok_xp = [None, None]
        tok_xl = None
        for tau in range(4):
            g0 = s * S + tau * 512
            pb.wait("sync", tok_aob)
            pb.wait("sync", pb.tok("pe"))
            for kc in range(KC):
                taob = pb.dma("sync", AOB[:, kc * 512:(kc + 1) * 512],
                              D["ao"][s, kc, :, tau * 512:tau * 512 + 512], "d_aob")
            tok_aob = taob
            pe_sq = None
            sq_tok = None
            for f in range(KC):
                slot = f % 2
                pb.wait("gpsimd", tok_w[slot])
                tw = pb.dma("gpsimd", WDB[slot][:, :], wsrc[f], f"d_de{slot}")
                prep_step(1)
                pb.wait("tensor", tw)
                pb.wait("tensor", taob)
                pb.wait("tensor", tok_ev[slot])
                specs = [(PS_A[:, slot * 512:(slot + 1) * 512],
                          WDB[slot][:, kc * 128:(kc + 1) * 128],
                          AOB[:, kc * 512:(kc + 1) * 512], kc == 0, kc == KC - 1)
                         for kc in range(KC)]
                td = mm_loop(specs)
                tok_w[slot] = td
                pb.wait("sync", tok_ev[slot])
                txp = pb.dma("sync", XPV[slot][:, :], D[xprev][f, :, g0:g0 + 512],
                             f"d_dxp{slot}")
                pb.wait("vector", td)
                pb.wait("vector", txp)
                pb.wait("vector", tok_store[slot])
                pb.wait("vector", tok_stats[slot])
                if bias is not None:
                    txn = pb.op("vector", lambda e, o=XNW[slot][:, :],
                                i=PS_A[:, slot * 512:(slot + 1) * 512],
                                s1=bias[:, f:f + 1], b=XPV[slot][:, :]:
                                e.scalar_tensor_tensor(o, i, s1, b, op0=ALU.add,
                                                       op1=ALU.add), inc="dve")
                else:
                    txn = pb.op("vector", lambda e, o=XNW[slot][:, :],
                                i=PS_A[:, slot * 512:(slot + 1) * 512],
                                b=XPV[slot][:, :]:
                                e.tensor_tensor(o, i, b, ALU.add), inc="dve")
                tok_ev[slot] = txn
                pb.wait("sync", txn)
                tst = pb.dma("sync", D[xnext][f, :, g0:g0 + 512], XNW[slot][:, :],
                             f"d_dxn{slot}")
                tok_store[slot] = tst
                if xlnout is not None:
                    pb.wait("tensor", txn)
                    pb.op("tensor", lambda e, o=PS_B[0:1, 0:512], l=ONESM[:, :],
                          r=XNW[slot][:, :], st=(f == 0), sp=(f == KC - 1):
                          e.matmul(o, l, r, start=st, stop=sp, skip_group_check=True))
                    pb.wait("scalar", txn)
                    pb.wait("scalar", sq_tok)
                    ta = pb.op("scalar", lambda e, o=SQT[:, :], i=XNW[slot][:, :]:
                               e.activation(o, i, AF.Square), inc="act")
                    pb.wait("tensor", ta)
                    pe_sq = pb.op("tensor", lambda e, o=PS_B[0:1, 512:1024],
                                  l=ONESM[:, :], r=SQT[:, :],
                                  st=(f == 0), sp=(f == KC - 1):
                                  e.matmul(o, l, r, start=st, stop=sp,
                                           skip_group_check=True), inc="pe")
                    sq_tok = pe_sq
                    tok_stats[slot] = pe_sq
            if xlnout is not None:
                tfin = ln_finalize(PS_B[0:1, 0:512], PS_B[0:1, 512:1024],
                                   MBC, RBC, MRT, 512, pe_sq, 1024)
                pb.waitc("sync", "d_dxn0")
                pb.waitc("sync", "d_dxn1")
                for f in range(KC):
                    slot = f % 2
                    pb.wait("sync", tok_xp[slot])
                    tl = pb.dma("sync", XPV2 := XPV[slot][:, :],
                                D[xnext][f, :, g0:g0 + 512], f"d_dx2{slot}")
                    pb.wait("vector", tl)
                    pb.wait("vector", tfin)
                    pb.wait("vector", tok_xl)
                    pb.op("vector", lambda e, o=T1[:, :], a=XPV2, b=RBC[:, :]:
                          e.tensor_tensor(o, a, b, ALU.mult), inc="dve")
                    t2 = pb.op("vector", lambda e, o=XLT[:, :], a=T1[:, :],
                               b=MRT[:, :]:
                               e.tensor_tensor(o, a, b, ALU.subtract), inc="dve")
                    tok_xp[slot] = t2
                    pb.wait("sync", t2)
                    tok_xl = pb.dma("sync", D[xlnout][f, :, g0:g0 + 512], XLT[:, :],
                                    "d_dxl")
                tok_ev = [pb.tok("dve"), pb.tok("dve")]

    def ffn(sc, pfx, xln_src, xprev, xnext, xlnout):
        HDN = sc.sb(f"{pfx}_hdn", (128, 44 * 512), BF16)
        XLT = [sc.sb(f"{pfx}_xlt{i}", (128, KC * 512), BF16) for i in range(2)]
        GWB = [sc.sb(f"{pfx}_gwb{i}", (128, H), BF16) for i in range(2)]
        UWB = [sc.sb(f"{pfx}_uwb{i}", (128, H), BF16) for i in range(2)]
        DWB = [sc.sb(f"{pfx}_dwb{i}", (128, INTER), BF16) for i in range(2)]
        SG = [sc.sb(f"{pfx}_sg{i}", (128, 512), F32) for i in range(2)]
        XPV = [sc.sb(f"{pfx}_xpv{i}", (128, 512), F32) for i in range(2)]
        XNW = [sc.sb(f"{pfx}_xnw{i}", (128, 512), F32) for i in range(2)]
        SQT = sc.sb(f"{pfx}_sqt", (128, 512), F32)
        MBC = sc.sb(f"{pfx}_mbc", (128, 512), F32)
        RBC = sc.sb(f"{pfx}_rbc", (128, 512), F32)
        MRT = sc.sb(f"{pfx}_mrt", (128, 512), F32)
        T1 = sc.sb(f"{pfx}_t1", (128, 512), F32)
        XLO = sc.sb(f"{pfx}_xlo", (128, 512), BF16)
        hbm_fence()
        pb.waitc("gpsimd", "d_preps0")
        pb.waitc("gpsimd", "d_preps1")
        tok_g = [None, None]; tok_u = [None, None]; tok_d = [None, None]
        tok_sg = [None, None]; tok_hd = [None, None]
        tok_ev = [None, None]
        tok_store = [None, None]; tok_stats = [None, None]
        tok_xp = [None, None]
        tok_xlt_use = [None, None]
        tok_xl = None
        for tt in range(8):
            t0 = tt * 512
            xslot = tt % 2
            pb.wait("sync", tok_xlt_use[xslot])
            for kc in range(KC):
                txl = pb.dma("sync", XLT[xslot][:, kc * 512:(kc + 1) * 512],
                             D[xln_src][kc, :, t0:t0 + 512], f"d_fxlt{xslot}")
            for m in range(44):
                slot = m % 2
                pb.wait("gpsimd", tok_g[slot])
                tgw = pb.dma("gpsimd", GWB[slot][:, :], D["gwb"][m], f"d_fg{slot}")
                pb.wait("gpsimd", tok_u[slot])
                tuw = pb.dma("gpsimd", UWB[slot][:, :], D["uwb"][m], f"d_fu{slot}")
                gps = PS_A[:, slot * 1024: slot * 1024 + 512]
                ups = PS_A[:, slot * 1024 + 512: slot * 1024 + 1024]
                pb.wait("tensor", tgw)
                pb.wait("tensor", txl)
                pb.wait("tensor", tok_sg[slot])
                specs = [(gps, GWB[slot][:, kc * 128:(kc + 1) * 128],
                          XLT[xslot][:, kc * 512:(kc + 1) * 512],
                          kc == 0, kc == KC - 1) for kc in range(KC)]
                tg = mm_loop(specs)
                tok_g[slot] = tg
                pb.wait("scalar", tg)
                tsg = pb.op("scalar", lambda e, o=SG[slot][:, :], i=gps,
                            s1=GS[:, m:m + 1]:
                            e.activation(o, i, AF.Silu, scale=s1), inc="act")
                tok_sg[slot] = tsg
                pb.wait("tensor", tuw)
                pb.wait("tensor", tok_hd[slot])
                specs = [(ups, UWB[slot][:, kc * 128:(kc + 1) * 128],
                          XLT[xslot][:, kc * 512:(kc + 1) * 512],
                          kc == 0, kc == KC - 1) for kc in range(KC)]
                tu = mm_loop(specs)
                tok_u[slot] = tu
                pb.wait("vector", tu)
                pb.wait("vector", tsg)
                thd = pb.op("vector", lambda e, o=HDN[:, m * 512:(m + 1) * 512],
                            i=ups, s1=US[:, m:m + 1], b=SG[slot][:, :]:
                            e.scalar_tensor_tensor(o, i, s1, b, op0=ALU.mult,
                                                   op1=ALU.mult), inc="dve")
                tok_hd[slot] = thd
                prep_step(1)
            tok_xlt_use[xslot] = pb.tok("pe")
            tok_hd_all = pb.tok("dve")
            pe_sq = None
            sq_tok = None
            for f in range(KC):
                slot = f % 2
                pb.wait("gpsimd", tok_d[slot])
                tdw = pb.dma("gpsimd", DWB[slot][:, :], D["dwb"][f], f"d_fd{slot}")
                pvw = PS_A[:, slot * 512:(slot + 1) * 512]
                pb.wait("tensor", tdw)
                pb.wait("tensor", tok_hd_all)
                pb.wait("tensor", tok_ev[slot])
                specs = [(pvw, DWB[slot][:, m * 128:(m + 1) * 128],
                          HDN[:, m * 512:(m + 1) * 512], m == 0, m == 43)
                         for m in range(44)]
                td = mm_loop(specs)
                tok_d[slot] = td
                pb.wait("sync", tok_ev[slot])
                txp = pb.dma("sync", XPV[slot][:, :], D[xprev][f, :, t0:t0 + 512],
                             f"d_fxp{slot}")
                pb.wait("vector", td)
                pb.wait("vector", txp)
                pb.wait("vector", tok_store[slot])
                pb.wait("vector", tok_stats[slot])
                txn = pb.op("vector", lambda e, o=XNW[slot][:, :], i=pvw,
                            s1=DS[:, f:f + 1], b=XPV[slot][:, :]:
                            e.scalar_tensor_tensor(o, i, s1, b, op0=ALU.mult,
                                                   op1=ALU.add), inc="dve")
                tok_ev[slot] = txn
                pb.wait("sync", txn)
                tst = pb.dma("sync", D[xnext][f, :, t0:t0 + 512], XNW[slot][:, :],
                             f"d_fxn{slot}")
                tok_store[slot] = tst
                if xlnout is not None:
                    pb.wait("tensor", txn)
                    pb.op("tensor", lambda e, o=PS_B[0:1, 0:512], l=ONESM[:, :],
                          r=XNW[slot][:, :], st=(f == 0), sp=(f == KC - 1):
                          e.matmul(o, l, r, start=st, stop=sp, skip_group_check=True))
                    pb.wait("scalar", txn)
                    pb.wait("scalar", sq_tok)
                    ta = pb.op("scalar", lambda e, o=SQT[:, :], i=XNW[slot][:, :]:
                               e.activation(o, i, AF.Square), inc="act")
                    pb.wait("tensor", ta)
                    pe_sq = pb.op("tensor", lambda e, o=PS_B[0:1, 512:1024],
                                  l=ONESM[:, :], r=SQT[:, :],
                                  st=(f == 0), sp=(f == KC - 1):
                                  e.matmul(o, l, r, start=st, stop=sp,
                                           skip_group_check=True), inc="pe")
                    sq_tok = pe_sq
                    tok_stats[slot] = pe_sq
            if xlnout is not None:
                tfin = ln_finalize(PS_B[0:1, 0:512], PS_B[0:1, 512:1024],
                                   MBC, RBC, MRT, 512, pe_sq, 1024)
                pb.waitc("sync", "d_fxn0")
                pb.waitc("sync", "d_fxn1")
                for f in range(KC):
                    slot = f % 2
                    pb.wait("sync", tok_xp[slot])
                    tl = pb.dma("sync", XPV[slot][:, :],
                                D[xnext][f, :, t0:t0 + 512], f"d_fx2{slot}")
                    pb.wait("vector", tl)
                    pb.wait("vector", tfin)
                    pb.wait("vector", tok_xl)
                    pb.op("vector", lambda e, o=T1[:, :],
                          a=XPV[slot][:, :], b=RBC[:, :]:
                          e.tensor_tensor(o, a, b, ALU.mult), inc="dve")
                    t2 = pb.op("vector", lambda e, o=XLO[:, :], a=T1[:, :],
                               b=MRT[:, :]:
                               e.tensor_tensor(o, a, b, ALU.subtract), inc="dve")
                    tok_xp[slot] = t2
                    pb.wait("sync", t2)
                    tok_xl = pb.dma("sync", D[xlnout][f, :, t0:t0 + 512],
                                    XLO[:, :], "d_fxl")
            tok_sg = [pb.tok("dve"), pb.tok("dve")]
            tok_hd = [pb.tok("dve"), pb.tok("dve")]
            tok_ev = [pb.tok("dve"), pb.tok("dve")]
            if xlnout is not None:
                tok_sg = [pb.tok("act"), pb.tok("act")]

    def attn_b_ccq(sc, pfx, s, CQ, CC, xln_src):
        scl = Scope()
        XLT = scl.sb(f"{pfx}_xlt", (128, KC * 512), BF16)
        WB = [scl.sb(f"{pfx}_wb{i}", (128, H), BF16) for i in range(2)]
        hbm_fence()
        tok_w = [None, None]
        tok_xlt = None
        tok_ev = [None, None]
        for tblk in range(4):
            g0 = s * S + tblk * 512
            pb.wait("sync", tok_xlt)
            pb.wait("sync", pb.tok("pe"))
            for kc in range(KC):
                txl = pb.dma("sync", XLT[:, kc * 512:(kc + 1) * 512],
                             D[xln_src][kc, :, g0:g0 + 512], "d_cbx")
            tok_xlt = txl
            for mb in range(16):
                is_c = mb >= 12
                wsrc = D["wkd"][mb - 12] if is_c else D["wqd"][mb]
                slot = mb % 2
                pb.wait("gpsimd", tok_w[slot])
                tw = pb.dma("gpsimd", WB[slot][:, :], wsrc, f"d_cb{slot}")
                prep_step(1)
                pb.wait("tensor", tw)
                pb.wait("tensor", txl)
                pb.wait("tensor", tok_ev[slot])
                specs = [(PS_A[:, slot * 512:(slot + 1) * 512],
                          WB[slot][:, kc * 128:(kc + 1) * 128],
                          XLT[:, kc * 512:(kc + 1) * 512], kc == 0, kc == KC - 1)
                         for kc in range(KC)]
                tq = mm_loop(specs)
                tok_w[slot] = tq
                if is_c:
                    dst = CC[:, (mb - 12) * S + tblk * 512: (mb - 12) * S + tblk * 512 + 512]
                else:
                    dst = CQ[:, mb * S + tblk * 512: mb * S + tblk * 512 + 512]
                pb.wait("scalar", tq)
                tev = pb.op("scalar", lambda e, o=dst,
                            i=PS_A[:, slot * 512:(slot + 1) * 512]:
                            e.activation(o, i, AF.Copy), inc="act")
                tok_ev[slot] = tev
        pb.waitc("tensor", "act")
        scl.close()

    pb.barrier()

    sc1 = Scope()
    XLN = sc1.sb("a_xln", (128, KC * S), BF16)
    bufs_a = dict(
        QF=sc1.sb("a_qf", (128, S), F32),
        SCR=sc1.sb("a_scr", (128, S), F32),
        QT=sc1.sb("a_qt", (128, S), BF16),
        KT=sc1.sb("a_kt", (128, S), BF16),
        VSB=sc1.sb("a_vsb", (128, KC * 256), BF16),
        FBC=sc1.sb("a_fbc", (128, S), F32),
        AO=sc1.sb("a_ao", (128, S), BF16),
        WQ=[sc1.sb(f"a_wq{i}", (128, H), BF16) for i in range(2)],
        WK=[sc1.sb(f"a_wk{i}", (128, H), BF16) for i in range(2)],
        WV=sc1.sb("a_wv", (128, KC * 256), BF16),
        BVBC=sc1.sb("a_bvbc", (128, H), F32),
        TCQ=sc1.sb("a_tcq", (128, S), F32),
        TSQ=sc1.sb("a_tsq", (128, S), F32),
        TCK=sc1.sb("a_tck", (128, S), F32),
        TSK=sc1.sb("a_tsk", (128, S), F32),
    )
    bufs_a["TRASH"] = bufs_a["QF"]
    for dst, srcn in [(bufs_a["TCQ"], "tbl_cq"), (bufs_a["TSQ"], "tbl_sq"),
                      (bufs_a["TCK"], "tbl_ck"), (bufs_a["TSK"], "tbl_sk")]:
        pb.dma("sync", dst[:, :], D[srcn][:, :], "d_const")
    tok_const = pb.tok("d_const")
    for e in ("tensor", "vector", "scalar"):
        pb.wait(e, tok_const)
    for n in range(4):
        pb.wait("tensor", tok_const)
        tb_ = pb.op("tensor", lambda e, o=PS_B[:, n * 512:(n + 1) * 512],
                    l=ONESK1[:, :], r=BVROW[0:1, n * 512:(n + 1) * 512]:
                    e.matmul(o, l, r, start=True, stop=True, skip_group_check=True),
                    inc="pe")
        pb.wait("scalar", tb_)
        pb.op("scalar", lambda e, o=bufs_a["BVBC"][:, n * 512:(n + 1) * 512],
              i=PS_B[:, n * 512:(n + 1) * 512]:
              e.activation(o, i, AF.Copy), inc="act")
    pb.waitc("tensor", "act")

    for s in range(NS):
        ln_prep(sc1, f"lp{s}", s, XLN, bufs_a["SCR"], bufs_a["FBC"])
        pb.barrier()
        if DEBUG and s == 0:
            for kc in range(KC):
                pb.dma("sync", D["dbg_xln"][kc], XLN[:, kc * S:(kc + 1) * S], "d_dbg")
            pb.barrier()
        attn_heads("a", s, XLN, bufs_a)
        pb.barrier()
    if DEBUG:
        for hh in range(NH):
            pb.dma("sync", D["dbg_ao"][hh], D["ao"][0, hh], "d_dbg")
        pb.barrier()
    sc1.close()

    sc1d = Scope()
    for s in range(NS):
        dense_epilogue(sc1d, f"d1{s}", s, "wd", BD, "x0",
                       "xA" if SUBLAYERS > 1 else "out",
                       "lnA" if SUBLAYERS > 1 else None)
        pb.barrier()
    sc1d.close()

    if SUBLAYERS > 1:
        sc2 = Scope()
        prep_finish()
        pb.barrier()
        ffn(sc2, "f1", "lnA", "xA", "xB" if SUBLAYERS > 2 else "out",
            "lnB" if SUBLAYERS > 2 else None)
        pb.barrier()
        sc2.close()

    if SUBLAYERS > 2:
        sc3 = Scope()
        CQ = sc3.sb("b_cq", (128, 12 * S), BF16)
        CC = sc3.sb("b_cc", (128, 4 * S), BF16)
        bufs_b = dict(
            QT=sc3.sb("b_qt", (128, S), BF16),
            KT=sc3.sb("b_kt", (128, S), BF16),
            VSB=sc3.sb("b_vsb", (128, KC * 256), BF16),
            FBC=sc3.sb("b_fbc", (128, S), F32),
            RAMP=sc3.sb("b_ramp", (128, S), F32),
            AO=sc3.sb("b_ao", (128, S), BF16),
            WQ=[sc3.sb(f"b_wq{i}", (128, RQ), BF16) for i in range(2)],
            WK=[sc3.sb(f"b_wk{i}", (128, RKV), BF16) for i in range(2)],
            WV=sc3.sb("b_wv", (128, 4 * 256), BF16),
            RH=sc3.sb("b_rh", (128, S), F32),
            ALB=sc3.sb("b_alb", (128, 512), F32),
            DCONST=sc3.sb("b_dconst", (128, 512), F32),
            CQ=CQ, CC=CC,
        )
        bufs_b["TRASH"] = bufs_b["FBC"]
        pb.dma("sync", bufs_b["RAMP"][:, :], D["ramp"][:, :], "d_const")
        tok_r = pb.dma("sync", bufs_b["DCONST"][:, :], D["dconst"][:, :], "d_const")
        pb.wait("vector", tok_r)
        for s in range(NS):
            attn_b_ccq(sc3, f"cb{s}", s, CQ, CC, "lnB")
            pb.barrier()
            attn_heads("b", s, None, bufs_b)
            pb.barrier()
        sc3.close()
        sc3d = Scope()
        for s in range(NS):
            dense_epilogue(sc3d, f"d3{s}", s, "wo", None, "xB",
                           "xA" if SUBLAYERS > 3 else "out",
                           "lnA" if SUBLAYERS > 3 else None)
            pb.barrier()
        sc3d.close()

    if SUBLAYERS > 3:
        sc4 = Scope()
        ffn(sc4, "f2", "lnA", "xA", "out", None)
        pb.barrier()
        sc4.close()

    for sem in sorted(pb.cnt):
        if sem.startswith("d_") and pb.cnt[sem] > 0:
            pb.wait("sync", (sem, pb.cnt[sem]))

    with ExitStack() as st:
        sems = {}
        for name in pb.cnt:
            assert pb.cnt[name] < 60000, (name, pb.cnt[name])
            sems[name] = st.enter_context(nc.semaphore(name))
        block = st.enter_context(nc.Block())
        for eng in ENGS:
            def make(items):
                def f(e):
                    last = {}
                    for it in items:
                        if it[0] == "wait":
                            _, s_, v = it
                            if last.get(s_, -1) >= v:
                                continue
                            e.wait_ge(sems[s_], v)
                            last[s_] = v
                        else:
                            _, fn, inc, n = it
                            ins = fn(e)
                            if inc is not None:
                                ins.then_inc(sems[inc], n)
                return f
            getattr(block, eng)(make(pb.q[eng]))
    return nc


def kernel(**inputs):
    cache = {}
    in_maps = [_host_inputs(inputs, c, cache) for c in range(8)]
    nc = build_program()
    res = run_bass_kernel_spmd(nc, in_maps, core_ids=list(range(8)))
    out = np.empty((16, S, H), np.float32)
    for c in range(8):
        o = np.asarray(res.results[c]["out"])
        o = o.reshape(KC, 128, NS, S).transpose(2, 3, 0, 1).reshape(NS, S, H)
        out[2 * c: 2 * c + 2] = o
    return out


# version 02
# speedup vs baseline: 1.0316x; 1.0316x over previous
import math
from contextlib import ExitStack

import numpy as np

import concourse.bass as bass
import concourse.mybir as mybir
from concourse.bass_utils import run_bass_kernel_spmd

F32 = mybir.dt.float32
BF16 = mybir.dt.bfloat16
AF = mybir.ActivationFunctionType
ALU = mybir.AluOpType
AX = mybir.AxisListType

H = 2048
NH = 16
HD = 128
S = 2048
NS = 2
T = NS * S
INTER = 5632
RQ = 1536
RKV = 512
KC = H // 128
SCALE = 1.0 / math.sqrt(HD)
ROT = 32
EPS = 1e-5
NEG = -1.0e30
_BASE = 2.0 ** (-2.0 ** (-(math.log2(NH) - 3)))
SLOPES = [_BASE ** i for i in range(1, NH + 1)]

SUBLAYERS = 4
DEBUG = False
HEADS = 16
NO_H1_LOADS = False
H1_STAGE = 9
NO_H1_PREP = False

ENGS = ("tensor", "vector", "scalar", "gpsimd", "sync")


class PB:
    def __init__(self):
        self.q = {e: [] for e in ENGS}
        self.cnt = {}

    def _sem(self, s):
        if s not in self.cnt:
            self.cnt[s] = 0

    def op(self, eng, fn, inc=None, n=1):
        if inc is not None:
            self._sem(inc)
            self.cnt[inc] += n
            self.q[eng].append(("op", fn, inc, n))
            return (inc, self.cnt[inc])
        self.q[eng].append(("op", fn, None, 0))
        return None

    def wait(self, eng, token):
        if token is None:
            return
        s, v = token
        if v > 0:
            self.q[eng].append(("wait", s, v))

    def tok(self, sem):
        self._sem(sem)
        return (sem, self.cnt[sem])

    def waitc(self, eng, sem):
        self.wait(eng, self.tok(sem))

    def barrier(self):
        toks = {}
        for e in ENGS:
            toks[e] = self.op(e, lambda en: en.nop(), inc="bar_" + e)
        for e in ENGS:
            for e2 in ENGS:
                if e2 != e:
                    self.wait(e, toks[e2])

    def dma(self, eng, dst, src, sem):
        return self.op(eng, lambda e, d=dst, s=src: e.dma_start(d, s), inc=sem, n=16)


def _pack_lhsT(W):
    Kt, Mt = W.shape
    nk, nm = Kt // 128, Mt // 128
    w = W.reshape(nk, 128, nm, 128)
    return np.ascontiguousarray(w.transpose(2, 1, 0, 3).reshape(nm, 128, nk * 128))


def _host_inputs(inputs, core, cache={}):
    f32 = np.float32
    if not cache:
        W_qkv = np.asarray(inputs["L0_attn_a_W_qkv"], f32)
        b_qkv = np.asarray(inputs["L0_attn_a_b_qkv"], f32)
        W_d = np.asarray(inputs["L0_attn_a_W_dense"], f32)
        b_d = np.asarray(inputs["L0_attn_a_b_dense"], f32)
        qcols = np.concatenate([np.arange(h * 384, h * 384 + 128) for h in range(NH)])
        kcols = qcols + 128
        vcols = qcols + 256
        c = cache
        c["wq"] = _pack_lhsT(W_qkv[:, qcols])
        c["wk"] = _pack_lhsT(W_qkv[:, kcols])
        c["wv"] = np.ascontiguousarray(W_qkv[:, vcols].reshape(KC, 128, H))
        c["bq"] = np.ascontiguousarray(b_qkv[qcols].reshape(NH, 128).T)
        c["bk"] = np.ascontiguousarray(b_qkv[kcols].reshape(NH, 128).T)
        c["bv"] = np.ascontiguousarray(b_qkv[vcols].reshape(1, H))
        c["wd"] = _pack_lhsT(W_d)
        c["bd"] = np.ascontiguousarray(b_d.reshape(NH, 128).T)
        Wqd = np.asarray(inputs["L0_attn_b_Wq_down"], f32)
        Wqu = np.asarray(inputs["L0_attn_b_Wq_up"], f32)
        Wkd = np.asarray(inputs["L0_attn_b_Wkv_down"], f32)
        kup = np.asarray(inputs["L0_attn_b_k_up"], f32)
        vup = np.asarray(inputs["L0_attn_b_v_up"], f32)
        Wo = np.asarray(inputs["L0_attn_b_Wo"], f32)
        c["wqd"] = _pack_lhsT(np.ascontiguousarray(Wqd.T))
        c["wqu"] = _pack_lhsT(np.ascontiguousarray(Wqu.T))
        c["wkd"] = _pack_lhsT(np.ascontiguousarray(Wkd.T))
        c["kup"] = np.ascontiguousarray(
            np.stack([_pack_lhsT(kup[h])[0] for h in range(NH)]))
        c["vup"] = np.ascontiguousarray(
            vup.transpose(1, 0, 2).reshape(4, 128, NH * HD))
        c["wo"] = _pack_lhsT(np.ascontiguousarray(Wo.T))
        gw = np.asarray(inputs["L0_ffn_gate_w"], f32)
        uw = np.asarray(inputs["L0_ffn_up_w"], f32)
        dwn = np.asarray(inputs["L0_ffn_down_w"], f32)
        c["gw"] = _pack_lhsT(np.ascontiguousarray(gw.T))
        c["uw"] = _pack_lhsT(np.ascontiguousarray(uw.T))
        c["dw"] = _pack_lhsT(np.ascontiguousarray(dwn.T))
        c["gs"] = np.ascontiguousarray(
            np.asarray(inputs["L0_ffn_gate_s"], f32).reshape(44, 128).T)
        c["us"] = np.ascontiguousarray(
            np.asarray(inputs["L0_ffn_up_s"], f32).reshape(44, 128).T)
        c["ds"] = np.ascontiguousarray(
            np.asarray(inputs["L0_ffn_down_s"], f32).reshape(16, 128).T)
        c["ident"] = np.eye(128, dtype=f32)
        c["epsc"] = np.full((128, 1), EPS, f32)
        c["onesk1"] = np.ones((1, 128), f32)
        c["onesm"] = np.full((128, 1), 1.0 / H, f32)
        iu = np.arange(128)
        c["upmask"] = np.where(iu[None, :] > iu[:, None], np.float32(NEG), 0.0).astype(f32)
        c["lowmask"] = np.where(iu[:, None] > iu[None, :], np.float32(NEG), 0.0).astype(f32)
        c["ramp"] = np.ascontiguousarray(
            np.broadcast_to(np.arange(S, dtype=f32), (128, S)))
        c["qpos"] = (np.arange(16)[None, :] * 128 + iu[:, None]).astype(f32)
        c["dconst"] = (iu[:, None] - np.arange(512)[None, :]).astype(f32)
        c["dn16"] = np.broadcast_to((-128.0 * np.arange(16, dtype=f32))[None, :],
                                    (128, 16)).copy()
        R = np.zeros((128, 128), f32)
        for i in range(16):
            R[i, i + 16] = -1.0
            R[i + 16, i] = 1.0
        c["rt"] = np.ascontiguousarray(R.T)
        inv = 1.0 / (10000.0 ** (np.arange(0, ROT, 2, dtype=f32) / ROT))
        tv = np.arange(S, dtype=f32)
        fr = np.outer(tv, inv)
        emb = np.concatenate([fr, fr], axis=1).T
        cos = np.cos(emb).astype(f32)
        sin = np.sin(emb).astype(f32)
        cq = np.full((128, S), SCALE, f32); cq[:ROT] = cos * SCALE
        sq = np.zeros((128, S), f32);       sq[:ROT] = sin * SCALE
        ck = np.ones((128, S), f32);        ck[:ROT] = cos
        sk = np.zeros((128, S), f32);       sk[:ROT] = sin
        c["tbl_cq"], c["tbl_sq"], c["tbl_ck"], c["tbl_sk"] = cq, sq, ck, sk
    m = dict(cache)
    hs = np.asarray(inputs["L0_attn_a_hidden_states"], np.float32)
    hc = hs[2 * core: 2 * core + 2]
    m["x0"] = np.ascontiguousarray(hc.transpose(2, 0, 1).reshape(KC, 128, T))
    return m


def build_program():
    nc = bass.Bass("TRN2", target_bir_lowering=False)
    D = {}

    def din(name, shape, dt=F32):
        D[name] = nc.dram_tensor(name, list(shape), dt, kind="ExternalInput").ap()

    din("x0", (KC, 128, T))
    din("wq", (NH, 128, H)); din("wk", (NH, 128, H)); din("wv", (KC, 128, H))
    din("bq", (128, NH)); din("bk", (128, NH)); din("bv", (1, H))
    din("wd", (KC, 128, H)); din("bd", (128, NH))
    din("wqd", (12, 128, H)); din("wqu", (NH, 128, RQ)); din("wkd", (4, 128, H))
    din("kup", (NH, 128, RKV)); din("vup", (4, 128, H)); din("wo", (KC, 128, H))
    din("gw", (44, 128, H)); din("uw", (44, 128, H)); din("dw", (KC, 128, INTER))
    din("gs", (128, 44)); din("us", (128, 44)); din("ds", (128, KC))
    din("ident", (128, 128)); din("onesk1", (1, 128)); din("onesm", (128, 1))
    din("epsc", (128, 1))
    din("upmask", (128, 128)); din("lowmask", (128, 128))
    din("ramp", (128, S)); din("qpos", (128, 16)); din("rt", (128, 128))
    din("dconst", (128, 512)); din("dn16", (128, 16))
    for n in ("tbl_cq", "tbl_sq", "tbl_ck", "tbl_sk"):
        din(n, (128, S))

    D["out"] = nc.dram_tensor("out", [KC, 128, T], F32, kind="ExternalOutput").ap()
    if DEBUG:
        D["dbg_xln"] = nc.dram_tensor("dbg_xln", [KC, 128, S], BF16, kind="ExternalOutput").ap()
        D["dbg_qt"] = nc.dram_tensor("dbg_qt", [128, S], BF16, kind="ExternalOutput").ap()
        D["dbg_kt"] = nc.dram_tensor("dbg_kt", [128, S], BF16, kind="ExternalOutput").ap()
        D["dbg_ao"] = nc.dram_tensor("dbg_ao", [NH, 128, S], BF16, kind="ExternalOutput").ap()
        D["dbg_st"] = nc.dram_tensor("dbg_st", [4, 128, 16], F32, kind="ExternalOutput").ap()
        D["dbg_nmg"] = nc.dram_tensor("dbg_nmg", [128, 1], F32, kind="ExternalOutput").ap()
        D["dbg_fbc"] = nc.dram_tensor("dbg_fbc", [128, S], F32, kind="ExternalOutput").ap()
        D["dbg_ao0"] = nc.dram_tensor("dbg_ao0", [128, S], BF16, kind="ExternalOutput").ap()
    for n, dt in (("xA", F32), ("xB", F32), ("lnA", BF16), ("lnB", BF16)):
        D[n] = nc.dram_tensor(n, [KC, 128, T], dt, kind="Internal").ap()
    D["ao"] = nc.dram_tensor("ao", [NS, NH, 128, S], BF16, kind="Internal").ap()
    D["gwb"] = nc.dram_tensor("gwb", [44, 128, H], BF16, kind="Internal").ap()
    D["uwb"] = nc.dram_tensor("uwb", [44, 128, H], BF16, kind="Internal").ap()
    D["dwb"] = nc.dram_tensor("dwb", [KC, 128, INTER], BF16, kind="Internal").ap()

    pb = PB()

    def sb(name, shape, dt):
        return nc.alloc_sbuf_tensor(name, list(shape), dt)

    IDENT = sb("IDENT", (128, 128), F32)
    ONESK1 = sb("ONESK1", (1, 128), F32)
    ONESM = sb("ONESM", (128, 1), F32)
    EPSC = sb("EPSC", (128, 1), F32)
    UPM = sb("UPM", (128, 128), F32)
    LOWM = sb("LOWM", (128, 128), F32)
    QPOS = sb("QPOS", (128, 16), F32)
    DN16 = sb("DN16", (128, 16), F32)
    RT = sb("RT", (128, 128), F32)
    BQ = sb("BQ", (128, 16), F32)
    BK = sb("BK", (128, 16), F32)
    BD = sb("BD", (128, 16), F32)
    BVROW = sb("BVROW", (1, H), F32)
    GS = sb("GS", (128, 44), F32)
    US = sb("US", (128, 44), F32)
    DS = sb("DS", (128, 16), F32)
    NEGM = sb("NEGM", (128, 16), F32)
    SACC = sb("SACC", (128, 16), F32)
    LNSB = sb("LNSB", (128, 16), F32)
    GTB = sb("GTB", (128, 16), F32)
    FQT = sb("FQT", (128, 16), F32)
    KB = sb("KB", (128, 16), F32)
    QPS = sb("QPS", (128, 16), F32)
    NM1 = sb("NM1", (128, 1), F32)
    NMG = sb("NMG", (128, 1), F32)
    TROW = sb("TROW", (1, 128), F32)
    MGR = sb("MGR", (1, 1), F32)
    FROW = sb("FROW", (1, S), F32)
    SROW = sb("SROW", (1, 512), F32)
    SROW2 = sb("SROW2", (1, 512), F32)
    ATN = tuple(sb(f"ATN{i}", (128, 512), BF16) for i in range(4))
    PREP = tuple(sb(f"PREP{i}", (128, H), BF16) for i in range(2))

    PS_A = nc.alloc_psum_tensor("PS_A", [128, 2048], F32)
    PS_B = nc.alloc_psum_tensor("PS_B", [128, 2048], F32)

    class Scope:
        def __init__(self):
            self.base = nc.sbuf_base
        def sb(self, name, shape, dt):
            return nc.alloc_sbuf_tensor(name, list(shape), dt)
        def close(self):
            nc.sbuf_base = self.base

    for dst, src in [(IDENT, "ident"), (ONESK1, "onesk1"), (ONESM, "onesm"),
                     (UPM, "upmask"), (LOWM, "lowmask"), (EPSC, "epsc"),
                     (QPOS, "qpos"), (RT, "rt"), (DN16, "dn16"),
                     (BQ, "bq"), (BK, "bk"),
                     (BD, "bd"), (BVROW, "bv"), (GS, "gs"), (US, "us"), (DS, "ds")]:
        pb.dma("sync", dst[:, :], D[src][:, :], "d_const")
    tok_const = pb.tok("d_const")
    for e in ("tensor", "vector", "scalar"):
        pb.wait(e, tok_const)

    prep_jobs = []
    for m in range(44):
        prep_jobs.append((D["gw"][m], D["gwb"][m], H))
        prep_jobs.append((D["uw"][m], D["uwb"][m], H))
    for f in range(KC):
        for c0 in range(0, INTER, 2048):
            w = min(2048, INTER - c0)
            prep_jobs.append((D["dw"][f, :, c0:c0 + w], D["dwb"][f, :, c0:c0 + w], w))
    prep_state = {"i": 0, "tok": [None, None]}

    def prep_step(k=2):
        for _ in range(k):
            i = prep_state["i"]
            if i >= len(prep_jobs):
                return
            slot = i % 2
            src, dst, w = prep_jobs[i]
            pb.wait("gpsimd", prep_state["tok"][slot])
            tl = pb.dma("gpsimd", PREP[slot][:, 0:w], src, f"d_prep{slot}")
            pb.wait("sync", tl)
            ts = pb.dma("sync", dst, PREP[slot][:, 0:w], f"d_preps{slot}")
            prep_state["tok"][slot] = ts
            prep_state["i"] = i + 1

    def prep_finish():
        while prep_state["i"] < len(prep_jobs):
            prep_step(8)

    def mm_loop(eng_specs, final_inc="pe"):
        tok = None
        nsp = len(eng_specs)
        for i, (o, l, r, st, sp) in enumerate(eng_specs):
            inc = final_inc if i == nsp - 1 else None
            tok = pb.op("tensor", lambda e, o=o, l=l, r=r, st=st, sp=sp:
                        e.matmul(o, l, r, start=st, stop=sp, skip_group_check=True),
                        inc=inc)
        return tok

    STORE_SEMS = ("d_dxn0", "d_dxn1", "d_dxl", "d_fxn0", "d_fxn1", "d_fxl",
                  "d_ao", "d_fx20", "d_fx21", "d_dx20", "d_dx21")

    def hbm_fence(eng="sync"):
        for s_ in STORE_SEMS:
            pb.waitc(eng, s_)

    def ln_finalize(ps_sum, ps_sq, Mbc, Rbc, MRt, width, tok_pe_stats, psA_off):
        pb.wait("scalar", tok_pe_stats)
        pb.waitc("scalar", "pe")
        pb.op("scalar", lambda e, o=SROW[0:1, 0:width], i=ps_sum:
              e.activation(o, i, AF.Copy), inc="act")
        t2 = pb.op("scalar", lambda e, o=SROW2[0:1, 0:width], i=ps_sq:
                   e.activation(o, i, AF.Copy), inc="act")
        pb.wait("tensor", t2)
        pb.waitc("tensor", "dve")
        bm = pb.op("tensor", lambda e, o=PS_A[:, psA_off:psA_off + width],
                   l=ONESK1[:, :], r=SROW[0:1, 0:width]:
                   e.matmul(o, l, r, start=True, stop=True, skip_group_check=True),
                   inc="pe")
        bq_ = pb.op("tensor", lambda e, o=PS_A[:, psA_off + width:psA_off + 2 * width],
                    l=ONESK1[:, :], r=SROW2[0:1, 0:width]:
                    e.matmul(o, l, r, start=True, stop=True, skip_group_check=True),
                    inc="pe")
        pb.wait("scalar", bq_)
        pb.op("scalar", lambda e, o=Mbc[:, 0:width], i=PS_A[:, psA_off:psA_off + width]:
              e.activation(o, i, AF.Copy), inc="act")
        tsq = pb.op("scalar", lambda e, o=MRt[:, 0:width],
                    i=PS_A[:, psA_off:psA_off + width]:
                    e.activation(o, i, AF.Square), inc="act")
        pb.wait("vector", tsq)
        tv = pb.op("vector", lambda e, o=Rbc[:, 0:width],
                   a=PS_A[:, psA_off + width:psA_off + 2 * width], b=MRt[:, 0:width]:
                   e.tensor_tensor(o, a, b, ALU.subtract), inc="dve")
        pb.wait("scalar", tv)
        pb.op("scalar", lambda e, o=Rbc[:, 0:width], i=Rbc[:, 0:width]:
              e.activation(o, i, AF.Ln, bias=EPSC[:, :]), inc="act")
        te = pb.op("scalar", lambda e, o=Rbc[:, 0:width], i=Rbc[:, 0:width]:
                   e.activation(o, i, AF.Exp, scale=-0.5), inc="act")
        pb.wait("vector", te)
        tmr = pb.op("vector", lambda e, o=MRt[:, 0:width], a=Mbc[:, 0:width],
                    b=Rbc[:, 0:width]:
                    e.tensor_tensor(o, a, b, ALU.mult), inc="dve")
        return tmr

    def ln_prep(sc, pfx, s, XLN, scr, fbc):
        XPV = [scr[:, 0:512], scr[:, 512:1024]]
        SQT = scr[:, 1024:1536]
        MBC = fbc[:, 0:512]
        RBC = fbc[:, 512:1024]
        MRT = fbc[:, 1024:1536]
        T1 = fbc[:, 1536:2048]
        xv_tok = [None, None]
        for tau in range(4):
            g0 = s * S + tau * 512
            sq_tok = None
            pe_sq = None
            for kc in range(KC):
                slot = kc % 2
                pb.wait("sync", xv_tok[slot])
                tl = pb.dma("sync", XPV[slot], D["x0"][kc, :, g0:g0 + 512],
                            f"d_lpx{slot}")
                pb.wait("tensor", tl)
                pb.op("tensor", lambda e, o=PS_B[0:1, 0:512], l=ONESM[:, :],
                      r=XPV[slot], st=(kc == 0), sp=(kc == KC - 1):
                      e.matmul(o, l, r, start=st, stop=sp, skip_group_check=True))
                pb.wait("scalar", tl)
                pb.wait("scalar", sq_tok)
                ta = pb.op("scalar", lambda e, o=SQT, i=XPV[slot]:
                           e.activation(o, i, AF.Square), inc="act")
                pb.wait("tensor", ta)
                pe_sq = pb.op("tensor", lambda e, o=PS_B[0:1, 512:1024], l=ONESM[:, :],
                              r=SQT, st=(kc == 0), sp=(kc == KC - 1):
                              e.matmul(o, l, r, start=st, stop=sp,
                                       skip_group_check=True), inc="pe")
                sq_tok = pe_sq
                xv_tok[slot] = pe_sq
            tfin = ln_finalize(PS_B[0:1, 0:512], PS_B[0:1, 512:1024],
                               MBC, RBC, MRT, 512, pe_sq, 0)
            for kc in range(KC):
                slot = kc % 2
                pb.wait("sync", xv_tok[slot])
                tl = pb.dma("sync", XPV[slot], D["x0"][kc, :, g0:g0 + 512],
                            f"d_lpx{slot}")
                pb.wait("vector", tl)
                pb.wait("vector", tfin)
                pb.op("vector", lambda e, o=T1, a=XPV[slot], b=RBC:
                      e.tensor_tensor(o, a, b, ALU.mult), inc="dve")
                t2 = pb.op("vector", lambda e,
                           o=XLN[:, kc * S + tau * 512: kc * S + tau * 512 + 512],
                           a=T1, b=MRT:
                           e.tensor_tensor(o, a, b, ALU.subtract), inc="dve")
                xv_tok[slot] = t2

    def attn_heads(lay, s, XLN, bufs):
        QT = bufs["QT"]; KT = bufs["KT"]; VSB = bufs["VSB"]
        TRASH = bufs["TRASH"]; FBC = bufs["FBC"]; AO = bufs["AO"]
        tok_ao_store = None
        tok_wq_free = [None, None]
        tok_wk_free = [None, None]
        tok_wv_free = None
        tok_exp_c1_last = None
        tok_fmult = [None, None, None, None]
        tok_attnv = {}
        tok_exp_c2 = [None, None, None, None]
        for h in range(HEADS):
            slot = h % 2
            if NO_H1_LOADS and h > 0:
                twq = twk = None
            elif lay == "a":
                pb.wait("gpsimd", tok_wq_free[slot])
                twq = pb.dma("gpsimd", bufs["WQ"][slot][:, :], D["wq"][h], f"d_wq{slot}")
                pb.wait("gpsimd", tok_wk_free[slot])
                twk = pb.dma("gpsimd", bufs["WK"][slot][:, :], D["wk"][h], f"d_wk{slot}")
            else:
                pb.wait("gpsimd", tok_wq_free[slot])
                twq = pb.dma("gpsimd", bufs["WQ"][slot][:, :], D["wqu"][h], f"d_wq{slot}")
                pb.wait("gpsimd", tok_wk_free[slot])
                twk = pb.dma("gpsimd", bufs["WK"][slot][:, :], D["kup"][h], f"d_wk{slot}")
            if h % 2 == 0:
                pb.wait("gpsimd", tok_wv_free)
                nkv = KC if lay == "a" else 4
                wsrc3 = D["wv"] if lay == "a" else D["vup"]
                for kc in range(nkv):
                    twv = pb.dma("gpsimd", bufs["WV"][:, kc * 256:(kc + 1) * 256],
                                 wsrc3[kc, :, h * 128:(h + 2) * 128], "d_wv")
            if not (NO_H1_PREP and h > 0):
                prep_step(3)

            if h % 2 == 0:
                tok_ve = [None, None]
                pb.waitc("tensor", "act")
                pb.waitc("tensor", "dve")
                for tm in range(KC):
                    pv = PS_A[:, (tm % 2) * 512:(tm % 2) * 512 + 256]
                    pb.wait("tensor", twv)
                    pb.wait("tensor", tok_ve[tm % 2])
                    specs = []
                    for kc in range(nkv):
                        if lay == "a":
                            l = XLN[:, kc * S + tm * 128: kc * S + tm * 128 + 128]
                        else:
                            l = bufs["CC"][:, kc * S + tm * 128: kc * S + tm * 128 + 128]
                        r = bufs["WV"][:, kc * 256:(kc + 1) * 256]
                        specs.append((pv, l, r, kc == 0, kc == nkv - 1))
                    tv = mm_loop(specs)
                    ov = VSB[:, tm * 256:(tm + 1) * 256]
                    if lay == "a":
                        pb.wait("vector", tv)
                        te = pb.op("vector", lambda e, o=ov, i=pv,
                                   b=bufs["BVBC"][:, h * 128:h * 128 + 256]:
                                   e.scalar_tensor_tensor(o, i, 1.0, b, op0=ALU.mult,
                                                          op1=ALU.add), inc="dve")
                    else:
                        pb.wait("scalar", tv)
                        te = pb.op("scalar", lambda e, o=ov, i=pv:
                                   e.activation(o, i, AF.Copy), inc="act")
                    tok_ve[tm % 2] = te
                tok_wv_free = tok_ve[1]
                tok_v_done = (tok_ve[0], tok_ve[1])

            if h > 0 and H1_STAGE < 1:
                continue
            nq_kc = KC if lay == "a" else 12
            qsrc = XLN if lay == "a" else bufs["CQ"]
            pb.wait("tensor", twq)
            pb.waitc("tensor", "act")
            pb.waitc("tensor", "dve")
            for t_ in tok_v_done:
                pb.wait("tensor", t_)
            specs = []
            for n in range(4):
                for kc in range(nq_kc):
                    specs.append((PS_A[:, n * 512:(n + 1) * 512],
                                  bufs["WQ"][slot][:, kc * 128:(kc + 1) * 128],
                                  qsrc[:, kc * S + n * 512: kc * S + (n + 1) * 512],
                                  kc == 0, kc == nq_kc - 1))
            tq = mm_loop(specs)
            tok_wq_free[slot] = tq
            if lay == "a":
                QF = bufs["QF"]
                pb.wait("scalar", tq)
                pb.waitc("scalar", "dve")
                tqf = pb.op("scalar", lambda e, o=QF[:, :], i=PS_A[:, :],
                            bb=BQ[:, h:h + 1]:
                            e.activation(o, i, AF.Identity, bias=bb),
                            inc="act")
                pb.wait("tensor", tqf)
                pb.waitc("tensor", "dve")
                specs = [(PS_B[:, n * 512:(n + 1) * 512], RT[:, :],
                          QF[:, n * 512:(n + 1) * 512], True, True) for n in range(4)]
                trq = mm_loop(specs)
                pb.wait("vector", trq)
                pb.op("vector", lambda e, o=QF[:, :], a=QF[:, :], b=bufs["TCQ"][:, :]:
                      e.tensor_tensor(o, a, b, ALU.mult), inc="dve")
                pb.op("vector", lambda e, o=PS_B[:, :], a=PS_B[:, :], b=bufs["TSQ"][:, :]:
                      e.tensor_tensor(o, a, b, ALU.mult), inc="dve")
                tqt = pb.op("vector", lambda e, o=QT[:, :], a=QF[:, :], b=PS_B[:, :]:
                            e.tensor_tensor(o, a, b, ALU.add), inc="dve")
            else:
                pb.wait("scalar", tq)
                tqt = pb.op("scalar", lambda e, o=QT[:, :], i=PS_A[:, :]:
                            e.activation(o, i, AF.Copy, scale=float(SCALE)), inc="act")

            if h > 0 and H1_STAGE < 2:
                continue
            nk_kc = KC if lay == "a" else 4
            ksrc = XLN if lay == "a" else bufs["CC"]
            pb.wait("tensor", twk)
            if lay == "a":
                pb.wait("tensor", tqf)
            else:
                pb.wait("tensor", tqt)
            specs = []
            for n in range(4):
                for kc in range(nk_kc):
                    specs.append((PS_A[:, n * 512:(n + 1) * 512],
                                  bufs["WK"][slot][:, kc * 128:(kc + 1) * 128],
                                  ksrc[:, kc * S + n * 512: kc * S + (n + 1) * 512],
                                  kc == 0, kc == nk_kc - 1))
            tk = mm_loop(specs)
            tok_wk_free[slot] = tk
            if lay == "a":
                pb.wait("scalar", tk)
                pb.wait("scalar", tqt)
                tkf = pb.op("scalar", lambda e, o=QF[:, :], i=PS_A[:, :],
                            bb=BK[:, h:h + 1]:
                            e.activation(o, i, AF.Identity, bias=bb),
                            inc="act")
                pb.wait("tensor", tkf)
                pb.wait("tensor", tqt)
                specs = [(PS_B[:, n * 512:(n + 1) * 512], RT[:, :],
                          QF[:, n * 512:(n + 1) * 512], True, True) for n in range(4)]
                trk = mm_loop(specs)
                pb.wait("vector", trk)
                pb.op("vector", lambda e, o=QF[:, :], a=QF[:, :], b=bufs["TCK"][:, :]:
                      e.tensor_tensor(o, a, b, ALU.mult), inc="dve")
                pb.op("vector", lambda e, o=PS_B[:, :], a=PS_B[:, :], b=bufs["TSK"][:, :]:
                      e.tensor_tensor(o, a, b, ALU.mult), inc="dve")
                tkt = pb.op("vector", lambda e, o=KT[:, :], a=QF[:, :], b=PS_B[:, :]:
                            e.tensor_tensor(o, a, b, ALU.add), inc="dve")
                tok_qkt = tkt
            else:
                pb.wait("scalar", tk)
                tkt = pb.op("scalar", lambda e, o=KT[:, :], i=PS_A[:, :]:
                            e.activation(o, i, AF.Copy), inc="act")
                tok_qkt = tkt

            if DEBUG and lay == "a" and s == 0 and h == 0:
                pb.wait("sync", tok_qkt)
                pb.dma("sync", D["dbg_qt"][:, :], QT[:, :], "d_dbg")
                pb.dma("sync", D["dbg_kt"][:, :], KT[:, :], "d_dbg")
            if lay == "b":
                trh = pb.op("vector", lambda e, o=bufs["RH"][:, :], i=bufs["RAMP"][:, :],
                            sl=float(SLOPES[h]):
                            e.tensor_scalar(o, i, sl, None, op0=ALU.mult), inc="dve")
                tqps = pb.op("vector", lambda e, o=QPS[:, :], i=QPOS[:, :],
                             sl=-float(SLOPES[h]):
                             e.tensor_scalar(o, i, sl, None, op0=ALU.mult), inc="dve")
                talb = pb.op("vector", lambda e, o=bufs["ALB"][:, :], i=bufs["DCONST"][:, :],
                             sl=float(SLOPES[h]):
                             e.tensor_scalar(o, i, sl, None, op0=ALU.mult), inc="dve")

            if h > 0 and H1_STAGE < 3:
                continue
            pb.waitc("tensor", "dve")
            pb.waitc("tensor", "act")
            if lay == "a":
                pb.wait("tensor", tok_qkt)
                pb.wait("tensor", tkf)
            else:
                pb.wait("tensor", tok_qkt)
            tok_exp_pp = [None, None]
            tok_exp_qi = None
            for qi in range(16):
                klen = (qi + 1) * 128
                base = PS_A if qi % 2 == 0 else PS_B
                pb.wait("tensor", tok_exp_pp[qi % 2])
                specs = []
                for n in range(0, klen, 512):
                    w = min(512, klen - n)
                    specs.append((base[:, n:n + w], QT[:, qi * 128:(qi + 1) * 128],
                                  KT[:, n:n + w], True, True))
                tsc = mm_loop(specs)
                pb.wait("vector", tsc)
                if lay == "b":
                    pb.wait("vector", trh)
                    pb.wait("vector", tqps)
                    pb.op("vector", lambda e, o=base[:, 0:klen], i=base[:, 0:klen],
                          s1=QPS[:, qi:qi + 1], b=bufs["RH"][:, 0:klen]:
                          e.scalar_tensor_tensor(o, i, s1, b, op0=ALU.add, op1=ALU.add),
                          inc="dve")
                pb.op("vector", lambda e, o=base[:, klen - 128:klen],
                      a=base[:, klen - 128:klen], b=UPM[:, :]:
                      e.tensor_tensor(o, a, b, ALU.add), inc="dve")
                tmx = pb.op("vector", lambda e, o=NEGM[:, qi:qi + 1], i=base[:, 0:klen]:
                            e.reduce_max(o, i, axis=AX.X, negate=True), inc="dve")
                pb.wait("scalar", tmx)
                tok_exp_qi = pb.op("scalar", lambda e, o=TRASH[:, 0:klen],
                                   i=base[:, 0:klen], b=NEGM[:, qi:qi + 1],
                                   a=SACC[:, qi:qi + 1]:
                                   e.activation(o, i, AF.Exp, bias=b, accum_out=a),
                                   inc="act")
                tok_exp_pp[qi % 2] = tok_exp_qi
            tok_exp_c1_last = tok_exp_qi

            if h > 0 and H1_STAGE < 4:
                continue
            pb.wait("scalar", tok_exp_c1_last)
            pb.op("scalar", lambda e, o=LNSB[:, :], i=SACC[:, :]:
                  e.activation(o, i, AF.Ln), inc="act")
            pb.wait("vector", tok_exp_c1_last)
            tnm = pb.op("vector", lambda e, o=NM1[:, :], i=NEGM[:, :]:
                        e.tensor_reduce(o, i, axis=AX.X, op=ALU.min), inc="dve")
            pb.wait("tensor", tnm)
            pb.waitc("tensor", "dve")
            pb.waitc("tensor", "act")
            ttr = pb.op("tensor", lambda e, o=PS_B[0:1, 0:128], l=NM1[:, :],
                        r=IDENT[:, :]:
                        e.matmul(o, l, r, start=True, stop=True,
                                 skip_group_check=True), inc="pe")
            pb.wait("scalar", ttr)
            ttrc = pb.op("scalar", lambda e, o=TROW[:, :], i=PS_B[0:1, 0:128]:
                         e.activation(o, i, AF.Copy), inc="act")
            pb.wait("vector", ttrc)
            tmg = pb.op("vector", lambda e, o=MGR[:, :], i=TROW[:, :]:
                        e.tensor_reduce(o, i, axis=AX.X, op=ALU.min), inc="dve")
            pb.wait("tensor", tmg)
            tbg = pb.op("tensor", lambda e, o=PS_B[:, 512:513], l=ONESK1[:, :],
                        r=MGR[:, :]:
                        e.matmul(o, l, r, start=True, stop=True,
                                 skip_group_check=True), inc="pe")
            pb.wait("scalar", tbg)
            tnmg = pb.op("scalar", lambda e, o=NMG[:, :], i=PS_B[:, 512:513]:
                         e.activation(o, i, AF.Copy), inc="act")
            pb.wait("vector", tnmg)
            tg = pb.op("vector", lambda e, o=GTB[:, :], i=NEGM[:, :], s1=NMG[:, :],
                       b=LNSB[:, :]:
                       e.scalar_tensor_tensor(o, i, s1, b, op0=ALU.subtract,
                                              op1=ALU.subtract), inc="dve")
            if lay == "b":
                tkb = pb.op("vector", lambda e, o=KB[:, :], i=DN16[:, :],
                            sl=float(SLOPES[h]), s2=NMG[:, :]:
                            e.tensor_scalar(o, i, sl, s2, op0=ALU.mult, op1=ALU.add),
                            inc="dve")
            pb.wait("scalar", tg)
            tf = pb.op("scalar", lambda e, o=FQT[:, :], i=GTB[:, :]:
                       e.activation(o, i, AF.Exp), inc="act")
            tfr = None
            trc_toks = [None, None, None, None]
            for qi in range(16):
                pb.wait("tensor", tf)
                pb.wait("tensor", trc_toks[qi % 4])
                ttr2 = pb.op("tensor", lambda e,
                             o=PS_B[0:1, (qi % 4) * 512:(qi % 4) * 512 + 128],
                             l=FQT[:, qi:qi + 1], r=IDENT[:, :]:
                             e.matmul(o, l, r, start=True, stop=True,
                                      skip_group_check=True), inc="pe")
                pb.wait("scalar", ttr2)
                tfr = pb.op("scalar", lambda e, o=FROW[0:1, qi * 128:(qi + 1) * 128],
                            i=PS_B[0:1, (qi % 4) * 512:(qi % 4) * 512 + 128]:
                            e.activation(o, i, AF.Copy), inc="act")
                trc_toks[qi % 4] = tfr
            tfb = None
            for n in range(4):
                pb.wait("tensor", tfr)
                tb_ = pb.op("tensor", lambda e, o=PS_B[:, n * 512:(n + 1) * 512],
                            l=ONESK1[:, :], r=FROW[0:1, n * 512:(n + 1) * 512]:
                            e.matmul(o, l, r, start=True, stop=True,
                                     skip_group_check=True), inc="pe")
                pb.wait("scalar", tb_)
                tfb = pb.op("scalar", lambda e, o=FBC[:, n * 512:(n + 1) * 512],
                            i=PS_B[:, n * 512:(n + 1) * 512]:
                            e.activation(o, i, AF.Copy), inc="act")

            if DEBUG and lay == "a" and s == 0 and h == 0:
                pb.barrier()
                pb.dma("sync", D["dbg_st"][0], NEGM[:, :], "d_dbg")
                pb.dma("sync", D["dbg_st"][1], SACC[:, :], "d_dbg")
                pb.dma("sync", D["dbg_st"][2], FQT[:, :], "d_dbg")
                pb.dma("sync", D["dbg_nmg"][:, :], NMG[:, :], "d_dbg")
                pb.dma("sync", D["dbg_fbc"][:, :], FBC[:, :], "d_dbg")
                pb.barrier()
            if h > 0 and H1_STAGE < 5:
                continue
            if lay == "b":
                pb.wait("scalar", tkb)
            pb.waitc("tensor", "act")
            for j in range(4):
                ov = PS_B[:, 1024 + (j % 2) * 512: 1024 + (j % 2) * 512 + 512]
                tattnv = None
                for kc in range(4 * j + 4):
                    jjs = max(0, kc - 4 * j)
                    w = 512 - jjs * 128
                    q0 = j * 512 + jjs * 128
                    par = kc % 4
                    pv = PS_A[:, par * 512 + jjs * 128: par * 512 + 512]
                    pb.wait("tensor", tok_exp_c2[par])
                    if kc == 0:
                        pb.wait("tensor", tfb)
                        pb.wait("vector", tfb)
                        pb.wait("tensor", tok_fmult[(j + 2) % 4])
                    tst = pb.op("tensor", lambda e, o=pv,
                                l=KT[:, kc * 128:(kc + 1) * 128],
                                r=QT[:, q0: (j + 1) * 512]:
                                e.matmul(o, l, r, start=True, stop=True,
                                         skip_group_check=True), inc="pe")
                    tmk = None
                    if lay == "b":
                        pb.wait("vector", tst)
                        pb.wait("vector", talb)
                        tmk = pb.op("vector", lambda e, o=pv, a=pv, b=bufs["ALB"][:, 0:w]:
                                    e.tensor_tensor(o, a, b, ALU.add), inc="dve")
                    if kc >= 4 * j:
                        dg = PS_A[:, par * 512 + jjs * 128: par * 512 + jjs * 128 + 128]
                        pb.wait("vector", tst)
                        tmk = pb.op("vector", lambda e, o=dg, a=dg, b=LOWM[:, :]:
                                    e.tensor_tensor(o, a, b, ALU.add), inc="dve")
                    pb.wait("scalar", tst)
                    pb.wait("scalar", tmk)
                    pb.wait("scalar", tok_attnv.get(par))
                    delta = 4 * j + jjs - kc
                    bias_ap = NMG[:, :] if lay == "a" else KB[:, delta:delta + 1]
                    te2 = pb.op("scalar", lambda e, o=ATN[par][:, 0:w], i=pv, b=bias_ap:
                                e.activation(o, i, AF.Exp, bias=b), inc="act")
                    tok_exp_c2[par] = te2
                    pb.wait("tensor", te2)
                    vl = VSB[:, kc * 256 + (h % 2) * 128: kc * 256 + (h % 2) * 128 + 128]
                    tattnv = pb.op("tensor", lambda e, o=ov[:, jjs * 128:512], l=vl,
                                   r=ATN[par][:, 0:w], st=(kc == 0), sp=(kc == 4 * j + 3):
                                   e.matmul(o, l, r, start=st, stop=sp,
                                            skip_group_check=True), inc="pe")
                    tok_attnv[par] = tattnv
                pb.wait("vector", tattnv)
                pb.wait("vector", tok_ao_store)
                tfm = pb.op("vector", lambda e, o=AO[:, j * 512:(j + 1) * 512], a=ov,
                            b=FBC[:, j * 512:(j + 1) * 512]:
                            e.tensor_tensor(o, a, b, ALU.mult), inc="dve")
                tok_fmult[j] = tfm
            if DEBUG and lay == "a" and s == 0 and h == 0:
                pb.barrier()
                pb.dma("sync", D["dbg_ao0"][:, :], AO[:, :], "d_dbg")
                pb.barrier()
            pb.wait("sync", tok_fmult[3])
            tok_ao_store = pb.dma("sync", D["ao"][s, h], AO[:, :], "d_ao")
        pb.wait("sync", tok_ao_store)

    def dense_epilogue(sc, pfx, s, wsrc_name, bias, xprev, xnext, xlnout):
        AOB = sc.sb(f"{pfx}_aob", (128, KC * 512), BF16)
        WDB = [sc.sb(f"{pfx}_wdb{i}", (128, H), BF16) for i in range(2)]
        XPV = [sc.sb(f"{pfx}_xpv{i}", (128, 512), F32) for i in range(2)]
        XNW = [sc.sb(f"{pfx}_xnw{i}", (128, 512), F32) for i in range(2)]
        SQT = sc.sb(f"{pfx}_sqt", (128, 512), F32)
        MBC = sc.sb(f"{pfx}_mbc", (128, 512), F32)
        RBC = sc.sb(f"{pfx}_rbc", (128, 512), F32)
        MRT = sc.sb(f"{pfx}_mrt", (128, 512), F32)
        T1 = sc.sb(f"{pfx}_t1", (128, 512), F32)
        XLT = sc.sb(f"{pfx}_xlt", (128, 512), BF16)
        wsrc = D[wsrc_name]
        hbm_fence()
        tok_w = [None, None]
        tok_aob = None
        tok_ev = [None, None]
        tok_store = [None, None]
        tok_stats = [None, None]
        tok_xp = [None, None]
        tok_xl = None
        for tau in range(4):
            g0 = s * S + tau * 512
            pb.wait("sync", tok_aob)
            pb.wait("sync", pb.tok("pe"))
            for kc in range(KC):
                taob = pb.dma("sync", AOB[:, kc * 512:(kc + 1) * 512],
                              D["ao"][s, kc, :, tau * 512:tau * 512 + 512], "d_aob")
            tok_aob = taob
            pe_sq = None
            sq_tok = None
            for f in range(KC):
                slot = f % 2
                pb.wait("gpsimd", tok_w[slot])
                tw = pb.dma("gpsimd", WDB[slot][:, :], wsrc[f], f"d_de{slot}")
                prep_step(1)
                pb.wait("tensor", tw)
                pb.wait("tensor", taob)
                pb.wait("tensor", tok_ev[slot])
                specs = [(PS_A[:, slot * 512:(slot + 1) * 512],
                          WDB[slot][:, kc * 128:(kc + 1) * 128],
                          AOB[:, kc * 512:(kc + 1) * 512], kc == 0, kc == KC - 1)
                         for kc in range(KC)]
                td = mm_loop(specs)
                tok_w[slot] = td
                pb.wait("sync", tok_ev[slot])
                txp = pb.dma("sync", XPV[slot][:, :], D[xprev][f, :, g0:g0 + 512],
                             f"d_dxp{slot}")
                pb.wait("vector", td)
                pb.wait("vector", txp)
                pb.wait("vector", tok_store[slot])
                pb.wait("vector", tok_stats[slot])
                if bias is not None:
                    txn = pb.op("vector", lambda e, o=XNW[slot][:, :],
                                i=PS_A[:, slot * 512:(slot + 1) * 512],
                                s1=bias[:, f:f + 1], b=XPV[slot][:, :]:
                                e.scalar_tensor_tensor(o, i, s1, b, op0=ALU.add,
                                                       op1=ALU.add), inc="dve")
                else:
                    txn = pb.op("vector", lambda e, o=XNW[slot][:, :],
                                i=PS_A[:, slot * 512:(slot + 1) * 512],
                                b=XPV[slot][:, :]:
                                e.tensor_tensor(o, i, b, ALU.add), inc="dve")
                tok_ev[slot] = txn
                pb.wait("sync", txn)
                tst = pb.dma("sync", D[xnext][f, :, g0:g0 + 512], XNW[slot][:, :],
                             f"d_dxn{slot}")
                tok_store[slot] = tst
                if xlnout is not None:
                    pb.wait("tensor", txn)
                    pb.op("tensor", lambda e, o=PS_B[0:1, 0:512], l=ONESM[:, :],
                          r=XNW[slot][:, :], st=(f == 0), sp=(f == KC - 1):
                          e.matmul(o, l, r, start=st, stop=sp, skip_group_check=True))
                    pb.wait("scalar", txn)
                    pb.wait("scalar", sq_tok)
                    ta = pb.op("scalar", lambda e, o=SQT[:, :], i=XNW[slot][:, :]:
                               e.activation(o, i, AF.Square), inc="act")
                    pb.wait("tensor", ta)
                    pe_sq = pb.op("tensor", lambda e, o=PS_B[0:1, 512:1024],
                                  l=ONESM[:, :], r=SQT[:, :],
                                  st=(f == 0), sp=(f == KC - 1):
                                  e.matmul(o, l, r, start=st, stop=sp,
                                           skip_group_check=True), inc="pe")
                    sq_tok = pe_sq
                    tok_stats[slot] = pe_sq
            if xlnout is not None:
                tfin = ln_finalize(PS_B[0:1, 0:512], PS_B[0:1, 512:1024],
                                   MBC, RBC, MRT, 512, pe_sq, 1024)
                pb.waitc("sync", "d_dxn0")
                pb.waitc("sync", "d_dxn1")
                for f in range(KC):
                    slot = f % 2
                    pb.wait("sync", tok_xp[slot])
                    tl = pb.dma("sync", XPV2 := XPV[slot][:, :],
                                D[xnext][f, :, g0:g0 + 512], f"d_dx2{slot}")
                    pb.wait("vector", tl)
                    pb.wait("vector", tfin)
                    pb.wait("vector", tok_xl)
                    pb.op("vector", lambda e, o=T1[:, :], a=XPV2, b=RBC[:, :]:
                          e.tensor_tensor(o, a, b, ALU.mult), inc="dve")
                    t2 = pb.op("vector", lambda e, o=XLT[:, :], a=T1[:, :],
                               b=MRT[:, :]:
                               e.tensor_tensor(o, a, b, ALU.subtract), inc="dve")
                    tok_xp[slot] = t2
                    pb.wait("sync", t2)
                    tok_xl = pb.dma("sync", D[xlnout][f, :, g0:g0 + 512], XLT[:, :],
                                    "d_dxl")
                tok_ev = [pb.tok("dve"), pb.tok("dve")]

    def ffn(sc, pfx, xln_src, xprev, xnext, xlnout):
        HDN = sc.sb(f"{pfx}_hdn", (128, 44 * 512), BF16)
        XLT = [sc.sb(f"{pfx}_xlt{i}", (128, KC * 512), BF16) for i in range(2)]
        GWB = [sc.sb(f"{pfx}_gwb{i}", (128, H), BF16) for i in range(2)]
        UWB = [sc.sb(f"{pfx}_uwb{i}", (128, H), BF16) for i in range(2)]
        DWB = [sc.sb(f"{pfx}_dwb{i}", (128, INTER), BF16) for i in range(2)]
        SG = [sc.sb(f"{pfx}_sg{i}", (128, 512), F32) for i in range(2)]
        XPV = [sc.sb(f"{pfx}_xpv{i}", (128, 512), F32) for i in range(2)]
        XNW = [sc.sb(f"{pfx}_xnw{i}", (128, 512), F32) for i in range(2)]
        SQT = sc.sb(f"{pfx}_sqt", (128, 512), F32)
        MBC = sc.sb(f"{pfx}_mbc", (128, 512), F32)
        RBC = sc.sb(f"{pfx}_rbc", (128, 512), F32)
        MRT = sc.sb(f"{pfx}_mrt", (128, 512), F32)
        T1 = sc.sb(f"{pfx}_t1", (128, 512), F32)
        XLO = sc.sb(f"{pfx}_xlo", (128, 512), BF16)
        hbm_fence()
        pb.waitc("gpsimd", "d_preps0")
        pb.waitc("gpsimd", "d_preps1")
        tok_g = [None, None]; tok_u = [None, None]; tok_d = [None, None]
        tok_sg = [None, None]; tok_hd = [None, None]
        tok_ev = [None, None]
        tok_store = [None, None]; tok_stats = [None, None]
        tok_xp = [None, None]
        tok_xlt_use = [None, None]
        tok_xl = None
        for tt in range(8):
            t0 = tt * 512
            xslot = tt % 2
            pb.wait("sync", tok_xlt_use[xslot])
            for kc in range(KC):
                txl = pb.dma("sync", XLT[xslot][:, kc * 512:(kc + 1) * 512],
                             D[xln_src][kc, :, t0:t0 + 512], f"d_fxlt{xslot}")
            for m in range(44):
                slot = m % 2
                pb.wait("gpsimd", tok_g[slot])
                tgw = pb.dma("gpsimd", GWB[slot][:, :], D["gwb"][m], f"d_fg{slot}")
                pb.wait("scalar", tok_u[slot])
                tuw = pb.dma("scalar", UWB[slot][:, :], D["uwb"][m], f"d_fu{slot}")
                gps = PS_A[:, slot * 1024: slot * 1024 + 512]
                ups = PS_A[:, slot * 1024 + 512: slot * 1024 + 1024]
                pb.wait("tensor", tgw)
                pb.wait("tensor", txl)
                pb.wait("tensor", tok_sg[slot])
                specs = [(gps, GWB[slot][:, kc * 128:(kc + 1) * 128],
                          XLT[xslot][:, kc * 512:(kc + 1) * 512],
                          kc == 0, kc == KC - 1) for kc in range(KC)]
                tg = mm_loop(specs)
                tok_g[slot] = tg
                pb.wait("scalar", tg)
                tsg = pb.op("scalar", lambda e, o=SG[slot][:, :], i=gps,
                            s1=GS[:, m:m + 1]:
                            e.activation(o, i, AF.Silu, scale=s1), inc="act")
                tok_sg[slot] = tsg
                pb.wait("tensor", tuw)
                pb.wait("tensor", tok_hd[slot])
                specs = [(ups, UWB[slot][:, kc * 128:(kc + 1) * 128],
                          XLT[xslot][:, kc * 512:(kc + 1) * 512],
                          kc == 0, kc == KC - 1) for kc in range(KC)]
                tu = mm_loop(specs)
                tok_u[slot] = tu
                pb.wait("vector", tu)
                pb.wait("vector", tsg)
                thd = pb.op("vector", lambda e, o=HDN[:, m * 512:(m + 1) * 512],
                            i=ups, s1=US[:, m:m + 1], b=SG[slot][:, :]:
                            e.scalar_tensor_tensor(o, i, s1, b, op0=ALU.mult,
                                                   op1=ALU.mult), inc="dve")
                tok_hd[slot] = thd
                prep_step(1)
            tok_xlt_use[xslot] = pb.tok("pe")
            tok_hd_all = pb.tok("dve")
            pe_sq = None
            sq_tok = None
            for f in range(KC):
                slot = f % 2
                pb.wait("scalar", tok_d[slot])
                tdw = pb.dma("scalar", DWB[slot][:, :], D["dwb"][f], f"d_fd{slot}")
                pvw = PS_A[:, slot * 512:(slot + 1) * 512]
                pb.wait("tensor", tdw)
                pb.wait("tensor", tok_hd_all)
                pb.wait("tensor", tok_ev[slot])
                specs = [(pvw, DWB[slot][:, m * 128:(m + 1) * 128],
                          HDN[:, m * 512:(m + 1) * 512], m == 0, m == 43)
                         for m in range(44)]
                td = mm_loop(specs)
                tok_d[slot] = td
                pb.wait("sync", tok_ev[slot])
                txp = pb.dma("sync", XPV[slot][:, :], D[xprev][f, :, t0:t0 + 512],
                             f"d_fxp{slot}")
                pb.wait("vector", td)
                pb.wait("vector", txp)
                pb.wait("vector", tok_store[slot])
                pb.wait("vector", tok_stats[slot])
                txn = pb.op("vector", lambda e, o=XNW[slot][:, :], i=pvw,
                            s1=DS[:, f:f + 1], b=XPV[slot][:, :]:
                            e.scalar_tensor_tensor(o, i, s1, b, op0=ALU.mult,
                                                   op1=ALU.add), inc="dve")
                tok_ev[slot] = txn
                pb.wait("sync", txn)
                tst = pb.dma("sync", D[xnext][f, :, t0:t0 + 512], XNW[slot][:, :],
                             f"d_fxn{slot}")
                tok_store[slot] = tst
                if xlnout is not None:
                    pb.wait("tensor", txn)
                    pb.op("tensor", lambda e, o=PS_B[0:1, 0:512], l=ONESM[:, :],
                          r=XNW[slot][:, :], st=(f == 0), sp=(f == KC - 1):
                          e.matmul(o, l, r, start=st, stop=sp, skip_group_check=True))
                    pb.wait("scalar", txn)
                    pb.wait("scalar", sq_tok)
                    ta = pb.op("scalar", lambda e, o=SQT[:, :], i=XNW[slot][:, :]:
                               e.activation(o, i, AF.Square), inc="act")
                    pb.wait("tensor", ta)
                    pe_sq = pb.op("tensor", lambda e, o=PS_B[0:1, 512:1024],
                                  l=ONESM[:, :], r=SQT[:, :],
                                  st=(f == 0), sp=(f == KC - 1):
                                  e.matmul(o, l, r, start=st, stop=sp,
                                           skip_group_check=True), inc="pe")
                    sq_tok = pe_sq
                    tok_stats[slot] = pe_sq
            if xlnout is not None:
                tfin = ln_finalize(PS_B[0:1, 0:512], PS_B[0:1, 512:1024],
                                   MBC, RBC, MRT, 512, pe_sq, 1024)
                pb.waitc("sync", "d_fxn0")
                pb.waitc("sync", "d_fxn1")
                for f in range(KC):
                    slot = f % 2
                    pb.wait("sync", tok_xp[slot])
                    tl = pb.dma("sync", XPV[slot][:, :],
                                D[xnext][f, :, t0:t0 + 512], f"d_fx2{slot}")
                    pb.wait("vector", tl)
                    pb.wait("vector", tfin)
                    pb.wait("vector", tok_xl)
                    pb.op("vector", lambda e, o=T1[:, :],
                          a=XPV[slot][:, :], b=RBC[:, :]:
                          e.tensor_tensor(o, a, b, ALU.mult), inc="dve")
                    t2 = pb.op("vector", lambda e, o=XLO[:, :], a=T1[:, :],
                               b=MRT[:, :]:
                               e.tensor_tensor(o, a, b, ALU.subtract), inc="dve")
                    tok_xp[slot] = t2
                    pb.wait("sync", t2)
                    tok_xl = pb.dma("sync", D[xlnout][f, :, t0:t0 + 512],
                                    XLO[:, :], "d_fxl")
            tok_sg = [pb.tok("dve"), pb.tok("dve")]
            tok_hd = [pb.tok("dve"), pb.tok("dve")]
            tok_ev = [pb.tok("dve"), pb.tok("dve")]
            if xlnout is not None:
                tok_sg = [pb.tok("act"), pb.tok("act")]

    def attn_b_ccq(sc, pfx, s, CQ, CC, xln_src):
        scl = Scope()
        XLT = scl.sb(f"{pfx}_xlt", (128, KC * 512), BF16)
        WB = [scl.sb(f"{pfx}_wb{i}", (128, H), BF16) for i in range(2)]
        hbm_fence()
        tok_w = [None, None]
        tok_xlt = None
        tok_ev = [None, None]
        for tblk in range(4):
            g0 = s * S + tblk * 512
            pb.wait("sync", tok_xlt)
            pb.wait("sync", pb.tok("pe"))
            for kc in range(KC):
                txl = pb.dma("sync", XLT[:, kc * 512:(kc + 1) * 512],
                             D[xln_src][kc, :, g0:g0 + 512], "d_cbx")
            tok_xlt = txl
            for mb in range(16):
                is_c = mb >= 12
                wsrc = D["wkd"][mb - 12] if is_c else D["wqd"][mb]
                slot = mb % 2
                pb.wait("gpsimd", tok_w[slot])
                tw = pb.dma("gpsimd", WB[slot][:, :], wsrc, f"d_cb{slot}")
                prep_step(1)
                pb.wait("tensor", tw)
                pb.wait("tensor", txl)
                pb.wait("tensor", tok_ev[slot])
                specs = [(PS_A[:, slot * 512:(slot + 1) * 512],
                          WB[slot][:, kc * 128:(kc + 1) * 128],
                          XLT[:, kc * 512:(kc + 1) * 512], kc == 0, kc == KC - 1)
                         for kc in range(KC)]
                tq = mm_loop(specs)
                tok_w[slot] = tq
                if is_c:
                    dst = CC[:, (mb - 12) * S + tblk * 512: (mb - 12) * S + tblk * 512 + 512]
                else:
                    dst = CQ[:, mb * S + tblk * 512: mb * S + tblk * 512 + 512]
                pb.wait("scalar", tq)
                tev = pb.op("scalar", lambda e, o=dst,
                            i=PS_A[:, slot * 512:(slot + 1) * 512]:
                            e.activation(o, i, AF.Copy), inc="act")
                tok_ev[slot] = tev
        pb.waitc("tensor", "act")
        scl.close()

    pb.barrier()

    sc1 = Scope()
    XLN = sc1.sb("a_xln", (128, KC * S), BF16)
    bufs_a = dict(
        QF=sc1.sb("a_qf", (128, S), F32),
        QT=sc1.sb("a_qt", (128, S), BF16),
        KT=sc1.sb("a_kt", (128, S), BF16),
        VSB=sc1.sb("a_vsb", (128, KC * 256), BF16),
        FBC=sc1.sb("a_fbc", (128, S), F32),
        AO=sc1.sb("a_ao", (128, S), BF16),
        WQ=[sc1.sb(f"a_wq{i}", (128, H), BF16) for i in range(2)],
        WK=[sc1.sb(f"a_wk{i}", (128, H), BF16) for i in range(2)],
        WV=sc1.sb("a_wv", (128, KC * 256), BF16),
        BVBC=sc1.sb("a_bvbc", (128, H), F32),
        TCQ=sc1.sb("a_tcq", (128, S), F32),
        TSQ=sc1.sb("a_tsq", (128, S), F32),
        TCK=sc1.sb("a_tck", (128, S), F32),
        TSK=sc1.sb("a_tsk", (128, S), F32),
    )
    bufs_a["TRASH"] = bufs_a["QF"]
    for dst, srcn in [(bufs_a["TCQ"], "tbl_cq"), (bufs_a["TSQ"], "tbl_sq"),
                      (bufs_a["TCK"], "tbl_ck"), (bufs_a["TSK"], "tbl_sk")]:
        pb.dma("sync", dst[:, :], D[srcn][:, :], "d_const")
    tok_const = pb.tok("d_const")
    for e in ("tensor", "vector", "scalar"):
        pb.wait(e, tok_const)
    for n in range(4):
        pb.wait("tensor", tok_const)
        tb_ = pb.op("tensor", lambda e, o=PS_B[:, n * 512:(n + 1) * 512],
                    l=ONESK1[:, :], r=BVROW[0:1, n * 512:(n + 1) * 512]:
                    e.matmul(o, l, r, start=True, stop=True, skip_group_check=True),
                    inc="pe")
        pb.wait("scalar", tb_)
        pb.op("scalar", lambda e, o=bufs_a["BVBC"][:, n * 512:(n + 1) * 512],
              i=PS_B[:, n * 512:(n + 1) * 512]:
              e.activation(o, i, AF.Copy), inc="act")
    pb.waitc("tensor", "act")

    for s in range(NS):
        ln_prep(sc1, f"lp{s}", s, XLN, bufs_a["QF"], bufs_a["FBC"])
        pb.barrier()
        if DEBUG and s == 0:
            for kc in range(KC):
                pb.dma("sync", D["dbg_xln"][kc], XLN[:, kc * S:(kc + 1) * S], "d_dbg")
            pb.barrier()
        attn_heads("a", s, XLN, bufs_a)
        pb.barrier()
    if DEBUG:
        for hh in range(NH):
            pb.dma("sync", D["dbg_ao"][hh], D["ao"][0, hh], "d_dbg")
        pb.barrier()
    sc1.close()

    sc1d = Scope()
    for s in range(NS):
        dense_epilogue(sc1d, f"d1{s}", s, "wd", BD, "x0",
                       "xA" if SUBLAYERS > 1 else "out",
                       "lnA" if SUBLAYERS > 1 else None)
        pb.barrier()
    sc1d.close()

    if SUBLAYERS > 1:
        sc2 = Scope()
        prep_finish()
        pb.barrier()
        ffn(sc2, "f1", "lnA", "xA", "xB" if SUBLAYERS > 2 else "out",
            "lnB" if SUBLAYERS > 2 else None)
        pb.barrier()
        sc2.close()

    if SUBLAYERS > 2:
        sc3 = Scope()
        CQ = sc3.sb("b_cq", (128, 12 * S), BF16)
        CC = sc3.sb("b_cc", (128, 4 * S), BF16)
        bufs_b = dict(
            QT=sc3.sb("b_qt", (128, S), BF16),
            KT=sc3.sb("b_kt", (128, S), BF16),
            VSB=sc3.sb("b_vsb", (128, KC * 256), BF16),
            FBC=sc3.sb("b_fbc", (128, S), F32),
            RAMP=sc3.sb("b_ramp", (128, S), F32),
            AO=sc3.sb("b_ao", (128, S), BF16),
            WQ=[sc3.sb(f"b_wq{i}", (128, RQ), BF16) for i in range(2)],
            WK=[sc3.sb(f"b_wk{i}", (128, RKV), BF16) for i in range(2)],
            WV=sc3.sb("b_wv", (128, 4 * 256), BF16),
            RH=sc3.sb("b_rh", (128, S), F32),
            ALB=sc3.sb("b_alb", (128, 512), F32),
            DCONST=sc3.sb("b_dconst", (128, 512), F32),
            CQ=CQ, CC=CC,
        )
        bufs_b["TRASH"] = bufs_b["FBC"]
        pb.dma("sync", bufs_b["RAMP"][:, :], D["ramp"][:, :], "d_const")
        tok_r = pb.dma("sync", bufs_b["DCONST"][:, :], D["dconst"][:, :], "d_const")
        pb.wait("vector", tok_r)
        for s in range(NS):
            attn_b_ccq(sc3, f"cb{s}", s, CQ, CC, "lnB")
            pb.barrier()
            attn_heads("b", s, None, bufs_b)
            pb.barrier()
        sc3.close()
        sc3d = Scope()
        for s in range(NS):
            dense_epilogue(sc3d, f"d3{s}", s, "wo", None, "xB",
                           "xA" if SUBLAYERS > 3 else "out",
                           "lnA" if SUBLAYERS > 3 else None)
            pb.barrier()
        sc3d.close()

    if SUBLAYERS > 3:
        sc4 = Scope()
        ffn(sc4, "f2", "lnA", "xA", "out", None)
        pb.barrier()
        sc4.close()

    for sem in sorted(pb.cnt):
        if sem.startswith("d_") and pb.cnt[sem] > 0:
            pb.wait("sync", (sem, pb.cnt[sem]))

    with ExitStack() as st:
        sems = {}
        for name in pb.cnt:
            assert pb.cnt[name] < 60000, (name, pb.cnt[name])
            sems[name] = st.enter_context(nc.semaphore(name))
        block = st.enter_context(nc.Block())
        for eng in ENGS:
            def make(items):
                def f(e):
                    last = {}
                    for it in items:
                        if it[0] == "wait":
                            _, s_, v = it
                            if last.get(s_, -1) >= v:
                                continue
                            e.wait_ge(sems[s_], v)
                            last[s_] = v
                        else:
                            _, fn, inc, n = it
                            ins = fn(e)
                            if inc is not None:
                                ins.then_inc(sems[inc], n)
                return f
            getattr(block, eng)(make(pb.q[eng]))
    return nc


def kernel(**inputs):
    cache = {}
    in_maps = [_host_inputs(inputs, c, cache) for c in range(8)]
    nc = build_program()
    res = run_bass_kernel_spmd(nc, in_maps, core_ids=list(range(8)))
    out = np.empty((16, S, H), np.float32)
    for c in range(8):
        o = np.asarray(res.results[c]["out"])
        o = o.reshape(KC, 128, NS, S).transpose(2, 3, 0, 1).reshape(NS, S, H)
        out[2 * c: 2 * c + 2] = o
    return out


# version 03
# speedup vs baseline: 1.0474x; 1.0153x over previous
import math
from contextlib import ExitStack

import numpy as np

import concourse.bass as bass
import concourse.mybir as mybir
from concourse.bass_utils import run_bass_kernel_spmd

F32 = mybir.dt.float32
BF16 = mybir.dt.bfloat16
AF = mybir.ActivationFunctionType
ALU = mybir.AluOpType
AX = mybir.AxisListType

H = 2048
NH = 16
HD = 128
S = 2048
NS = 2
T = NS * S
INTER = 5632
RQ = 1536
RKV = 512
KC = H // 128
SCALE = 1.0 / math.sqrt(HD)
ROT = 32
EPS = 1e-5
NEG = -1.0e30
_BASE = 2.0 ** (-2.0 ** (-(math.log2(NH) - 3)))
SLOPES = [_BASE ** i for i in range(1, NH + 1)]

SUBLAYERS = 4
DEBUG = False
HEADS = 16
NO_H1_LOADS = False
H1_STAGE = 9
NO_H1_PREP = False

ENGS = ("tensor", "vector", "scalar", "gpsimd", "sync")


class PB:
    def __init__(self):
        self.q = {e: [] for e in ENGS}
        self.cnt = {}

    def _sem(self, s):
        if s not in self.cnt:
            self.cnt[s] = 0

    def op(self, eng, fn, inc=None, n=1):
        if inc is not None:
            self._sem(inc)
            self.cnt[inc] += n
            self.q[eng].append(("op", fn, inc, n))
            return (inc, self.cnt[inc])
        self.q[eng].append(("op", fn, None, 0))
        return None

    def wait(self, eng, token):
        if token is None:
            return
        s, v = token
        if v > 0:
            self.q[eng].append(("wait", s, v))

    def tok(self, sem):
        self._sem(sem)
        return (sem, self.cnt[sem])

    def waitc(self, eng, sem):
        self.wait(eng, self.tok(sem))

    def barrier(self):
        toks = {}
        for e in ENGS:
            toks[e] = self.op(e, lambda en: en.nop(), inc="bar_" + e)
        for e in ENGS:
            for e2 in ENGS:
                if e2 != e:
                    self.wait(e, toks[e2])

    def dma(self, eng, dst, src, sem):
        return self.op(eng, lambda e, d=dst, s=src: e.dma_start(d, s), inc=sem, n=16)


def _pack_lhsT(W):
    Kt, Mt = W.shape
    nk, nm = Kt // 128, Mt // 128
    w = W.reshape(nk, 128, nm, 128)
    return np.ascontiguousarray(w.transpose(2, 1, 0, 3).reshape(nm, 128, nk * 128))


def _host_inputs(inputs, core, cache={}):
    f32 = np.float32
    if not cache:
        W_qkv = np.asarray(inputs["L0_attn_a_W_qkv"], f32)
        b_qkv = np.asarray(inputs["L0_attn_a_b_qkv"], f32)
        W_d = np.asarray(inputs["L0_attn_a_W_dense"], f32)
        b_d = np.asarray(inputs["L0_attn_a_b_dense"], f32)
        qcols = np.concatenate([np.arange(h * 384, h * 384 + 128) for h in range(NH)])
        kcols = qcols + 128
        vcols = qcols + 256
        c = cache
        c["wq"] = _pack_lhsT(W_qkv[:, qcols])
        c["wk"] = _pack_lhsT(W_qkv[:, kcols])
        c["wv"] = np.ascontiguousarray(W_qkv[:, vcols].reshape(KC, 128, H))
        c["bq"] = np.ascontiguousarray(b_qkv[qcols].reshape(NH, 128).T)
        c["bk"] = np.ascontiguousarray(b_qkv[kcols].reshape(NH, 128).T)
        c["bv"] = np.ascontiguousarray(b_qkv[vcols].reshape(1, H))
        c["wd"] = _pack_lhsT(W_d)
        c["bd"] = np.ascontiguousarray(b_d.reshape(NH, 128).T)
        Wqd = np.asarray(inputs["L0_attn_b_Wq_down"], f32)
        Wqu = np.asarray(inputs["L0_attn_b_Wq_up"], f32)
        Wkd = np.asarray(inputs["L0_attn_b_Wkv_down"], f32)
        kup = np.asarray(inputs["L0_attn_b_k_up"], f32)
        vup = np.asarray(inputs["L0_attn_b_v_up"], f32)
        Wo = np.asarray(inputs["L0_attn_b_Wo"], f32)
        c["wqd"] = _pack_lhsT(np.ascontiguousarray(Wqd.T))
        c["wqu"] = _pack_lhsT(np.ascontiguousarray(Wqu.T))
        c["wkd"] = _pack_lhsT(np.ascontiguousarray(Wkd.T))
        c["kup"] = np.ascontiguousarray(
            np.stack([_pack_lhsT(kup[h])[0] for h in range(NH)]))
        c["vup"] = np.ascontiguousarray(
            vup.transpose(1, 0, 2).reshape(4, 128, NH * HD))
        c["wo"] = _pack_lhsT(np.ascontiguousarray(Wo.T))
        gw = np.asarray(inputs["L0_ffn_gate_w"], f32)
        uw = np.asarray(inputs["L0_ffn_up_w"], f32)
        dwn = np.asarray(inputs["L0_ffn_down_w"], f32)
        c["gw"] = _pack_lhsT(np.ascontiguousarray(gw.T))
        c["uw"] = _pack_lhsT(np.ascontiguousarray(uw.T))
        c["dw"] = _pack_lhsT(np.ascontiguousarray(dwn.T))
        c["gs"] = np.ascontiguousarray(
            np.asarray(inputs["L0_ffn_gate_s"], f32).reshape(44, 128).T)
        c["us"] = np.ascontiguousarray(
            np.asarray(inputs["L0_ffn_up_s"], f32).reshape(44, 128).T)
        c["ds"] = np.ascontiguousarray(
            np.asarray(inputs["L0_ffn_down_s"], f32).reshape(16, 128).T)
        c["ident"] = np.eye(128, dtype=f32)
        c["epsc"] = np.full((128, 1), EPS, f32)
        c["onesk1"] = np.ones((1, 128), f32)
        c["onesm"] = np.full((128, 1), 1.0 / H, f32)
        iu = np.arange(128)
        c["upmask"] = np.where(iu[None, :] > iu[:, None], np.float32(NEG), 0.0).astype(f32)
        c["lowmask"] = np.where(iu[:, None] > iu[None, :], np.float32(NEG), 0.0).astype(f32)
        c["ramp"] = np.ascontiguousarray(
            np.broadcast_to(np.arange(S, dtype=f32), (128, S)))
        c["qpos"] = (np.arange(16)[None, :] * 128 + iu[:, None]).astype(f32)
        c["dconst"] = (iu[:, None] - np.arange(512)[None, :]).astype(f32)
        c["dn16"] = np.broadcast_to((-128.0 * np.arange(16, dtype=f32))[None, :],
                                    (128, 16)).copy()
        R = np.zeros((128, 128), f32)
        for i in range(16):
            R[i, i + 16] = -1.0
            R[i + 16, i] = 1.0
        c["rt"] = np.ascontiguousarray(R.T)
        inv = 1.0 / (10000.0 ** (np.arange(0, ROT, 2, dtype=f32) / ROT))
        tv = np.arange(S, dtype=f32)
        fr = np.outer(tv, inv)
        emb = np.concatenate([fr, fr], axis=1).T
        cos = np.cos(emb).astype(f32)
        sin = np.sin(emb).astype(f32)
        cq = np.full((128, S), SCALE, f32); cq[:ROT] = cos * SCALE
        sq = np.zeros((128, S), f32);       sq[:ROT] = sin * SCALE
        ck = np.ones((128, S), f32);        ck[:ROT] = cos
        sk = np.zeros((128, S), f32);       sk[:ROT] = sin
        c["tbl_cq"], c["tbl_sq"], c["tbl_ck"], c["tbl_sk"] = cq, sq, ck, sk
    m = dict(cache)
    hs = np.asarray(inputs["L0_attn_a_hidden_states"], np.float32)
    hc = hs[2 * core: 2 * core + 2]
    m["x0"] = np.ascontiguousarray(hc.transpose(2, 0, 1).reshape(KC, 128, T))
    return m


def build_program():
    nc = bass.Bass("TRN2", target_bir_lowering=False)
    D = {}

    def din(name, shape, dt=F32):
        D[name] = nc.dram_tensor(name, list(shape), dt, kind="ExternalInput").ap()

    din("x0", (KC, 128, T))
    din("wq", (NH, 128, H)); din("wk", (NH, 128, H)); din("wv", (KC, 128, H))
    din("bq", (128, NH)); din("bk", (128, NH)); din("bv", (1, H))
    din("wd", (KC, 128, H)); din("bd", (128, NH))
    din("wqd", (12, 128, H)); din("wqu", (NH, 128, RQ)); din("wkd", (4, 128, H))
    din("kup", (NH, 128, RKV)); din("vup", (4, 128, H)); din("wo", (KC, 128, H))
    din("gw", (44, 128, H)); din("uw", (44, 128, H)); din("dw", (KC, 128, INTER))
    din("gs", (128, 44)); din("us", (128, 44)); din("ds", (128, KC))
    din("ident", (128, 128)); din("onesk1", (1, 128)); din("onesm", (128, 1))
    din("epsc", (128, 1))
    din("upmask", (128, 128)); din("lowmask", (128, 128))
    din("ramp", (128, S)); din("qpos", (128, 16)); din("rt", (128, 128))
    din("dconst", (128, 512)); din("dn16", (128, 16))
    for n in ("tbl_cq", "tbl_sq", "tbl_ck", "tbl_sk"):
        din(n, (128, S))

    D["out"] = nc.dram_tensor("out", [KC, 128, T], F32, kind="ExternalOutput").ap()
    if DEBUG:
        D["dbg_xln"] = nc.dram_tensor("dbg_xln", [KC, 128, S], BF16, kind="ExternalOutput").ap()
        D["dbg_qt"] = nc.dram_tensor("dbg_qt", [128, S], BF16, kind="ExternalOutput").ap()
        D["dbg_kt"] = nc.dram_tensor("dbg_kt", [128, S], BF16, kind="ExternalOutput").ap()
        D["dbg_ao"] = nc.dram_tensor("dbg_ao", [NH, 128, S], BF16, kind="ExternalOutput").ap()
        D["dbg_st"] = nc.dram_tensor("dbg_st", [4, 128, 16], F32, kind="ExternalOutput").ap()
        D["dbg_nmg"] = nc.dram_tensor("dbg_nmg", [128, 1], F32, kind="ExternalOutput").ap()
        D["dbg_fbc"] = nc.dram_tensor("dbg_fbc", [128, S], F32, kind="ExternalOutput").ap()
        D["dbg_ao0"] = nc.dram_tensor("dbg_ao0", [128, S], BF16, kind="ExternalOutput").ap()
    for n, dt in (("xA", F32), ("xB", F32), ("lnA", BF16), ("lnB", BF16)):
        D[n] = nc.dram_tensor(n, [KC, 128, T], dt, kind="Internal").ap()
    D["ao"] = nc.dram_tensor("ao", [NS, NH, 128, S], BF16, kind="Internal").ap()
    D["gwb"] = nc.dram_tensor("gwb", [44, 128, H], BF16, kind="Internal").ap()
    D["uwb"] = nc.dram_tensor("uwb", [44, 128, H], BF16, kind="Internal").ap()
    D["dwb"] = nc.dram_tensor("dwb", [KC, 128, INTER], BF16, kind="Internal").ap()

    pb = PB()

    def sb(name, shape, dt):
        return nc.alloc_sbuf_tensor(name, list(shape), dt)

    IDENT = sb("IDENT", (128, 128), F32)
    ONESK1 = sb("ONESK1", (1, 128), F32)
    ONESM = sb("ONESM", (128, 1), F32)
    EPSC = sb("EPSC", (128, 1), F32)
    UPM = sb("UPM", (128, 128), F32)
    LOWM = sb("LOWM", (128, 128), F32)
    QPOS = sb("QPOS", (128, 16), F32)
    DN16 = sb("DN16", (128, 16), F32)
    RT = sb("RT", (128, 128), F32)
    BQ = sb("BQ", (128, 16), F32)
    BK = sb("BK", (128, 16), F32)
    BD = sb("BD", (128, 16), F32)
    BVROW = sb("BVROW", (1, H), F32)
    GS = sb("GS", (128, 44), F32)
    US = sb("US", (128, 44), F32)
    DS = sb("DS", (128, 16), F32)
    NEGM = sb("NEGM", (128, 16), F32)
    SACC = sb("SACC", (128, 16), F32)
    LNSB = sb("LNSB", (128, 16), F32)
    GTB = sb("GTB", (128, 16), F32)
    FQT = sb("FQT", (128, 16), F32)
    KB = sb("KB", (128, 16), F32)
    QPS = sb("QPS", (128, 16), F32)
    NM1 = sb("NM1", (128, 1), F32)
    NMG = sb("NMG", (128, 1), F32)
    TROW = sb("TROW", (1, 128), F32)
    MGR = sb("MGR", (1, 1), F32)
    FROW = sb("FROW", (1, S), F32)
    SROW = sb("SROW", (1, 512), F32)
    SROW2 = sb("SROW2", (1, 512), F32)
    ATN = tuple(sb(f"ATN{i}", (128, 512), BF16) for i in range(4))
    PREP = tuple(sb(f"PREP{i}", (128, H), BF16) for i in range(2))

    PS_A = nc.alloc_psum_tensor("PS_A", [128, 2048], F32)
    PS_B = nc.alloc_psum_tensor("PS_B", [128, 2048], F32)

    class Scope:
        def __init__(self):
            self.base = nc.sbuf_base
        def sb(self, name, shape, dt):
            return nc.alloc_sbuf_tensor(name, list(shape), dt)
        def close(self):
            nc.sbuf_base = self.base

    for dst, src in [(IDENT, "ident"), (ONESK1, "onesk1"), (ONESM, "onesm"),
                     (UPM, "upmask"), (LOWM, "lowmask"), (EPSC, "epsc"),
                     (QPOS, "qpos"), (RT, "rt"), (DN16, "dn16"),
                     (BQ, "bq"), (BK, "bk"),
                     (BD, "bd"), (BVROW, "bv"), (GS, "gs"), (US, "us"), (DS, "ds")]:
        pb.dma("sync", dst[:, :], D[src][:, :], "d_const")
    tok_const = pb.tok("d_const")
    for e in ("tensor", "vector", "scalar"):
        pb.wait(e, tok_const)

    prep_jobs = []
    for m in range(44):
        prep_jobs.append((D["gw"][m], D["gwb"][m], H))
        prep_jobs.append((D["uw"][m], D["uwb"][m], H))
    for f in range(KC):
        for c0 in range(0, INTER, 2048):
            w = min(2048, INTER - c0)
            prep_jobs.append((D["dw"][f, :, c0:c0 + w], D["dwb"][f, :, c0:c0 + w], w))
    prep_state = {"i": 0, "tok": [None, None]}

    def prep_step(k=2):
        for _ in range(k):
            i = prep_state["i"]
            if i >= len(prep_jobs):
                return
            slot = i % 2
            src, dst, w = prep_jobs[i]
            pb.wait("gpsimd", prep_state["tok"][slot])
            tl = pb.dma("gpsimd", PREP[slot][:, 0:w], src, f"d_prep{slot}")
            pb.wait("sync", tl)
            ts = pb.dma("sync", dst, PREP[slot][:, 0:w], f"d_preps{slot}")
            prep_state["tok"][slot] = ts
            prep_state["i"] = i + 1

    def prep_finish():
        while prep_state["i"] < len(prep_jobs):
            prep_step(8)

    def mm_loop(eng_specs, final_inc="pe"):
        tok = None
        nsp = len(eng_specs)
        for i, (o, l, r, st, sp) in enumerate(eng_specs):
            inc = final_inc if i == nsp - 1 else None
            tok = pb.op("tensor", lambda e, o=o, l=l, r=r, st=st, sp=sp:
                        e.matmul(o, l, r, start=st, stop=sp, skip_group_check=True),
                        inc=inc)
        return tok

    STORE_SEMS = ("d_dxn0", "d_dxn1", "d_dxl", "d_fxn0", "d_fxn1", "d_fxl",
                  "d_ao", "d_fx20", "d_fx21", "d_dx20", "d_dx21")

    def hbm_fence(eng="sync"):
        for s_ in STORE_SEMS:
            pb.waitc(eng, s_)

    def ln_finalize(ps_sum, ps_sq, Mbc, Rbc, MRt, width, tok_pe_stats, psA_off):
        pb.wait("scalar", tok_pe_stats)
        pb.waitc("scalar", "pe")
        pb.op("scalar", lambda e, o=SROW[0:1, 0:width], i=ps_sum:
              e.activation(o, i, AF.Copy), inc="act")
        t2 = pb.op("scalar", lambda e, o=SROW2[0:1, 0:width], i=ps_sq:
                   e.activation(o, i, AF.Copy), inc="act")
        pb.wait("tensor", t2)
        pb.waitc("tensor", "dve")
        bm = pb.op("tensor", lambda e, o=PS_A[:, psA_off:psA_off + width],
                   l=ONESK1[:, :], r=SROW[0:1, 0:width]:
                   e.matmul(o, l, r, start=True, stop=True, skip_group_check=True),
                   inc="pe")
        bq_ = pb.op("tensor", lambda e, o=PS_A[:, psA_off + width:psA_off + 2 * width],
                    l=ONESK1[:, :], r=SROW2[0:1, 0:width]:
                    e.matmul(o, l, r, start=True, stop=True, skip_group_check=True),
                    inc="pe")
        pb.wait("scalar", bq_)
        pb.op("scalar", lambda e, o=Mbc[:, 0:width], i=PS_A[:, psA_off:psA_off + width]:
              e.activation(o, i, AF.Copy), inc="act")
        tsq = pb.op("scalar", lambda e, o=MRt[:, 0:width],
                    i=PS_A[:, psA_off:psA_off + width]:
                    e.activation(o, i, AF.Square), inc="act")
        pb.wait("vector", tsq)
        tv = pb.op("vector", lambda e, o=Rbc[:, 0:width],
                   a=PS_A[:, psA_off + width:psA_off + 2 * width], b=MRt[:, 0:width]:
                   e.tensor_tensor(o, a, b, ALU.subtract), inc="dve")
        pb.wait("scalar", tv)
        pb.op("scalar", lambda e, o=Rbc[:, 0:width], i=Rbc[:, 0:width]:
              e.activation(o, i, AF.Ln, bias=EPSC[:, :]), inc="act")
        te = pb.op("scalar", lambda e, o=Rbc[:, 0:width], i=Rbc[:, 0:width]:
                   e.activation(o, i, AF.Exp, scale=-0.5), inc="act")
        pb.wait("vector", te)
        tmr = pb.op("vector", lambda e, o=MRt[:, 0:width], a=Mbc[:, 0:width],
                    b=Rbc[:, 0:width]:
                    e.tensor_tensor(o, a, b, ALU.mult), inc="dve")
        return tmr, pb.tok("act")

    def ln_prep(sc, pfx, s, XLN, scr, fbc):
        XPV = [scr[:, 0:512], scr[:, 512:1024]]
        SQT = scr[:, 1024:1536]
        MBC = fbc[:, 0:512]
        RBC = fbc[:, 512:1024]
        MRT = fbc[:, 1024:1536]
        T1 = fbc[:, 1536:2048]
        xv_tok = [None, None]
        for tau in range(4):
            g0 = s * S + tau * 512
            sq_tok = None
            pe_sq = None
            for kc in range(KC):
                slot = kc % 2
                pb.wait("sync", xv_tok[slot])
                tl = pb.dma("sync", XPV[slot], D["x0"][kc, :, g0:g0 + 512],
                            f"d_lpx{slot}")
                pb.wait("tensor", tl)
                pb.op("tensor", lambda e, o=PS_B[0:1, 0:512], l=ONESM[:, :],
                      r=XPV[slot], st=(kc == 0), sp=(kc == KC - 1):
                      e.matmul(o, l, r, start=st, stop=sp, skip_group_check=True))
                pb.wait("scalar", tl)
                pb.wait("scalar", sq_tok)
                ta = pb.op("scalar", lambda e, o=SQT, i=XPV[slot]:
                           e.activation(o, i, AF.Square), inc="act")
                pb.wait("tensor", ta)
                pe_sq = pb.op("tensor", lambda e, o=PS_B[0:1, 512:1024], l=ONESM[:, :],
                              r=SQT, st=(kc == 0), sp=(kc == KC - 1):
                              e.matmul(o, l, r, start=st, stop=sp,
                                       skip_group_check=True), inc="pe")
                sq_tok = pe_sq
                xv_tok[slot] = pe_sq
            tfin, _tfa = ln_finalize(PS_B[0:1, 0:512], PS_B[0:1, 512:1024],
                                     MBC, RBC, MRT, 512, pe_sq, 0)
            for kc in range(KC):
                slot = kc % 2
                pb.wait("sync", xv_tok[slot])
                tl = pb.dma("sync", XPV[slot], D["x0"][kc, :, g0:g0 + 512],
                            f"d_lpx{slot}")
                pb.wait("vector", tl)
                pb.wait("vector", tfin)
                pb.op("vector", lambda e, o=T1, a=XPV[slot], b=RBC:
                      e.tensor_tensor(o, a, b, ALU.mult), inc="dve")
                t2 = pb.op("vector", lambda e,
                           o=XLN[:, kc * S + tau * 512: kc * S + tau * 512 + 512],
                           a=T1, b=MRT:
                           e.tensor_tensor(o, a, b, ALU.subtract), inc="dve")
                xv_tok[slot] = t2

    def attn_heads(lay, s, XLN, bufs):
        QT = bufs["QT"]; KT = bufs["KT"]; VSB = bufs["VSB"]
        TRASH = bufs["TRASH"]; FBC = bufs["FBC"]; AO = bufs["AO"]
        tok_ao_store = None
        tok_wq_free = [None, None]
        tok_wk_free = [None, None]
        tok_wv_free = None
        tok_exp_c1_last = None
        tok_fmult = [None, None, None, None]
        tok_attnv = {}
        tok_exp_c2 = [None, None, None, None]
        for h in range(HEADS):
            slot = h % 2
            if NO_H1_LOADS and h > 0:
                twq = twk = None
            elif lay == "a":
                pb.wait("gpsimd", tok_wq_free[slot])
                twq = pb.dma("gpsimd", bufs["WQ"][slot][:, :], D["wq"][h], f"d_wq{slot}")
                pb.wait("gpsimd", tok_wk_free[slot])
                twk = pb.dma("gpsimd", bufs["WK"][slot][:, :], D["wk"][h], f"d_wk{slot}")
            else:
                pb.wait("gpsimd", tok_wq_free[slot])
                twq = pb.dma("gpsimd", bufs["WQ"][slot][:, :], D["wqu"][h], f"d_wq{slot}")
                pb.wait("gpsimd", tok_wk_free[slot])
                twk = pb.dma("gpsimd", bufs["WK"][slot][:, :], D["kup"][h], f"d_wk{slot}")
            if h % 2 == 0:
                pb.wait("gpsimd", tok_wv_free)
                nkv = KC if lay == "a" else 4
                wsrc3 = D["wv"] if lay == "a" else D["vup"]
                for kc in range(nkv):
                    twv = pb.dma("gpsimd", bufs["WV"][:, kc * 256:(kc + 1) * 256],
                                 wsrc3[kc, :, h * 128:(h + 2) * 128], "d_wv")
            if not (NO_H1_PREP and h > 0):
                prep_step(3)

            if h % 2 == 0:
                tok_ve = [None, None]
                pb.waitc("tensor", "act")
                pb.waitc("tensor", "dve")
                for tm in range(KC):
                    pv = PS_A[:, (tm % 2) * 512:(tm % 2) * 512 + 256]
                    pb.wait("tensor", twv)
                    pb.wait("tensor", tok_ve[tm % 2])
                    specs = []
                    for kc in range(nkv):
                        if lay == "a":
                            l = XLN[:, kc * S + tm * 128: kc * S + tm * 128 + 128]
                        else:
                            l = bufs["CC"][:, kc * S + tm * 128: kc * S + tm * 128 + 128]
                        r = bufs["WV"][:, kc * 256:(kc + 1) * 256]
                        specs.append((pv, l, r, kc == 0, kc == nkv - 1))
                    tv = mm_loop(specs)
                    ov = VSB[:, tm * 256:(tm + 1) * 256]
                    if lay == "a":
                        pb.wait("vector", tv)
                        te = pb.op("vector", lambda e, o=ov, i=pv,
                                   b=bufs["BVBC"][:, h * 128:h * 128 + 256]:
                                   e.scalar_tensor_tensor(o, i, 1.0, b, op0=ALU.mult,
                                                          op1=ALU.add), inc="dve")
                    else:
                        pb.wait("scalar", tv)
                        te = pb.op("scalar", lambda e, o=ov, i=pv:
                                   e.activation(o, i, AF.Copy), inc="act")
                    tok_ve[tm % 2] = te
                tok_wv_free = tok_ve[1]
                tok_v_done = (tok_ve[0], tok_ve[1])

            if h > 0 and H1_STAGE < 1:
                continue
            nq_kc = KC if lay == "a" else 12
            qsrc = XLN if lay == "a" else bufs["CQ"]
            pb.wait("tensor", twq)
            pb.waitc("tensor", "act")
            pb.waitc("tensor", "dve")
            for t_ in tok_v_done:
                pb.wait("tensor", t_)
            specs = []
            for n in range(4):
                for kc in range(nq_kc):
                    specs.append((PS_A[:, n * 512:(n + 1) * 512],
                                  bufs["WQ"][slot][:, kc * 128:(kc + 1) * 128],
                                  qsrc[:, kc * S + n * 512: kc * S + (n + 1) * 512],
                                  kc == 0, kc == nq_kc - 1))
            tq = mm_loop(specs)
            tok_wq_free[slot] = tq
            if lay == "a":
                QF = bufs["QF"]
                pb.wait("scalar", tq)
                pb.waitc("scalar", "dve")
                tqf = pb.op("scalar", lambda e, o=QF[:, :], i=PS_A[:, :],
                            bb=BQ[:, h:h + 1]:
                            e.activation(o, i, AF.Identity, bias=bb),
                            inc="act")
                pb.wait("tensor", tqf)
                pb.waitc("tensor", "dve")
                specs = [(PS_B[:, n * 512:(n + 1) * 512], RT[:, :],
                          QF[:, n * 512:(n + 1) * 512], True, True) for n in range(4)]
                trq = mm_loop(specs)
                pb.wait("vector", trq)
                pb.op("vector", lambda e, o=QF[:, :], a=QF[:, :], b=bufs["TCQ"][:, :]:
                      e.tensor_tensor(o, a, b, ALU.mult), inc="dve")
                pb.op("vector", lambda e, o=PS_B[:, :], a=PS_B[:, :], b=bufs["TSQ"][:, :]:
                      e.tensor_tensor(o, a, b, ALU.mult), inc="dve")
                tqt = pb.op("vector", lambda e, o=QT[:, :], a=QF[:, :], b=PS_B[:, :]:
                            e.tensor_tensor(o, a, b, ALU.add), inc="dve")
            else:
                pb.wait("scalar", tq)
                tqt = pb.op("scalar", lambda e, o=QT[:, :], i=PS_A[:, :]:
                            e.activation(o, i, AF.Copy, scale=float(SCALE)), inc="act")

            if h > 0 and H1_STAGE < 2:
                continue
            nk_kc = KC if lay == "a" else 4
            ksrc = XLN if lay == "a" else bufs["CC"]
            pb.wait("tensor", twk)
            if lay == "a":
                pb.wait("tensor", tqf)
            else:
                pb.wait("tensor", tqt)
            specs = []
            for n in range(4):
                for kc in range(nk_kc):
                    specs.append((PS_A[:, n * 512:(n + 1) * 512],
                                  bufs["WK"][slot][:, kc * 128:(kc + 1) * 128],
                                  ksrc[:, kc * S + n * 512: kc * S + (n + 1) * 512],
                                  kc == 0, kc == nk_kc - 1))
            tk = mm_loop(specs)
            tok_wk_free[slot] = tk
            if lay == "a":
                pb.wait("scalar", tk)
                pb.wait("scalar", tqt)
                tkf = pb.op("scalar", lambda e, o=QF[:, :], i=PS_A[:, :],
                            bb=BK[:, h:h + 1]:
                            e.activation(o, i, AF.Identity, bias=bb),
                            inc="act")
                pb.wait("tensor", tkf)
                pb.wait("tensor", tqt)
                specs = [(PS_B[:, n * 512:(n + 1) * 512], RT[:, :],
                          QF[:, n * 512:(n + 1) * 512], True, True) for n in range(4)]
                trk = mm_loop(specs)
                pb.wait("vector", trk)
                pb.op("vector", lambda e, o=QF[:, :], a=QF[:, :], b=bufs["TCK"][:, :]:
                      e.tensor_tensor(o, a, b, ALU.mult), inc="dve")
                pb.op("vector", lambda e, o=PS_B[:, :], a=PS_B[:, :], b=bufs["TSK"][:, :]:
                      e.tensor_tensor(o, a, b, ALU.mult), inc="dve")
                tkt = pb.op("vector", lambda e, o=KT[:, :], a=QF[:, :], b=PS_B[:, :]:
                            e.tensor_tensor(o, a, b, ALU.add), inc="dve")
                tok_qkt = tkt
            else:
                pb.wait("scalar", tk)
                tkt = pb.op("scalar", lambda e, o=KT[:, :], i=PS_A[:, :]:
                            e.activation(o, i, AF.Copy), inc="act")
                tok_qkt = tkt

            if DEBUG and lay == "a" and s == 0 and h == 0:
                pb.wait("sync", tok_qkt)
                pb.dma("sync", D["dbg_qt"][:, :], QT[:, :], "d_dbg")
                pb.dma("sync", D["dbg_kt"][:, :], KT[:, :], "d_dbg")
            if lay == "b":
                trh = pb.op("vector", lambda e, o=bufs["RH"][:, :], i=bufs["RAMP"][:, :],
                            sl=float(SLOPES[h]):
                            e.tensor_scalar(o, i, sl, None, op0=ALU.mult), inc="dve")
                tqps = pb.op("vector", lambda e, o=QPS[:, :], i=QPOS[:, :],
                             sl=-float(SLOPES[h]):
                             e.tensor_scalar(o, i, sl, None, op0=ALU.mult), inc="dve")
                talb = pb.op("vector", lambda e, o=bufs["ALB"][:, :], i=bufs["DCONST"][:, :],
                             sl=float(SLOPES[h]):
                             e.tensor_scalar(o, i, sl, None, op0=ALU.mult), inc="dve")

            if h > 0 and H1_STAGE < 3:
                continue
            pb.waitc("tensor", "dve")
            pb.waitc("tensor", "act")
            if lay == "a":
                pb.wait("tensor", tok_qkt)
                pb.wait("tensor", tkf)
            else:
                pb.wait("tensor", tok_qkt)
            tok_exp_pp = [None, None]
            tok_exp_qi = None
            for qi in range(16):
                klen = (qi + 1) * 128
                base = PS_A if qi % 2 == 0 else PS_B
                pb.wait("tensor", tok_exp_pp[qi % 2])
                specs = []
                for n in range(0, klen, 512):
                    w = min(512, klen - n)
                    specs.append((base[:, n:n + w], QT[:, qi * 128:(qi + 1) * 128],
                                  KT[:, n:n + w], True, True))
                tsc = mm_loop(specs)
                pb.wait("vector", tsc)
                if lay == "b":
                    pb.wait("vector", trh)
                    pb.wait("vector", tqps)
                    pb.op("vector", lambda e, o=base[:, 0:klen], i=base[:, 0:klen],
                          s1=QPS[:, qi:qi + 1], b=bufs["RH"][:, 0:klen]:
                          e.scalar_tensor_tensor(o, i, s1, b, op0=ALU.add, op1=ALU.add),
                          inc="dve")
                pb.op("vector", lambda e, o=base[:, klen - 128:klen],
                      a=base[:, klen - 128:klen], b=UPM[:, :]:
                      e.tensor_tensor(o, a, b, ALU.add), inc="dve")
                tmx = pb.op("vector", lambda e, o=NEGM[:, qi:qi + 1], i=base[:, 0:klen]:
                            e.reduce_max(o, i, axis=AX.X, negate=True), inc="dve")
                pb.wait("scalar", tmx)
                tok_exp_qi = pb.op("scalar", lambda e, o=TRASH[:, 0:klen],
                                   i=base[:, 0:klen], b=NEGM[:, qi:qi + 1],
                                   a=SACC[:, qi:qi + 1]:
                                   e.activation(o, i, AF.Exp, bias=b, accum_out=a),
                                   inc="act")
                tok_exp_pp[qi % 2] = tok_exp_qi
            tok_exp_c1_last = tok_exp_qi

            if h > 0 and H1_STAGE < 4:
                continue
            pb.wait("scalar", tok_exp_c1_last)
            pb.op("scalar", lambda e, o=LNSB[:, :], i=SACC[:, :]:
                  e.activation(o, i, AF.Ln), inc="act")
            pb.wait("vector", tok_exp_c1_last)
            tnm = pb.op("vector", lambda e, o=NM1[:, :], i=NEGM[:, :]:
                        e.tensor_reduce(o, i, axis=AX.X, op=ALU.min), inc="dve")
            pb.wait("tensor", tnm)
            pb.waitc("tensor", "dve")
            pb.waitc("tensor", "act")
            ttr = pb.op("tensor", lambda e, o=PS_B[0:1, 0:128], l=NM1[:, :],
                        r=IDENT[:, :]:
                        e.matmul(o, l, r, start=True, stop=True,
                                 skip_group_check=True), inc="pe")
            pb.wait("scalar", ttr)
            ttrc = pb.op("scalar", lambda e, o=TROW[:, :], i=PS_B[0:1, 0:128]:
                         e.activation(o, i, AF.Copy), inc="act")
            pb.wait("vector", ttrc)
            tmg = pb.op("vector", lambda e, o=MGR[:, :], i=TROW[:, :]:
                        e.tensor_reduce(o, i, axis=AX.X, op=ALU.min), inc="dve")
            pb.wait("tensor", tmg)
            tbg = pb.op("tensor", lambda e, o=PS_B[:, 512:513], l=ONESK1[:, :],
                        r=MGR[:, :]:
                        e.matmul(o, l, r, start=True, stop=True,
                                 skip_group_check=True), inc="pe")
            pb.wait("scalar", tbg)
            tnmg = pb.op("scalar", lambda e, o=NMG[:, :], i=PS_B[:, 512:513]:
                         e.activation(o, i, AF.Copy), inc="act")
            pb.wait("vector", tnmg)
            tg = pb.op("vector", lambda e, o=GTB[:, :], i=NEGM[:, :], s1=NMG[:, :],
                       b=LNSB[:, :]:
                       e.scalar_tensor_tensor(o, i, s1, b, op0=ALU.subtract,
                                              op1=ALU.subtract), inc="dve")
            if lay == "b":
                tkb = pb.op("vector", lambda e, o=KB[:, :], i=DN16[:, :],
                            sl=float(SLOPES[h]), s2=NMG[:, :]:
                            e.tensor_scalar(o, i, sl, s2, op0=ALU.mult, op1=ALU.add),
                            inc="dve")
            pb.wait("scalar", tg)
            tf = pb.op("scalar", lambda e, o=FQT[:, :], i=GTB[:, :]:
                       e.activation(o, i, AF.Exp), inc="act")
            tfr = None
            trc_toks = [None, None, None, None]
            for qi in range(16):
                pb.wait("tensor", tf)
                pb.wait("tensor", trc_toks[qi % 4])
                ttr2 = pb.op("tensor", lambda e,
                             o=PS_B[0:1, (qi % 4) * 512:(qi % 4) * 512 + 128],
                             l=FQT[:, qi:qi + 1], r=IDENT[:, :]:
                             e.matmul(o, l, r, start=True, stop=True,
                                      skip_group_check=True), inc="pe")
                pb.wait("scalar", ttr2)
                tfr = pb.op("scalar", lambda e, o=FROW[0:1, qi * 128:(qi + 1) * 128],
                            i=PS_B[0:1, (qi % 4) * 512:(qi % 4) * 512 + 128]:
                            e.activation(o, i, AF.Copy), inc="act")
                trc_toks[qi % 4] = tfr
            tfb = None
            for n in range(4):
                pb.wait("tensor", tfr)
                tb_ = pb.op("tensor", lambda e, o=PS_B[:, n * 512:(n + 1) * 512],
                            l=ONESK1[:, :], r=FROW[0:1, n * 512:(n + 1) * 512]:
                            e.matmul(o, l, r, start=True, stop=True,
                                     skip_group_check=True), inc="pe")
                pb.wait("scalar", tb_)
                tfb = pb.op("scalar", lambda e, o=FBC[:, n * 512:(n + 1) * 512],
                            i=PS_B[:, n * 512:(n + 1) * 512]:
                            e.activation(o, i, AF.Copy), inc="act")

            if DEBUG and lay == "a" and s == 0 and h == 0:
                pb.barrier()
                pb.dma("sync", D["dbg_st"][0], NEGM[:, :], "d_dbg")
                pb.dma("sync", D["dbg_st"][1], SACC[:, :], "d_dbg")
                pb.dma("sync", D["dbg_st"][2], FQT[:, :], "d_dbg")
                pb.dma("sync", D["dbg_nmg"][:, :], NMG[:, :], "d_dbg")
                pb.dma("sync", D["dbg_fbc"][:, :], FBC[:, :], "d_dbg")
                pb.barrier()
            if h > 0 and H1_STAGE < 5:
                continue
            if lay == "b":
                pb.wait("scalar", tkb)
            pb.waitc("tensor", "act")
            for j in range(4):
                ov = PS_B[:, 1024 + (j % 2) * 512: 1024 + (j % 2) * 512 + 512]
                tattnv = None
                for kc in range(4 * j + 4):
                    jjs = max(0, kc - 4 * j)
                    w = 512 - jjs * 128
                    q0 = j * 512 + jjs * 128
                    par = kc % 4
                    pv = PS_A[:, par * 512 + jjs * 128: par * 512 + 512]
                    pb.wait("tensor", tok_exp_c2[par])
                    if kc == 0:
                        pb.wait("tensor", tfb)
                        pb.wait("vector", tfb)
                        pb.wait("tensor", tok_fmult[(j + 2) % 4])
                    tst = pb.op("tensor", lambda e, o=pv,
                                l=KT[:, kc * 128:(kc + 1) * 128],
                                r=QT[:, q0: (j + 1) * 512]:
                                e.matmul(o, l, r, start=True, stop=True,
                                         skip_group_check=True), inc="pe")
                    tmk = None
                    if lay == "b":
                        pb.wait("vector", tst)
                        pb.wait("vector", talb)
                        tmk = pb.op("vector", lambda e, o=pv, a=pv, b=bufs["ALB"][:, 0:w]:
                                    e.tensor_tensor(o, a, b, ALU.add), inc="dve")
                    if kc >= 4 * j:
                        dg = PS_A[:, par * 512 + jjs * 128: par * 512 + jjs * 128 + 128]
                        pb.wait("vector", tst)
                        tmk = pb.op("vector", lambda e, o=dg, a=dg, b=LOWM[:, :]:
                                    e.tensor_tensor(o, a, b, ALU.add), inc="dve")
                    pb.wait("scalar", tst)
                    pb.wait("scalar", tmk)
                    pb.wait("scalar", tok_attnv.get(par))
                    delta = 4 * j + jjs - kc
                    bias_ap = NMG[:, :] if lay == "a" else KB[:, delta:delta + 1]
                    te2 = pb.op("scalar", lambda e, o=ATN[par][:, 0:w], i=pv, b=bias_ap:
                                e.activation(o, i, AF.Exp, bias=b), inc="act")
                    tok_exp_c2[par] = te2
                    pb.wait("tensor", te2)
                    vl = VSB[:, kc * 256 + (h % 2) * 128: kc * 256 + (h % 2) * 128 + 128]
                    tattnv = pb.op("tensor", lambda e, o=ov[:, jjs * 128:512], l=vl,
                                   r=ATN[par][:, 0:w], st=(kc == 0), sp=(kc == 4 * j + 3):
                                   e.matmul(o, l, r, start=st, stop=sp,
                                            skip_group_check=True), inc="pe")
                    tok_attnv[par] = tattnv
                pb.wait("vector", tattnv)
                pb.wait("vector", tok_ao_store)
                tfm = pb.op("vector", lambda e, o=AO[:, j * 512:(j + 1) * 512], a=ov,
                            b=FBC[:, j * 512:(j + 1) * 512]:
                            e.tensor_tensor(o, a, b, ALU.mult), inc="dve")
                tok_fmult[j] = tfm
            if DEBUG and lay == "a" and s == 0 and h == 0:
                pb.barrier()
                pb.dma("sync", D["dbg_ao0"][:, :], AO[:, :], "d_dbg")
                pb.barrier()
            pb.wait("sync", tok_fmult[3])
            tok_ao_store = pb.dma("sync", D["ao"][s, h], AO[:, :], "d_ao")
        pb.wait("sync", tok_ao_store)

    def dense_epilogue(sc, pfx, s, wsrc_name, bias, xprev, xnext, xlnout):
        AOB = sc.sb(f"{pfx}_aob", (128, KC * 512), BF16)
        WDB = [sc.sb(f"{pfx}_wdb{i}", (128, H), BF16) for i in range(2)]
        XPV = [sc.sb(f"{pfx}_xpv{i}", (128, 512), F32) for i in range(2)]
        XNW = [sc.sb(f"{pfx}_xnw{i}", (128, 512), F32) for i in range(2)]
        SQT = sc.sb(f"{pfx}_sqt", (128, 512), F32)
        MBC = sc.sb(f"{pfx}_mbc", (128, 512), F32)
        RBC = sc.sb(f"{pfx}_rbc", (128, 512), F32)
        MRT = sc.sb(f"{pfx}_mrt", (128, 512), F32)
        T1 = sc.sb(f"{pfx}_t1", (128, 512), F32)
        XLT = sc.sb(f"{pfx}_xlt", (128, 512), BF16)
        wsrc = D[wsrc_name]
        hbm_fence()
        tok_w = [None, None]
        tok_aob = None
        tok_ev = [None, None]
        tok_store = [None, None]
        tok_stats = [None, None]
        tok_xp = [None, None]
        tok_xl = None
        for tau in range(4):
            g0 = s * S + tau * 512
            pb.wait("sync", tok_aob)
            pb.wait("sync", pb.tok("pe"))
            for kc in range(KC):
                taob = pb.dma("sync", AOB[:, kc * 512:(kc + 1) * 512],
                              D["ao"][s, kc, :, tau * 512:tau * 512 + 512], "d_aob")
            tok_aob = taob
            pe_sq = None
            sq_tok = None
            for f in range(KC):
                slot = f % 2
                pb.wait("gpsimd", tok_w[slot])
                tw = pb.dma("gpsimd", WDB[slot][:, :], wsrc[f], f"d_de{slot}")
                prep_step(1)
                pb.wait("tensor", tw)
                pb.wait("tensor", taob)
                pb.wait("tensor", tok_ev[slot])
                specs = [(PS_A[:, slot * 512:(slot + 1) * 512],
                          WDB[slot][:, kc * 128:(kc + 1) * 128],
                          AOB[:, kc * 512:(kc + 1) * 512], kc == 0, kc == KC - 1)
                         for kc in range(KC)]
                td = mm_loop(specs)
                tok_w[slot] = td
                pb.wait("sync", tok_ev[slot])
                txp = pb.dma("sync", XPV[slot][:, :], D[xprev][f, :, g0:g0 + 512],
                             f"d_dxp{slot}")
                pb.wait("vector", td)
                pb.wait("vector", txp)
                pb.wait("vector", tok_store[slot])
                pb.wait("vector", tok_stats[slot])
                if bias is not None:
                    txn = pb.op("vector", lambda e, o=XNW[slot][:, :],
                                i=PS_A[:, slot * 512:(slot + 1) * 512],
                                s1=bias[:, f:f + 1], b=XPV[slot][:, :]:
                                e.scalar_tensor_tensor(o, i, s1, b, op0=ALU.add,
                                                       op1=ALU.add), inc="dve")
                else:
                    txn = pb.op("vector", lambda e, o=XNW[slot][:, :],
                                i=PS_A[:, slot * 512:(slot + 1) * 512],
                                b=XPV[slot][:, :]:
                                e.tensor_tensor(o, i, b, ALU.add), inc="dve")
                tok_ev[slot] = txn
                pb.wait("sync", txn)
                tst = pb.dma("sync", D[xnext][f, :, g0:g0 + 512], XNW[slot][:, :],
                             f"d_dxn{slot}")
                tok_store[slot] = tst
                if xlnout is not None:
                    pb.wait("tensor", txn)
                    pb.op("tensor", lambda e, o=PS_B[0:1, 0:512], l=ONESM[:, :],
                          r=XNW[slot][:, :], st=(f == 0), sp=(f == KC - 1):
                          e.matmul(o, l, r, start=st, stop=sp, skip_group_check=True))
                    pb.wait("scalar", txn)
                    pb.wait("scalar", sq_tok)
                    ta = pb.op("scalar", lambda e, o=SQT[:, :], i=XNW[slot][:, :]:
                               e.activation(o, i, AF.Square), inc="act")
                    pb.wait("tensor", ta)
                    pe_sq = pb.op("tensor", lambda e, o=PS_B[0:1, 512:1024],
                                  l=ONESM[:, :], r=SQT[:, :],
                                  st=(f == 0), sp=(f == KC - 1):
                                  e.matmul(o, l, r, start=st, stop=sp,
                                           skip_group_check=True), inc="pe")
                    sq_tok = pe_sq
                    tok_stats[slot] = pe_sq
            if xlnout is not None:
                tfin, _tfa = ln_finalize(PS_B[0:1, 0:512], PS_B[0:1, 512:1024],
                                         MBC, RBC, MRT, 512, pe_sq, 1024)
                pb.waitc("sync", "d_dxn0")
                pb.waitc("sync", "d_dxn1")
                for f in range(KC):
                    slot = f % 2
                    pb.wait("sync", tok_xp[slot])
                    tl = pb.dma("sync", XPV2 := XPV[slot][:, :],
                                D[xnext][f, :, g0:g0 + 512], f"d_dx2{slot}")
                    pb.wait("vector", tl)
                    pb.wait("vector", tfin)
                    pb.wait("vector", tok_xl)
                    pb.op("vector", lambda e, o=T1[:, :], a=XPV2, b=RBC[:, :]:
                          e.tensor_tensor(o, a, b, ALU.mult), inc="dve")
                    t2 = pb.op("vector", lambda e, o=XLT[:, :], a=T1[:, :],
                               b=MRT[:, :]:
                               e.tensor_tensor(o, a, b, ALU.subtract), inc="dve")
                    tok_xp[slot] = t2
                    pb.wait("sync", t2)
                    tok_xl = pb.dma("sync", D[xlnout][f, :, g0:g0 + 512], XLT[:, :],
                                    "d_dxl")

    def ffn(sc, pfx, xln_src, xprev, xnext, xlnout):
        HDN = sc.sb(f"{pfx}_hdn", (128, 44 * 512), BF16)
        XLT = [sc.sb(f"{pfx}_xlt{i}", (128, KC * 512), BF16) for i in range(2)]
        GWB = [sc.sb(f"{pfx}_gwb{i}", (128, H), BF16) for i in range(2)]
        UWB = [sc.sb(f"{pfx}_uwb{i}", (128, H), BF16) for i in range(2)]
        DWB = [sc.sb(f"{pfx}_dwb{i}", (128, INTER), BF16) for i in range(2)]
        SG = [sc.sb(f"{pfx}_sg{i}", (128, 512), F32) for i in range(2)]
        XPV = [sc.sb(f"{pfx}_xpv{i}", (128, 512), F32) for i in range(2)]
        XNW = [sc.sb(f"{pfx}_xnw{i}", (128, 512), F32) for i in range(2)]
        SQT = sc.sb(f"{pfx}_sqt", (128, 512), F32)
        MBC = sc.sb(f"{pfx}_mbc", (128, 512), F32)
        RBC = sc.sb(f"{pfx}_rbc", (128, 512), F32)
        MRT = sc.sb(f"{pfx}_mrt", (128, 512), F32)
        T1 = sc.sb(f"{pfx}_t1", (128, 512), F32)
        XLO = sc.sb(f"{pfx}_xlo", (128, 512), BF16)
        hbm_fence()
        pb.waitc("gpsimd", "d_preps0")
        pb.waitc("gpsimd", "d_preps1")
        tok_g = [None, None]; tok_u = [None, None]; tok_d = [None, None]
        tok_sg = [None, None]; tok_hd = [None, None]
        tok_ev = [None, None]
        tok_store = [None, None]; tok_stats = [None, None]
        tok_xp = [None, None]
        tok_xlt_use = [None, None]
        tok_xl = None
        for tt in range(8):
            t0 = tt * 512
            xslot = tt % 2
            pb.wait("sync", tok_xlt_use[xslot])
            for kc in range(KC):
                txl = pb.dma("sync", XLT[xslot][:, kc * 512:(kc + 1) * 512],
                             D[xln_src][kc, :, t0:t0 + 512], f"d_fxlt{xslot}")
            for m in range(44):
                slot = m % 2
                pb.wait("gpsimd", tok_g[slot])
                tgw = pb.dma("gpsimd", GWB[slot][:, :], D["gwb"][m], f"d_fg{slot}")
                pb.wait("scalar", tok_u[slot])
                tuw = pb.dma("scalar", UWB[slot][:, :], D["uwb"][m], f"d_fu{slot}")
                gps = PS_A[:, slot * 1024: slot * 1024 + 512]
                ups = PS_A[:, slot * 1024 + 512: slot * 1024 + 1024]
                pb.wait("tensor", tgw)
                pb.wait("tensor", txl)
                pb.wait("tensor", tok_sg[slot])
                specs = [(gps, GWB[slot][:, kc * 128:(kc + 1) * 128],
                          XLT[xslot][:, kc * 512:(kc + 1) * 512],
                          kc == 0, kc == KC - 1) for kc in range(KC)]
                tg = mm_loop(specs)
                tok_g[slot] = tg
                pb.wait("scalar", tg)
                tsg = pb.op("scalar", lambda e, o=SG[slot][:, :], i=gps,
                            s1=GS[:, m:m + 1]:
                            e.activation(o, i, AF.Silu, scale=s1), inc="act")
                tok_sg[slot] = tsg
                pb.wait("tensor", tuw)
                pb.wait("tensor", tok_hd[slot])
                specs = [(ups, UWB[slot][:, kc * 128:(kc + 1) * 128],
                          XLT[xslot][:, kc * 512:(kc + 1) * 512],
                          kc == 0, kc == KC - 1) for kc in range(KC)]
                tu = mm_loop(specs)
                tok_u[slot] = tu
                pb.wait("vector", tu)
                pb.wait("vector", tsg)
                thd = pb.op("vector", lambda e, o=HDN[:, m * 512:(m + 1) * 512],
                            i=ups, s1=US[:, m:m + 1], b=SG[slot][:, :]:
                            e.scalar_tensor_tensor(o, i, s1, b, op0=ALU.mult,
                                                   op1=ALU.mult), inc="dve")
                tok_hd[slot] = thd
                prep_step(1)
            tok_xlt_use[xslot] = pb.tok("pe")
            tok_hd_all = pb.tok("dve")
            pe_sq = None
            sq_tok = None
            for f in range(KC):
                slot = f % 2
                pb.wait("scalar", tok_d[slot])
                tdw = pb.dma("scalar", DWB[slot][:, :], D["dwb"][f], f"d_fd{slot}")
                pvw = PS_A[:, slot * 512:(slot + 1) * 512]
                pb.wait("tensor", tdw)
                pb.wait("tensor", tok_hd_all)
                pb.wait("tensor", tok_ev[slot])
                specs = [(pvw, DWB[slot][:, m * 128:(m + 1) * 128],
                          HDN[:, m * 512:(m + 1) * 512], m == 0, m == 43)
                         for m in range(44)]
                td = mm_loop(specs)
                tok_d[slot] = td
                pb.wait("sync", tok_ev[slot])
                txp = pb.dma("sync", XPV[slot][:, :], D[xprev][f, :, t0:t0 + 512],
                             f"d_fxp{slot}")
                pb.wait("vector", td)
                pb.wait("vector", txp)
                pb.wait("vector", tok_store[slot])
                pb.wait("vector", tok_stats[slot])
                txn = pb.op("vector", lambda e, o=XNW[slot][:, :], i=pvw,
                            s1=DS[:, f:f + 1], b=XPV[slot][:, :]:
                            e.scalar_tensor_tensor(o, i, s1, b, op0=ALU.mult,
                                                   op1=ALU.add), inc="dve")
                tok_ev[slot] = txn
                pb.wait("sync", txn)
                tst = pb.dma("sync", D[xnext][f, :, t0:t0 + 512], XNW[slot][:, :],
                             f"d_fxn{slot}")
                tok_store[slot] = tst
                if xlnout is not None:
                    pb.wait("tensor", txn)
                    pb.op("tensor", lambda e, o=PS_B[0:1, 0:512], l=ONESM[:, :],
                          r=XNW[slot][:, :], st=(f == 0), sp=(f == KC - 1):
                          e.matmul(o, l, r, start=st, stop=sp, skip_group_check=True))
                    pb.wait("scalar", txn)
                    pb.wait("scalar", sq_tok)
                    ta = pb.op("scalar", lambda e, o=SQT[:, :], i=XNW[slot][:, :]:
                               e.activation(o, i, AF.Square), inc="act")
                    pb.wait("tensor", ta)
                    pe_sq = pb.op("tensor", lambda e, o=PS_B[0:1, 512:1024],
                                  l=ONESM[:, :], r=SQT[:, :],
                                  st=(f == 0), sp=(f == KC - 1):
                                  e.matmul(o, l, r, start=st, stop=sp,
                                           skip_group_check=True), inc="pe")
                    sq_tok = pe_sq
                    tok_stats[slot] = pe_sq
            if xlnout is not None:
                tfin, tfin_act = ln_finalize(PS_B[0:1, 0:512], PS_B[0:1, 512:1024],
                                             MBC, RBC, MRT, 512, pe_sq, 1024)
                tok_sg = [tfin_act, tfin_act]
                tok_hd = [tfin, tfin]
                tok_ev = list(tok_ev)
                pb.waitc("sync", "d_fxn0")
                pb.waitc("sync", "d_fxn1")
                for f in range(KC):
                    slot = f % 2
                    pb.wait("sync", tok_xp[slot])
                    tl = pb.dma("sync", XPV[slot][:, :],
                                D[xnext][f, :, t0:t0 + 512], f"d_fx2{slot}")
                    pb.wait("vector", tl)
                    pb.wait("vector", tfin)
                    pb.wait("vector", tok_xl)
                    pb.op("vector", lambda e, o=T1[:, :],
                          a=XPV[slot][:, :], b=RBC[:, :]:
                          e.tensor_tensor(o, a, b, ALU.mult), inc="dve")
                    t2 = pb.op("vector", lambda e, o=XLO[:, :], a=T1[:, :],
                               b=MRT[:, :]:
                               e.tensor_tensor(o, a, b, ALU.subtract), inc="dve")
                    tok_xp[slot] = t2
                    pb.wait("sync", t2)
                    tok_xl = pb.dma("sync", D[xlnout][f, :, t0:t0 + 512],
                                    XLO[:, :], "d_fxl")
            if xlnout is None:
                tok_sg = list(tok_ev)
                tok_hd = list(tok_ev)

    def attn_b_ccq(sc, pfx, s, CQ, CC, xln_src):
        scl = Scope()
        XLT = scl.sb(f"{pfx}_xlt", (128, KC * 512), BF16)
        WB = [scl.sb(f"{pfx}_wb{i}", (128, H), BF16) for i in range(2)]
        hbm_fence()
        tok_w = [None, None]
        tok_xlt = None
        tok_ev = [None, None]
        for tblk in range(4):
            g0 = s * S + tblk * 512
            pb.wait("sync", tok_xlt)
            pb.wait("sync", pb.tok("pe"))
            for kc in range(KC):
                txl = pb.dma("sync", XLT[:, kc * 512:(kc + 1) * 512],
                             D[xln_src][kc, :, g0:g0 + 512], "d_cbx")
            tok_xlt = txl
            for mb in range(16):
                is_c = mb >= 12
                wsrc = D["wkd"][mb - 12] if is_c else D["wqd"][mb]
                slot = mb % 2
                pb.wait("gpsimd", tok_w[slot])
                tw = pb.dma("gpsimd", WB[slot][:, :], wsrc, f"d_cb{slot}")
                prep_step(1)
                pb.wait("tensor", tw)
                pb.wait("tensor", txl)
                pb.wait("tensor", tok_ev[slot])
                specs = [(PS_A[:, slot * 512:(slot + 1) * 512],
                          WB[slot][:, kc * 128:(kc + 1) * 128],
                          XLT[:, kc * 512:(kc + 1) * 512], kc == 0, kc == KC - 1)
                         for kc in range(KC)]
                tq = mm_loop(specs)
                tok_w[slot] = tq
                if is_c:
                    dst = CC[:, (mb - 12) * S + tblk * 512: (mb - 12) * S + tblk * 512 + 512]
                else:
                    dst = CQ[:, mb * S + tblk * 512: mb * S + tblk * 512 + 512]
                pb.wait("scalar", tq)
                tev = pb.op("scalar", lambda e, o=dst,
                            i=PS_A[:, slot * 512:(slot + 1) * 512]:
                            e.activation(o, i, AF.Copy), inc="act")
                tok_ev[slot] = tev
        pb.waitc("tensor", "act")
        scl.close()

    pb.barrier()

    sc1 = Scope()
    XLN = sc1.sb("a_xln", (128, KC * S), BF16)
    bufs_a = dict(
        QF=sc1.sb("a_qf", (128, S), F32),
        QT=sc1.sb("a_qt", (128, S), BF16),
        KT=sc1.sb("a_kt", (128, S), BF16),
        VSB=sc1.sb("a_vsb", (128, KC * 256), BF16),
        FBC=sc1.sb("a_fbc", (128, S), F32),
        AO=sc1.sb("a_ao", (128, S), BF16),
        WQ=[sc1.sb(f"a_wq{i}", (128, H), BF16) for i in range(2)],
        WK=[sc1.sb(f"a_wk{i}", (128, H), BF16) for i in range(2)],
        WV=sc1.sb("a_wv", (128, KC * 256), BF16),
        BVBC=sc1.sb("a_bvbc", (128, H), F32),
        TCQ=sc1.sb("a_tcq", (128, S), F32),
        TSQ=sc1.sb("a_tsq", (128, S), F32),
        TCK=sc1.sb("a_tck", (128, S), F32),
        TSK=sc1.sb("a_tsk", (128, S), F32),
    )
    bufs_a["TRASH"] = bufs_a["QF"]
    for dst, srcn in [(bufs_a["TCQ"], "tbl_cq"), (bufs_a["TSQ"], "tbl_sq"),
                      (bufs_a["TCK"], "tbl_ck"), (bufs_a["TSK"], "tbl_sk")]:
        pb.dma("sync", dst[:, :], D[srcn][:, :], "d_const")
    tok_const = pb.tok("d_const")
    for e in ("tensor", "vector", "scalar"):
        pb.wait(e, tok_const)
    for n in range(4):
        pb.wait("tensor", tok_const)
        tb_ = pb.op("tensor", lambda e, o=PS_B[:, n * 512:(n + 1) * 512],
                    l=ONESK1[:, :], r=BVROW[0:1, n * 512:(n + 1) * 512]:
                    e.matmul(o, l, r, start=True, stop=True, skip_group_check=True),
                    inc="pe")
        pb.wait("scalar", tb_)
        pb.op("scalar", lambda e, o=bufs_a["BVBC"][:, n * 512:(n + 1) * 512],
              i=PS_B[:, n * 512:(n + 1) * 512]:
              e.activation(o, i, AF.Copy), inc="act")
    pb.waitc("tensor", "act")

    for s in range(NS):
        ln_prep(sc1, f"lp{s}", s, XLN, bufs_a["QF"], bufs_a["FBC"])
        pb.barrier()
        if DEBUG and s == 0:
            for kc in range(KC):
                pb.dma("sync", D["dbg_xln"][kc], XLN[:, kc * S:(kc + 1) * S], "d_dbg")
            pb.barrier()
        attn_heads("a", s, XLN, bufs_a)
        pb.barrier()
    if DEBUG:
        for hh in range(NH):
            pb.dma("sync", D["dbg_ao"][hh], D["ao"][0, hh], "d_dbg")
        pb.barrier()
    sc1.close()

    sc1d = Scope()
    for s in range(NS):
        dense_epilogue(sc1d, f"d1{s}", s, "wd", BD, "x0",
                       "xA" if SUBLAYERS > 1 else "out",
                       "lnA" if SUBLAYERS > 1 else None)
        pb.barrier()
    sc1d.close()

    if SUBLAYERS > 1:
        sc2 = Scope()
        prep_finish()
        pb.barrier()
        ffn(sc2, "f1", "lnA", "xA", "xB" if SUBLAYERS > 2 else "out",
            "lnB" if SUBLAYERS > 2 else None)
        pb.barrier()
        sc2.close()

    if SUBLAYERS > 2:
        sc3 = Scope()
        CQ = sc3.sb("b_cq", (128, 12 * S), BF16)
        CC = sc3.sb("b_cc", (128, 4 * S), BF16)
        bufs_b = dict(
            QT=sc3.sb("b_qt", (128, S), BF16),
            KT=sc3.sb("b_kt", (128, S), BF16),
            VSB=sc3.sb("b_vsb", (128, KC * 256), BF16),
            FBC=sc3.sb("b_fbc", (128, S), F32),
            RAMP=sc3.sb("b_ramp", (128, S), F32),
            AO=sc3.sb("b_ao", (128, S), BF16),
            WQ=[sc3.sb(f"b_wq{i}", (128, RQ), BF16) for i in range(2)],
            WK=[sc3.sb(f"b_wk{i}", (128, RKV), BF16) for i in range(2)],
            WV=sc3.sb("b_wv", (128, 4 * 256), BF16),
            RH=sc3.sb("b_rh", (128, S), F32),
            ALB=sc3.sb("b_alb", (128, 512), F32),
            DCONST=sc3.sb("b_dconst", (128, 512), F32),
            CQ=CQ, CC=CC,
        )
        bufs_b["TRASH"] = bufs_b["FBC"]
        pb.dma("sync", bufs_b["RAMP"][:, :], D["ramp"][:, :], "d_const")
        tok_r = pb.dma("sync", bufs_b["DCONST"][:, :], D["dconst"][:, :], "d_const")
        pb.wait("vector", tok_r)
        for s in range(NS):
            attn_b_ccq(sc3, f"cb{s}", s, CQ, CC, "lnB")
            pb.barrier()
            attn_heads("b", s, None, bufs_b)
            pb.barrier()
        sc3.close()
        sc3d = Scope()
        for s in range(NS):
            dense_epilogue(sc3d, f"d3{s}", s, "wo", None, "xB",
                           "xA" if SUBLAYERS > 3 else "out",
                           "lnA" if SUBLAYERS > 3 else None)
            pb.barrier()
        sc3d.close()

    if SUBLAYERS > 3:
        sc4 = Scope()
        ffn(sc4, "f2", "lnA", "xA", "out", None)
        pb.barrier()
        sc4.close()

    for sem in sorted(pb.cnt):
        if sem.startswith("d_") and pb.cnt[sem] > 0:
            pb.wait("sync", (sem, pb.cnt[sem]))

    with ExitStack() as st:
        sems = {}
        for name in pb.cnt:
            assert pb.cnt[name] < 60000, (name, pb.cnt[name])
            sems[name] = st.enter_context(nc.semaphore(name))
        block = st.enter_context(nc.Block())
        for eng in ENGS:
            def make(items):
                def f(e):
                    last = {}
                    for it in items:
                        if it[0] == "wait":
                            _, s_, v = it
                            if last.get(s_, -1) >= v:
                                continue
                            e.wait_ge(sems[s_], v)
                            last[s_] = v
                        else:
                            _, fn, inc, n = it
                            ins = fn(e)
                            if inc is not None:
                                ins.then_inc(sems[inc], n)
                return f
            getattr(block, eng)(make(pb.q[eng]))
    return nc


def kernel(**inputs):
    cache = {}
    in_maps = [_host_inputs(inputs, c, cache) for c in range(8)]
    nc = build_program()
    res = run_bass_kernel_spmd(nc, in_maps, core_ids=list(range(8)))
    out = np.empty((16, S, H), np.float32)
    for c in range(8):
        o = np.asarray(res.results[c]["out"])
        o = o.reshape(KC, 128, NS, S).transpose(2, 3, 0, 1).reshape(NS, S, H)
        out[2 * c: 2 * c + 2] = o
    return out


# version 06
# speedup vs baseline: 1.0537x; 1.0061x over previous
import math
from contextlib import ExitStack

import numpy as np

import concourse.bass as bass
import concourse.mybir as mybir
from concourse.bass_utils import run_bass_kernel_spmd

F32 = mybir.dt.float32
BF16 = mybir.dt.bfloat16
AF = mybir.ActivationFunctionType
ALU = mybir.AluOpType
AX = mybir.AxisListType

H = 2048
NH = 16
HD = 128
S = 2048
NS = 2
T = NS * S
INTER = 5632
RQ = 1536
RKV = 512
KC = H // 128
SCALE = 1.0 / math.sqrt(HD)
ROT = 32
EPS = 1e-5
NEG = -1.0e30
_BASE = 2.0 ** (-2.0 ** (-(math.log2(NH) - 3)))
SLOPES = [_BASE ** i for i in range(1, NH + 1)]

SUBLAYERS = 4
DEBUG = False
HEADS = 16
NO_H1_LOADS = False
H1_STAGE = 9
NO_H1_PREP = False

ENGS = ("tensor", "vector", "scalar", "gpsimd", "sync")


class PB:
    def __init__(self):
        self.q = {e: [] for e in ENGS}
        self.cnt = {}

    def _sem(self, s):
        if s not in self.cnt:
            self.cnt[s] = 0

    def op(self, eng, fn, inc=None, n=1):
        if inc is not None:
            self._sem(inc)
            self.cnt[inc] += n
            self.q[eng].append(("op", fn, inc, n))
            return (inc, self.cnt[inc])
        self.q[eng].append(("op", fn, None, 0))
        return None

    def wait(self, eng, token):
        if token is None:
            return
        s, v = token
        if v > 0:
            self.q[eng].append(("wait", s, v))

    def tok(self, sem):
        self._sem(sem)
        return (sem, self.cnt[sem])

    def waitc(self, eng, sem):
        self.wait(eng, self.tok(sem))

    def barrier(self):
        toks = {}
        for e in ENGS:
            toks[e] = self.op(e, lambda en: en.nop(), inc="bar_" + e)
        for e in ENGS:
            for e2 in ENGS:
                if e2 != e:
                    self.wait(e, toks[e2])

    def dma(self, eng, dst, src, sem):
        return self.op(eng, lambda e, d=dst, s=src: e.dma_start(d, s), inc=sem, n=16)


def _pack_lhsT(W):
    Kt, Mt = W.shape
    nk, nm = Kt // 128, Mt // 128
    w = W.reshape(nk, 128, nm, 128)
    return np.ascontiguousarray(w.transpose(2, 1, 0, 3).reshape(nm, 128, nk * 128))


def _host_inputs(inputs, core, cache={}):
    f32 = np.float32
    if not cache:
        W_qkv = np.asarray(inputs["L0_attn_a_W_qkv"], f32)
        b_qkv = np.asarray(inputs["L0_attn_a_b_qkv"], f32)
        W_d = np.asarray(inputs["L0_attn_a_W_dense"], f32)
        b_d = np.asarray(inputs["L0_attn_a_b_dense"], f32)
        qcols = np.concatenate([np.arange(h * 384, h * 384 + 128) for h in range(NH)])
        kcols = qcols + 128
        vcols = qcols + 256
        c = cache
        c["wq"] = _pack_lhsT(W_qkv[:, qcols])
        c["wk"] = _pack_lhsT(W_qkv[:, kcols])
        c["wv"] = np.ascontiguousarray(W_qkv[:, vcols].reshape(KC, 128, H))
        c["bq"] = np.ascontiguousarray(b_qkv[qcols].reshape(NH, 128).T)
        c["bk"] = np.ascontiguousarray(b_qkv[kcols].reshape(NH, 128).T)
        c["bv"] = np.ascontiguousarray(b_qkv[vcols].reshape(1, H))
        c["wd"] = _pack_lhsT(W_d)
        c["bd"] = np.ascontiguousarray(b_d.reshape(NH, 128).T)
        Wqd = np.asarray(inputs["L0_attn_b_Wq_down"], f32)
        Wqu = np.asarray(inputs["L0_attn_b_Wq_up"], f32)
        Wkd = np.asarray(inputs["L0_attn_b_Wkv_down"], f32)
        kup = np.asarray(inputs["L0_attn_b_k_up"], f32)
        vup = np.asarray(inputs["L0_attn_b_v_up"], f32)
        Wo = np.asarray(inputs["L0_attn_b_Wo"], f32)
        c["wqd"] = _pack_lhsT(np.ascontiguousarray(Wqd.T))
        c["wqu"] = _pack_lhsT(np.ascontiguousarray(Wqu.T))
        c["wkd"] = _pack_lhsT(np.ascontiguousarray(Wkd.T))
        c["kup"] = np.ascontiguousarray(
            np.stack([_pack_lhsT(kup[h])[0] for h in range(NH)]))
        c["vup"] = np.ascontiguousarray(
            vup.transpose(1, 0, 2).reshape(4, 128, NH * HD))
        c["wo"] = _pack_lhsT(np.ascontiguousarray(Wo.T))
        gw = np.asarray(inputs["L0_ffn_gate_w"], f32)
        uw = np.asarray(inputs["L0_ffn_up_w"], f32)
        dwn = np.asarray(inputs["L0_ffn_down_w"], f32)
        c["gw"] = _pack_lhsT(np.ascontiguousarray(gw.T))
        c["uw"] = _pack_lhsT(np.ascontiguousarray(uw.T))
        c["dw"] = _pack_lhsT(np.ascontiguousarray(dwn.T))
        c["gs"] = np.ascontiguousarray(
            np.asarray(inputs["L0_ffn_gate_s"], f32).reshape(44, 128).T)
        c["us"] = np.ascontiguousarray(
            np.asarray(inputs["L0_ffn_up_s"], f32).reshape(44, 128).T)
        c["ds"] = np.ascontiguousarray(
            np.asarray(inputs["L0_ffn_down_s"], f32).reshape(16, 128).T)
        c["ident"] = np.eye(128, dtype=f32)
        c["epsc"] = np.full((128, 1), EPS, f32)
        c["onesk1"] = np.ones((1, 128), f32)
        c["onesm"] = np.full((128, 1), 1.0 / H, f32)
        iu = np.arange(128)
        c["upmask"] = np.where(iu[None, :] > iu[:, None], np.float32(NEG), 0.0).astype(f32)
        c["lowmask"] = np.where(iu[:, None] > iu[None, :], np.float32(NEG), 0.0).astype(f32)
        c["ramp"] = np.ascontiguousarray(
            np.broadcast_to(np.arange(S, dtype=f32), (128, S)))
        c["qpos"] = (np.arange(16)[None, :] * 128 + iu[:, None]).astype(f32)
        c["dconst"] = (iu[:, None] - np.arange(512)[None, :]).astype(f32)
        c["dn16"] = np.broadcast_to((-128.0 * np.arange(16, dtype=f32))[None, :],
                                    (128, 16)).copy()
        R = np.zeros((128, 128), f32)
        for i in range(16):
            R[i, i + 16] = -1.0
            R[i + 16, i] = 1.0
        c["rt"] = np.ascontiguousarray(R.T)
        inv = 1.0 / (10000.0 ** (np.arange(0, ROT, 2, dtype=f32) / ROT))
        tv = np.arange(S, dtype=f32)
        fr = np.outer(tv, inv)
        emb = np.concatenate([fr, fr], axis=1).T
        cos = np.cos(emb).astype(f32)
        sin = np.sin(emb).astype(f32)
        cq = np.full((128, S), SCALE, f32); cq[:ROT] = cos * SCALE
        sq = np.zeros((128, S), f32);       sq[:ROT] = sin * SCALE
        ck = np.ones((128, S), f32);        ck[:ROT] = cos
        sk = np.zeros((128, S), f32);       sk[:ROT] = sin
        c["tbl_cq"], c["tbl_sq"], c["tbl_ck"], c["tbl_sk"] = cq, sq, ck, sk
    m = dict(cache)
    hs = np.asarray(inputs["L0_attn_a_hidden_states"], np.float32)
    hc = hs[2 * core: 2 * core + 2]
    m["x0"] = np.ascontiguousarray(hc.transpose(2, 0, 1).reshape(KC, 128, T))
    return m


def build_program():
    nc = bass.Bass("TRN2", target_bir_lowering=False)
    D = {}

    def din(name, shape, dt=F32):
        D[name] = nc.dram_tensor(name, list(shape), dt, kind="ExternalInput").ap()

    din("x0", (KC, 128, T))
    din("wq", (NH, 128, H)); din("wk", (NH, 128, H)); din("wv", (KC, 128, H))
    din("bq", (128, NH)); din("bk", (128, NH)); din("bv", (1, H))
    din("wd", (KC, 128, H)); din("bd", (128, NH))
    din("wqd", (12, 128, H)); din("wqu", (NH, 128, RQ)); din("wkd", (4, 128, H))
    din("kup", (NH, 128, RKV)); din("vup", (4, 128, H)); din("wo", (KC, 128, H))
    din("gw", (44, 128, H)); din("uw", (44, 128, H)); din("dw", (KC, 128, INTER))
    din("gs", (128, 44)); din("us", (128, 44)); din("ds", (128, KC))
    din("ident", (128, 128)); din("onesk1", (1, 128)); din("onesm", (128, 1))
    din("epsc", (128, 1))
    din("upmask", (128, 128)); din("lowmask", (128, 128))
    din("ramp", (128, S)); din("qpos", (128, 16)); din("rt", (128, 128))
    din("dconst", (128, 512)); din("dn16", (128, 16))
    for n in ("tbl_cq", "tbl_sq", "tbl_ck", "tbl_sk"):
        din(n, (128, S))

    D["out"] = nc.dram_tensor("out", [KC, 128, T], F32, kind="ExternalOutput").ap()
    if DEBUG:
        D["dbg_xln"] = nc.dram_tensor("dbg_xln", [KC, 128, S], BF16, kind="ExternalOutput").ap()
        D["dbg_qt"] = nc.dram_tensor("dbg_qt", [128, S], BF16, kind="ExternalOutput").ap()
        D["dbg_kt"] = nc.dram_tensor("dbg_kt", [128, S], BF16, kind="ExternalOutput").ap()
        D["dbg_ao"] = nc.dram_tensor("dbg_ao", [NH, 128, S], BF16, kind="ExternalOutput").ap()
        D["dbg_st"] = nc.dram_tensor("dbg_st", [4, 128, 16], F32, kind="ExternalOutput").ap()
        D["dbg_nmg"] = nc.dram_tensor("dbg_nmg", [128, 1], F32, kind="ExternalOutput").ap()
        D["dbg_fbc"] = nc.dram_tensor("dbg_fbc", [128, S], F32, kind="ExternalOutput").ap()
        D["dbg_ao0"] = nc.dram_tensor("dbg_ao0", [128, S], BF16, kind="ExternalOutput").ap()
    for n, dt in (("xA", F32), ("xB", F32), ("lnA", BF16), ("lnB", BF16)):
        D[n] = nc.dram_tensor(n, [KC, 128, T], dt, kind="Internal").ap()
    D["ao"] = nc.dram_tensor("ao", [NS, NH, 128, S], BF16, kind="Internal").ap()
    D["gwb"] = nc.dram_tensor("gwb", [44, 128, H], BF16, kind="Internal").ap()
    D["uwb"] = nc.dram_tensor("uwb", [44, 128, H], BF16, kind="Internal").ap()
    D["dwb"] = nc.dram_tensor("dwb", [KC, 128, INTER], BF16, kind="Internal").ap()

    pb = PB()

    def sb(name, shape, dt):
        return nc.alloc_sbuf_tensor(name, list(shape), dt)

    IDENT = sb("IDENT", (128, 128), F32)
    ONESK1 = sb("ONESK1", (1, 128), F32)
    ONESM = sb("ONESM", (128, 1), F32)
    EPSC = sb("EPSC", (128, 1), F32)
    UPM = sb("UPM", (128, 128), F32)
    LOWM = sb("LOWM", (128, 128), F32)
    QPOS = sb("QPOS", (128, 16), F32)
    DN16 = sb("DN16", (128, 16), F32)
    RT = sb("RT", (128, 128), F32)
    BQ = sb("BQ", (128, 16), F32)
    BK = sb("BK", (128, 16), F32)
    BD = sb("BD", (128, 16), F32)
    BVROW = sb("BVROW", (1, H), F32)
    GS = sb("GS", (128, 44), F32)
    US = sb("US", (128, 44), F32)
    DS = sb("DS", (128, 16), F32)
    NEGM = sb("NEGM", (128, 16), F32)
    SACC = sb("SACC", (128, 16), F32)
    LNSB = sb("LNSB", (128, 16), F32)
    GTB = sb("GTB", (128, 16), F32)
    FQT = sb("FQT", (128, 16), F32)
    KB = sb("KB", (128, 16), F32)
    QPS = sb("QPS", (128, 16), F32)
    NM1 = sb("NM1", (128, 1), F32)
    NMG = sb("NMG", (128, 1), F32)
    TROW = sb("TROW", (1, 128), F32)
    MGR = sb("MGR", (1, 1), F32)
    FROW = sb("FROW", (1, S), F32)
    SROW = sb("SROW", (1, 512), F32)
    SROW2 = sb("SROW2", (1, 512), F32)
    ATN = tuple(sb(f"ATN{i}", (128, 512), BF16) for i in range(4))
    PREP = tuple(sb(f"PREP{i}", (128, H), BF16) for i in range(2))

    PS_A = nc.alloc_psum_tensor("PS_A", [128, 2048], F32)
    PS_B = nc.alloc_psum_tensor("PS_B", [128, 2048], F32)

    class Scope:
        def __init__(self):
            self.base = nc.sbuf_base
        def sb(self, name, shape, dt):
            return nc.alloc_sbuf_tensor(name, list(shape), dt)
        def close(self):
            nc.sbuf_base = self.base

    for dst, src in [(IDENT, "ident"), (ONESK1, "onesk1"), (ONESM, "onesm"),
                     (UPM, "upmask"), (LOWM, "lowmask"), (EPSC, "epsc"),
                     (QPOS, "qpos"), (RT, "rt"), (DN16, "dn16"),
                     (BQ, "bq"), (BK, "bk"),
                     (BD, "bd"), (BVROW, "bv"), (GS, "gs"), (US, "us"), (DS, "ds")]:
        pb.dma("sync", dst[:, :], D[src][:, :], "d_const")
    tok_const = pb.tok("d_const")
    for e in ("tensor", "vector", "scalar"):
        pb.wait(e, tok_const)

    prep_jobs = []
    for m in range(44):
        prep_jobs.append((D["gw"][m], D["gwb"][m], H))
        prep_jobs.append((D["uw"][m], D["uwb"][m], H))
    for f in range(KC):
        for c0 in range(0, INTER, 2048):
            w = min(2048, INTER - c0)
            prep_jobs.append((D["dw"][f, :, c0:c0 + w], D["dwb"][f, :, c0:c0 + w], w))
    prep_state = {"i": 0, "tok": [None, None]}

    def prep_step(k=2):
        for _ in range(k):
            i = prep_state["i"]
            if i >= len(prep_jobs):
                return
            slot = i % 2
            src, dst, w = prep_jobs[i]
            pb.wait("gpsimd", prep_state["tok"][slot])
            tl = pb.dma("gpsimd", PREP[slot][:, 0:w], src, f"d_prep{slot}")
            pb.wait("sync", tl)
            ts = pb.dma("sync", dst, PREP[slot][:, 0:w], f"d_preps{slot}")
            prep_state["tok"][slot] = ts
            prep_state["i"] = i + 1

    def prep_finish():
        while prep_state["i"] < len(prep_jobs):
            prep_step(8)

    def mm_loop(eng_specs, final_inc="pe"):
        tok = None
        nsp = len(eng_specs)
        for i, (o, l, r, st, sp) in enumerate(eng_specs):
            inc = final_inc if i == nsp - 1 else None
            tok = pb.op("tensor", lambda e, o=o, l=l, r=r, st=st, sp=sp:
                        e.matmul(o, l, r, start=st, stop=sp, skip_group_check=True),
                        inc=inc)
        return tok

    STORE_SEMS = ("d_dxn0", "d_dxn1", "d_dxl", "d_fxn0", "d_fxn1", "d_fxl",
                  "d_ao", "d_fx20", "d_fx21", "d_dx20", "d_dx21")

    def hbm_fence(eng="sync"):
        for s_ in STORE_SEMS:
            pb.waitc(eng, s_)

    def ln_finalize(ps_sum, ps_sq, Mbc, Rbc, MRt, width, tok_pe_stats, psA_off):
        pb.wait("scalar", tok_pe_stats)
        pb.waitc("scalar", "pe")
        pb.op("scalar", lambda e, o=SROW[0:1, 0:width], i=ps_sum:
              e.activation(o, i, AF.Copy), inc="act")
        t2 = pb.op("scalar", lambda e, o=SROW2[0:1, 0:width], i=ps_sq:
                   e.activation(o, i, AF.Copy), inc="act")
        pb.wait("tensor", t2)
        pb.waitc("tensor", "dve")
        bm = pb.op("tensor", lambda e, o=PS_A[:, psA_off:psA_off + width],
                   l=ONESK1[:, :], r=SROW[0:1, 0:width]:
                   e.matmul(o, l, r, start=True, stop=True, skip_group_check=True),
                   inc="pe")
        bq_ = pb.op("tensor", lambda e, o=PS_A[:, psA_off + width:psA_off + 2 * width],
                    l=ONESK1[:, :], r=SROW2[0:1, 0:width]:
                    e.matmul(o, l, r, start=True, stop=True, skip_group_check=True),
                    inc="pe")
        pb.wait("scalar", bq_)
        pb.op("scalar", lambda e, o=Mbc[:, 0:width], i=PS_A[:, psA_off:psA_off + width]:
              e.activation(o, i, AF.Copy), inc="act")
        tsq = pb.op("scalar", lambda e, o=MRt[:, 0:width],
                    i=PS_A[:, psA_off:psA_off + width]:
                    e.activation(o, i, AF.Square), inc="act")
        pb.wait("vector", tsq)
        tv = pb.op("vector", lambda e, o=Rbc[:, 0:width],
                   a=PS_A[:, psA_off + width:psA_off + 2 * width], b=MRt[:, 0:width]:
                   e.tensor_tensor(o, a, b, ALU.subtract), inc="dve")
        pb.wait("scalar", tv)
        pb.op("scalar", lambda e, o=Rbc[:, 0:width], i=Rbc[:, 0:width]:
              e.activation(o, i, AF.Ln, bias=EPSC[:, :]), inc="act")
        te = pb.op("scalar", lambda e, o=Rbc[:, 0:width], i=Rbc[:, 0:width]:
                   e.activation(o, i, AF.Exp, scale=-0.5), inc="act")
        pb.wait("vector", te)
        tmr = pb.op("vector", lambda e, o=MRt[:, 0:width], a=Mbc[:, 0:width],
                    b=Rbc[:, 0:width]:
                    e.tensor_tensor(o, a, b, ALU.mult), inc="dve")
        return tmr, pb.tok("act")

    def ln_prep(sc, pfx, s, XLN, scr, fbc):
        XPV = [scr[:, 0:512], scr[:, 512:1024]]
        SQT = scr[:, 1024:1536]
        MBC = fbc[:, 0:512]
        RBC = fbc[:, 512:1024]
        MRT = fbc[:, 1024:1536]
        T1 = fbc[:, 1536:2048]
        xv_tok = [None, None]
        for tau in range(4):
            g0 = s * S + tau * 512
            sq_tok = None
            pe_sq = None
            for kc in range(KC):
                slot = kc % 2
                pb.wait("sync", xv_tok[slot])
                tl = pb.dma("sync", XPV[slot], D["x0"][kc, :, g0:g0 + 512],
                            f"d_lpx{slot}")
                pb.wait("tensor", tl)
                pb.op("tensor", lambda e, o=PS_B[0:1, 0:512], l=ONESM[:, :],
                      r=XPV[slot], st=(kc == 0), sp=(kc == KC - 1):
                      e.matmul(o, l, r, start=st, stop=sp, skip_group_check=True))
                pb.wait("scalar", tl)
                pb.wait("scalar", sq_tok)
                ta = pb.op("scalar", lambda e, o=SQT, i=XPV[slot]:
                           e.activation(o, i, AF.Square), inc="act")
                pb.wait("tensor", ta)
                pe_sq = pb.op("tensor", lambda e, o=PS_B[0:1, 512:1024], l=ONESM[:, :],
                              r=SQT, st=(kc == 0), sp=(kc == KC - 1):
                              e.matmul(o, l, r, start=st, stop=sp,
                                       skip_group_check=True), inc="pe")
                sq_tok = pe_sq
                xv_tok[slot] = pe_sq
            tfin, _tfa = ln_finalize(PS_B[0:1, 0:512], PS_B[0:1, 512:1024],
                                     MBC, RBC, MRT, 512, pe_sq, 0)
            for kc in range(KC):
                slot = kc % 2
                pb.wait("sync", xv_tok[slot])
                tl = pb.dma("sync", XPV[slot], D["x0"][kc, :, g0:g0 + 512],
                            f"d_lpx{slot}")
                pb.wait("vector", tl)
                pb.wait("vector", tfin)
                pb.op("vector", lambda e, o=T1, a=XPV[slot], b=RBC:
                      e.tensor_tensor(o, a, b, ALU.mult), inc="dve")
                t2 = pb.op("vector", lambda e,
                           o=XLN[:, kc * S + tau * 512: kc * S + tau * 512 + 512],
                           a=T1, b=MRT:
                           e.tensor_tensor(o, a, b, ALU.subtract), inc="dve")
                xv_tok[slot] = t2

    def attn_heads(lay, s, XLN, bufs):
        QT = bufs["QT"]; KT = bufs["KT"]; VSB = bufs["VSB"]
        TRASH = bufs["TRASH"]; FBC = bufs["FBC"]; AO = bufs["AO"]
        tok_ao_store = None
        tok_wq_free = [None, None]
        tok_wk_free = [None, None]
        tok_wv_free = None
        tok_exp_c1_last = None
        tok_fmult = [None, None, None, None]
        tok_attnv = {}
        tok_exp_c2 = [None, None, None, None]
        for h in range(HEADS):
            slot = h % 2
            if NO_H1_LOADS and h > 0:
                twq = twk = None
            elif lay == "a":
                pb.wait("gpsimd", tok_wq_free[slot])
                twq = pb.dma("gpsimd", bufs["WQ"][slot][:, :], D["wq"][h], f"d_wq{slot}")
                pb.wait("gpsimd", tok_wk_free[slot])
                twk = pb.dma("gpsimd", bufs["WK"][slot][:, :], D["wk"][h], f"d_wk{slot}")
            else:
                pb.wait("gpsimd", tok_wq_free[slot])
                twq = pb.dma("gpsimd", bufs["WQ"][slot][:, :], D["wqu"][h], f"d_wq{slot}")
                pb.wait("gpsimd", tok_wk_free[slot])
                twk = pb.dma("gpsimd", bufs["WK"][slot][:, :], D["kup"][h], f"d_wk{slot}")
            if h % 2 == 0:
                pb.wait("gpsimd", tok_wv_free)
                nkv = KC if lay == "a" else 4
                wsrc3 = D["wv"] if lay == "a" else D["vup"]
                for kc in range(nkv):
                    twv = pb.dma("gpsimd", bufs["WV"][:, kc * 256:(kc + 1) * 256],
                                 wsrc3[kc, :, h * 128:(h + 2) * 128], "d_wv")
            if not (NO_H1_PREP and h > 0):
                prep_step(3)

            if h % 2 == 0:
                tok_ve = [None, None]
                pb.waitc("tensor", "act")
                pb.waitc("tensor", "dve")
                for tm in range(KC):
                    pv = PS_A[:, (tm % 2) * 512:(tm % 2) * 512 + 256]
                    pb.wait("tensor", twv)
                    pb.wait("tensor", tok_ve[tm % 2])
                    specs = []
                    for kc in range(nkv):
                        if lay == "a":
                            l = XLN[:, kc * S + tm * 128: kc * S + tm * 128 + 128]
                        else:
                            l = bufs["CC"][:, kc * S + tm * 128: kc * S + tm * 128 + 128]
                        r = bufs["WV"][:, kc * 256:(kc + 1) * 256]
                        specs.append((pv, l, r, kc == 0, kc == nkv - 1))
                    tv = mm_loop(specs)
                    ov = VSB[:, tm * 256:(tm + 1) * 256]
                    if lay == "a":
                        pb.wait("vector", tv)
                        te = pb.op("vector", lambda e, o=ov, i=pv,
                                   b=bufs["BVBC"][:, h * 128:h * 128 + 256]:
                                   e.scalar_tensor_tensor(o, i, 1.0, b, op0=ALU.mult,
                                                          op1=ALU.add), inc="dve")
                    else:
                        pb.wait("scalar", tv)
                        te = pb.op("scalar", lambda e, o=ov, i=pv:
                                   e.activation(o, i, AF.Copy), inc="act")
                    tok_ve[tm % 2] = te
                tok_wv_free = tok_ve[1]
                tok_v_done = (tok_ve[0], tok_ve[1])

            if h > 0 and H1_STAGE < 1:
                continue
            nq_kc = KC if lay == "a" else 12
            qsrc = XLN if lay == "a" else bufs["CQ"]
            pb.wait("tensor", twq)
            pb.waitc("tensor", "act")
            pb.waitc("tensor", "dve")
            for t_ in tok_v_done:
                pb.wait("tensor", t_)
            specs = []
            for n in range(4):
                for kc in range(nq_kc):
                    specs.append((PS_A[:, n * 512:(n + 1) * 512],
                                  bufs["WQ"][slot][:, kc * 128:(kc + 1) * 128],
                                  qsrc[:, kc * S + n * 512: kc * S + (n + 1) * 512],
                                  kc == 0, kc == nq_kc - 1))
            tq = mm_loop(specs)
            tok_wq_free[slot] = tq
            if lay == "a":
                QF = bufs["QF"]
                pb.wait("scalar", tq)
                pb.waitc("scalar", "dve")
                tqf = pb.op("scalar", lambda e, o=QF[:, :], i=PS_A[:, :],
                            bb=BQ[:, h:h + 1]:
                            e.activation(o, i, AF.Identity, bias=bb),
                            inc="act")
                pb.wait("tensor", tqf)
                pb.waitc("tensor", "dve")
                specs = [(PS_B[:, n * 512:(n + 1) * 512], RT[:, :],
                          QF[:, n * 512:(n + 1) * 512], True, True) for n in range(4)]
                trq = mm_loop(specs)
                pb.wait("vector", trq)
                pb.op("vector", lambda e, o=QF[:, :], a=QF[:, :], b=bufs["TCQ"][:, :]:
                      e.tensor_tensor(o, a, b, ALU.mult), inc="dve")
                pb.op("vector", lambda e, o=PS_B[:, :], a=PS_B[:, :], b=bufs["TSQ"][:, :]:
                      e.tensor_tensor(o, a, b, ALU.mult), inc="dve")
                tqt = pb.op("vector", lambda e, o=QT[:, :], a=QF[:, :], b=PS_B[:, :]:
                            e.tensor_tensor(o, a, b, ALU.add), inc="dve")
            else:
                pb.wait("scalar", tq)
                tqt = pb.op("scalar", lambda e, o=QT[:, :], i=PS_A[:, :]:
                            e.activation(o, i, AF.Copy, scale=float(SCALE)), inc="act")

            if h > 0 and H1_STAGE < 2:
                continue
            nk_kc = KC if lay == "a" else 4
            ksrc = XLN if lay == "a" else bufs["CC"]
            pb.wait("tensor", twk)
            if lay == "a":
                pb.wait("tensor", tqf)
            else:
                pb.wait("tensor", tqt)
            specs = []
            for n in range(4):
                for kc in range(nk_kc):
                    specs.append((PS_A[:, n * 512:(n + 1) * 512],
                                  bufs["WK"][slot][:, kc * 128:(kc + 1) * 128],
                                  ksrc[:, kc * S + n * 512: kc * S + (n + 1) * 512],
                                  kc == 0, kc == nk_kc - 1))
            tk = mm_loop(specs)
            tok_wk_free[slot] = tk
            if lay == "a":
                pb.wait("scalar", tk)
                pb.wait("scalar", tqt)
                tkf = pb.op("scalar", lambda e, o=QF[:, :], i=PS_A[:, :],
                            bb=BK[:, h:h + 1]:
                            e.activation(o, i, AF.Identity, bias=bb),
                            inc="act")
                pb.wait("tensor", tkf)
                pb.wait("tensor", tqt)
                specs = [(PS_B[:, n * 512:(n + 1) * 512], RT[:, :],
                          QF[:, n * 512:(n + 1) * 512], True, True) for n in range(4)]
                trk = mm_loop(specs)
                pb.wait("vector", trk)
                pb.op("vector", lambda e, o=QF[:, :], a=QF[:, :], b=bufs["TCK"][:, :]:
                      e.tensor_tensor(o, a, b, ALU.mult), inc="dve")
                pb.op("vector", lambda e, o=PS_B[:, :], a=PS_B[:, :], b=bufs["TSK"][:, :]:
                      e.tensor_tensor(o, a, b, ALU.mult), inc="dve")
                tkt = pb.op("vector", lambda e, o=KT[:, :], a=QF[:, :], b=PS_B[:, :]:
                            e.tensor_tensor(o, a, b, ALU.add), inc="dve")
                tok_qkt = tkt
            else:
                pb.wait("scalar", tk)
                tkt = pb.op("scalar", lambda e, o=KT[:, :], i=PS_A[:, :]:
                            e.activation(o, i, AF.Copy), inc="act")
                tok_qkt = tkt

            if DEBUG and lay == "a" and s == 0 and h == 0:
                pb.wait("sync", tok_qkt)
                pb.dma("sync", D["dbg_qt"][:, :], QT[:, :], "d_dbg")
                pb.dma("sync", D["dbg_kt"][:, :], KT[:, :], "d_dbg")
            if lay == "b":
                trh = pb.op("vector", lambda e, o=bufs["RH"][:, :], i=bufs["RAMP"][:, :],
                            sl=float(SLOPES[h]):
                            e.tensor_scalar(o, i, sl, None, op0=ALU.mult), inc="dve")
                tqps = pb.op("vector", lambda e, o=QPS[:, :], i=QPOS[:, :],
                             sl=-float(SLOPES[h]):
                             e.tensor_scalar(o, i, sl, None, op0=ALU.mult), inc="dve")
                talb = pb.op("vector", lambda e, o=bufs["ALB"][:, :], i=bufs["DCONST"][:, :],
                             sl=float(SLOPES[h]):
                             e.tensor_scalar(o, i, sl, None, op0=ALU.mult), inc="dve")

            if h > 0 and H1_STAGE < 3:
                continue
            pb.waitc("tensor", "dve")
            pb.waitc("tensor", "act")
            if lay == "a":
                pb.wait("tensor", tok_qkt)
                pb.wait("tensor", tkf)
            else:
                pb.wait("tensor", tok_qkt)
            tok_exp_pp = [None, None]
            tok_exp_qi = None
            for qi in range(16):
                klen = (qi + 1) * 128
                base = PS_A if qi % 2 == 0 else PS_B
                pb.wait("tensor", tok_exp_pp[qi % 2])
                specs = []
                for n in range(0, klen, 512):
                    w = min(512, klen - n)
                    specs.append((base[:, n:n + w], QT[:, qi * 128:(qi + 1) * 128],
                                  KT[:, n:n + w], True, True))
                tsc = mm_loop(specs)
                pb.wait("vector", tsc)
                if lay == "b":
                    pb.wait("vector", trh)
                    pb.wait("vector", tqps)
                    pb.op("vector", lambda e, o=base[:, 0:klen], i=base[:, 0:klen],
                          s1=QPS[:, qi:qi + 1], b=bufs["RH"][:, 0:klen]:
                          e.scalar_tensor_tensor(o, i, s1, b, op0=ALU.add, op1=ALU.add),
                          inc="dve")
                pb.op("vector", lambda e, o=base[:, klen - 128:klen],
                      a=base[:, klen - 128:klen], b=UPM[:, :]:
                      e.tensor_tensor(o, a, b, ALU.add), inc="dve")
                tmx = pb.op("vector", lambda e, o=NEGM[:, qi:qi + 1], i=base[:, 0:klen]:
                            e.reduce_max(o, i, axis=AX.X, negate=True), inc="dve")
                pb.wait("scalar", tmx)
                tok_exp_qi = pb.op("scalar", lambda e, o=TRASH[:, 0:klen],
                                   i=base[:, 0:klen], b=NEGM[:, qi:qi + 1],
                                   a=SACC[:, qi:qi + 1]:
                                   e.activation(o, i, AF.Exp, bias=b, accum_out=a),
                                   inc="act")
                tok_exp_pp[qi % 2] = tok_exp_qi
            tok_exp_c1_last = tok_exp_qi

            if h > 0 and H1_STAGE < 4:
                continue
            pb.wait("scalar", tok_exp_c1_last)
            pb.op("scalar", lambda e, o=LNSB[:, :], i=SACC[:, :]:
                  e.activation(o, i, AF.Ln), inc="act")
            pb.wait("vector", tok_exp_c1_last)
            tnm = pb.op("vector", lambda e, o=NM1[:, :], i=NEGM[:, :]:
                        e.tensor_reduce(o, i, axis=AX.X, op=ALU.min), inc="dve")
            pb.wait("tensor", tnm)
            pb.waitc("tensor", "dve")
            pb.waitc("tensor", "act")
            ttr = pb.op("tensor", lambda e, o=PS_B[0:1, 0:128], l=NM1[:, :],
                        r=IDENT[:, :]:
                        e.matmul(o, l, r, start=True, stop=True,
                                 skip_group_check=True), inc="pe")
            pb.wait("scalar", ttr)
            ttrc = pb.op("scalar", lambda e, o=TROW[:, :], i=PS_B[0:1, 0:128]:
                         e.activation(o, i, AF.Copy), inc="act")
            pb.wait("vector", ttrc)
            tmg = pb.op("vector", lambda e, o=MGR[:, :], i=TROW[:, :]:
                        e.tensor_reduce(o, i, axis=AX.X, op=ALU.min), inc="dve")
            pb.wait("tensor", tmg)
            tbg = pb.op("tensor", lambda e, o=PS_B[:, 512:513], l=ONESK1[:, :],
                        r=MGR[:, :]:
                        e.matmul(o, l, r, start=True, stop=True,
                                 skip_group_check=True), inc="pe")
            pb.wait("scalar", tbg)
            tnmg = pb.op("scalar", lambda e, o=NMG[:, :], i=PS_B[:, 512:513]:
                         e.activation(o, i, AF.Copy), inc="act")
            pb.wait("vector", tnmg)
            tg = pb.op("vector", lambda e, o=GTB[:, :], i=NEGM[:, :], s1=NMG[:, :],
                       b=LNSB[:, :]:
                       e.scalar_tensor_tensor(o, i, s1, b, op0=ALU.subtract,
                                              op1=ALU.subtract), inc="dve")
            if lay == "b":
                tkb = pb.op("vector", lambda e, o=KB[:, :], i=DN16[:, :],
                            sl=float(SLOPES[h]), s2=NMG[:, :]:
                            e.tensor_scalar(o, i, sl, s2, op0=ALU.mult, op1=ALU.add),
                            inc="dve")
            pb.wait("scalar", tg)
            tf = pb.op("scalar", lambda e, o=FQT[:, :], i=GTB[:, :]:
                       e.activation(o, i, AF.Exp), inc="act")
            tfr = None
            trc_toks = [None, None, None, None]
            for qi in range(16):
                pb.wait("tensor", tf)
                pb.wait("tensor", trc_toks[qi % 4])
                ttr2 = pb.op("tensor", lambda e,
                             o=PS_B[0:1, (qi % 4) * 512:(qi % 4) * 512 + 128],
                             l=FQT[:, qi:qi + 1], r=IDENT[:, :]:
                             e.matmul(o, l, r, start=True, stop=True,
                                      skip_group_check=True), inc="pe")
                pb.wait("scalar", ttr2)
                tfr = pb.op("scalar", lambda e, o=FROW[0:1, qi * 128:(qi + 1) * 128],
                            i=PS_B[0:1, (qi % 4) * 512:(qi % 4) * 512 + 128]:
                            e.activation(o, i, AF.Copy), inc="act")
                trc_toks[qi % 4] = tfr
            tfb = None
            for n in range(4):
                pb.wait("tensor", tfr)
                tb_ = pb.op("tensor", lambda e, o=PS_B[:, n * 512:(n + 1) * 512],
                            l=ONESK1[:, :], r=FROW[0:1, n * 512:(n + 1) * 512]:
                            e.matmul(o, l, r, start=True, stop=True,
                                     skip_group_check=True), inc="pe")
                pb.wait("scalar", tb_)
                tfb = pb.op("scalar", lambda e, o=FBC[:, n * 512:(n + 1) * 512],
                            i=PS_B[:, n * 512:(n + 1) * 512]:
                            e.activation(o, i, AF.Copy), inc="act")

            if DEBUG and lay == "a" and s == 0 and h == 0:
                pb.barrier()
                pb.dma("sync", D["dbg_st"][0], NEGM[:, :], "d_dbg")
                pb.dma("sync", D["dbg_st"][1], SACC[:, :], "d_dbg")
                pb.dma("sync", D["dbg_st"][2], FQT[:, :], "d_dbg")
                pb.dma("sync", D["dbg_nmg"][:, :], NMG[:, :], "d_dbg")
                pb.dma("sync", D["dbg_fbc"][:, :], FBC[:, :], "d_dbg")
                pb.barrier()
            if h > 0 and H1_STAGE < 5:
                continue
            if lay == "b":
                pb.wait("scalar", tkb)
            pb.waitc("tensor", "act")
            for j in range(4):
                ov = PS_B[:, 1024 + (j % 2) * 512: 1024 + (j % 2) * 512 + 512]
                tattnv = None
                for kc in range(4 * j + 4):
                    jjs = max(0, kc - 4 * j)
                    w = 512 - jjs * 128
                    q0 = j * 512 + jjs * 128
                    par = kc % 4
                    pv = PS_A[:, par * 512 + jjs * 128: par * 512 + 512]
                    pb.wait("tensor", tok_exp_c2[par])
                    if kc == 0:
                        pb.wait("tensor", tfb)
                        pb.wait("vector", tfb)
                        pb.wait("tensor", tok_fmult[(j + 2) % 4])
                    tst = pb.op("tensor", lambda e, o=pv,
                                l=KT[:, kc * 128:(kc + 1) * 128],
                                r=QT[:, q0: (j + 1) * 512]:
                                e.matmul(o, l, r, start=True, stop=True,
                                         skip_group_check=True), inc="pe")
                    tmk = None
                    if lay == "b":
                        pb.wait("vector", tst)
                        pb.wait("vector", talb)
                        tmk = pb.op("vector", lambda e, o=pv, a=pv, b=bufs["ALB"][:, 0:w]:
                                    e.tensor_tensor(o, a, b, ALU.add), inc="dve")
                    if kc >= 4 * j:
                        dg = PS_A[:, par * 512 + jjs * 128: par * 512 + jjs * 128 + 128]
                        pb.wait("vector", tst)
                        tmk = pb.op("vector", lambda e, o=dg, a=dg, b=LOWM[:, :]:
                                    e.tensor_tensor(o, a, b, ALU.add), inc="dve")
                    pb.wait("scalar", tst)
                    pb.wait("scalar", tmk)
                    pb.wait("scalar", tok_attnv.get(par))
                    delta = 4 * j + jjs - kc
                    bias_ap = NMG[:, :] if lay == "a" else KB[:, delta:delta + 1]
                    te2 = pb.op("scalar", lambda e, o=ATN[par][:, 0:w], i=pv, b=bias_ap:
                                e.activation(o, i, AF.Exp, bias=b), inc="act")
                    tok_exp_c2[par] = te2
                    pb.wait("tensor", te2)
                    vl = VSB[:, kc * 256 + (h % 2) * 128: kc * 256 + (h % 2) * 128 + 128]
                    tattnv = pb.op("tensor", lambda e, o=ov[:, jjs * 128:512], l=vl,
                                   r=ATN[par][:, 0:w], st=(kc == 0), sp=(kc == 4 * j + 3):
                                   e.matmul(o, l, r, start=st, stop=sp,
                                            skip_group_check=True), inc="pe")
                    tok_attnv[par] = tattnv
                pb.wait("vector", tattnv)
                pb.wait("vector", tok_ao_store)
                tfm = pb.op("vector", lambda e, o=AO[:, j * 512:(j + 1) * 512], a=ov,
                            b=FBC[:, j * 512:(j + 1) * 512]:
                            e.tensor_tensor(o, a, b, ALU.mult), inc="dve")
                tok_fmult[j] = tfm
            if DEBUG and lay == "a" and s == 0 and h == 0:
                pb.barrier()
                pb.dma("sync", D["dbg_ao0"][:, :], AO[:, :], "d_dbg")
                pb.barrier()
            pb.wait("sync", tok_fmult[3])
            tok_ao_store = pb.dma("sync", D["ao"][s, h], AO[:, :], "d_ao")
        pb.wait("sync", tok_ao_store)

    def dense_epilogue(sc, pfx, s, wsrc_name, bias, xprev, xnext, xlnout):
        AOB = sc.sb(f"{pfx}_aob", (128, KC * 512), BF16)
        WDB = [sc.sb(f"{pfx}_wdb{i}", (128, H), BF16) for i in range(2)]
        XPV = [sc.sb(f"{pfx}_xpv{i}", (128, 512), F32) for i in range(2)]
        XNW = [sc.sb(f"{pfx}_xnw{i}", (128, 512), F32) for i in range(2)]
        SQT = sc.sb(f"{pfx}_sqt", (128, 512), F32)
        MBC = sc.sb(f"{pfx}_mbc", (128, 512), F32)
        RBC = sc.sb(f"{pfx}_rbc", (128, 512), F32)
        MRT = sc.sb(f"{pfx}_mrt", (128, 512), F32)
        T1 = sc.sb(f"{pfx}_t1", (128, 512), F32)
        XLT = sc.sb(f"{pfx}_xlt", (128, 512), BF16)
        wsrc = D[wsrc_name]
        hbm_fence()
        tok_w = [None, None]
        tok_ev = [None, None]
        tok_store = [None, None]
        tok_stats = [None, None]
        tok_xp = [None, None]
        tok_xl = None
        aob_tok = {}

        def emit_aob_load(tau):
            if tau >= 4:
                return
            pb.wait("sync", pb.tok("pe"))
            for kc in range(KC):
                t_ = pb.dma("sync", AOB[:, kc * 512:(kc + 1) * 512],
                            D["ao"][s, kc, :, tau * 512:tau * 512 + 512], "d_aob")
            aob_tok[tau] = t_

        emit_aob_load(0)
        for tau in range(4):
            g0 = s * S + tau * 512
            taob = aob_tok[tau]
            pe_sq = None
            sq_tok = None
            for f in range(KC):
                slot = f % 2
                pb.wait("gpsimd", tok_w[slot])
                tw = pb.dma("gpsimd", WDB[slot][:, :], wsrc[f], f"d_de{slot}")
                prep_step(1)
                pb.wait("tensor", tw)
                pb.wait("tensor", taob)
                pb.wait("tensor", tok_ev[slot])
                specs = [(PS_A[:, slot * 512:(slot + 1) * 512],
                          WDB[slot][:, kc * 128:(kc + 1) * 128],
                          AOB[:, kc * 512:(kc + 1) * 512], kc == 0, kc == KC - 1)
                         for kc in range(KC)]
                td = mm_loop(specs)
                tok_w[slot] = td
                pb.wait("sync", tok_ev[slot])
                txp = pb.dma("sync", XPV[slot][:, :], D[xprev][f, :, g0:g0 + 512],
                             f"d_dxp{slot}")
                pb.wait("vector", td)
                pb.wait("vector", txp)
                pb.wait("vector", tok_store[slot])
                pb.wait("vector", tok_stats[slot])
                if bias is not None:
                    txn = pb.op("vector", lambda e, o=XNW[slot][:, :],
                                i=PS_A[:, slot * 512:(slot + 1) * 512],
                                s1=bias[:, f:f + 1], b=XPV[slot][:, :]:
                                e.scalar_tensor_tensor(o, i, s1, b, op0=ALU.add,
                                                       op1=ALU.add), inc="dve")
                else:
                    txn = pb.op("vector", lambda e, o=XNW[slot][:, :],
                                i=PS_A[:, slot * 512:(slot + 1) * 512],
                                b=XPV[slot][:, :]:
                                e.tensor_tensor(o, i, b, ALU.add), inc="dve")
                tok_ev[slot] = txn
                pb.wait("sync", txn)
                tst = pb.dma("sync", D[xnext][f, :, g0:g0 + 512], XNW[slot][:, :],
                             f"d_dxn{slot}")
                tok_store[slot] = tst
                if xlnout is not None:
                    pb.wait("tensor", txn)
                    pb.op("tensor", lambda e, o=PS_B[0:1, 0:512], l=ONESM[:, :],
                          r=XNW[slot][:, :], st=(f == 0), sp=(f == KC - 1):
                          e.matmul(o, l, r, start=st, stop=sp, skip_group_check=True))
                    pb.wait("scalar", txn)
                    pb.wait("scalar", sq_tok)
                    ta = pb.op("scalar", lambda e, o=SQT[:, :], i=XNW[slot][:, :]:
                               e.activation(o, i, AF.Square), inc="act")
                    pb.wait("tensor", ta)
                    pe_sq = pb.op("tensor", lambda e, o=PS_B[0:1, 512:1024],
                                  l=ONESM[:, :], r=SQT[:, :],
                                  st=(f == 0), sp=(f == KC - 1):
                                  e.matmul(o, l, r, start=st, stop=sp,
                                           skip_group_check=True), inc="pe")
                    sq_tok = pe_sq
                    tok_stats[slot] = pe_sq
            emit_aob_load(tau + 1)
            if xlnout is not None:
                tfin, _tfa = ln_finalize(PS_B[0:1, 0:512], PS_B[0:1, 512:1024],
                                         MBC, RBC, MRT, 512, pe_sq, 1024)
                pb.waitc("sync", "d_dxn0")
                pb.waitc("sync", "d_dxn1")
                for f in range(KC):
                    slot = f % 2
                    pb.wait("sync", tok_xp[slot])
                    tl = pb.dma("sync", XPV2 := XPV[slot][:, :],
                                D[xnext][f, :, g0:g0 + 512], f"d_dx2{slot}")
                    pb.wait("vector", tl)
                    pb.wait("vector", tfin)
                    pb.wait("vector", tok_xl)
                    pb.op("vector", lambda e, o=T1[:, :], a=XPV2, b=RBC[:, :]:
                          e.tensor_tensor(o, a, b, ALU.mult), inc="dve")
                    t2 = pb.op("vector", lambda e, o=XLT[:, :], a=T1[:, :],
                               b=MRT[:, :]:
                               e.tensor_tensor(o, a, b, ALU.subtract), inc="dve")
                    tok_xp[slot] = t2
                    pb.wait("sync", t2)
                    tok_xl = pb.dma("sync", D[xlnout][f, :, g0:g0 + 512], XLT[:, :],
                                    "d_dxl")

    def ffn(sc, pfx, xln_src, xprev, xnext, xlnout):
        HDN = sc.sb(f"{pfx}_hdn", (128, 44 * 512), BF16)
        XLT = [sc.sb(f"{pfx}_xlt{i}", (128, KC * 512), BF16) for i in range(2)]
        GWB = [sc.sb(f"{pfx}_gwb{i}", (128, H), BF16) for i in range(2)]
        UWB = [sc.sb(f"{pfx}_uwb{i}", (128, H), BF16) for i in range(2)]
        DWB = [sc.sb(f"{pfx}_dwb{i}", (128, INTER), BF16) for i in range(2)]
        SG = [sc.sb(f"{pfx}_sg{i}", (128, 512), F32) for i in range(2)]
        XPV = [sc.sb(f"{pfx}_xpv{i}", (128, 512), F32) for i in range(2)]
        XNW = [sc.sb(f"{pfx}_xnw{i}", (128, 512), F32) for i in range(2)]
        SQT = sc.sb(f"{pfx}_sqt", (128, 512), F32)
        MBC = sc.sb(f"{pfx}_mbc", (128, 512), F32)
        RBC = sc.sb(f"{pfx}_rbc", (128, 512), F32)
        MRT = sc.sb(f"{pfx}_mrt", (128, 512), F32)
        T1 = sc.sb(f"{pfx}_t1", (128, 512), F32)
        XLO = sc.sb(f"{pfx}_xlo", (128, 512), BF16)
        hbm_fence()
        pb.waitc("gpsimd", "d_preps0")
        pb.waitc("gpsimd", "d_preps1")
        tok_g = [None, None]; tok_u = [None, None]; tok_d = [None, None]
        tok_sg = [None, None]; tok_hd = [None, None]
        tok_ev = [None, None]
        tok_store = [None, None]; tok_stats = [None, None]
        tok_xp = [None, None]
        tok_xlt_use = [None, None]
        tok_xl = None
        for tt in range(8):
            t0 = tt * 512
            xslot = tt % 2
            pb.wait("sync", tok_xlt_use[xslot])
            for kc in range(KC):
                txl = pb.dma("sync", XLT[xslot][:, kc * 512:(kc + 1) * 512],
                             D[xln_src][kc, :, t0:t0 + 512], f"d_fxlt{xslot}")
            for m in range(44):
                slot = m % 2
                pb.wait("gpsimd", tok_g[slot])
                tgw = pb.dma("gpsimd", GWB[slot][:, :], D["gwb"][m], f"d_fg{slot}")
                pb.wait("scalar", tok_u[slot])
                tuw = pb.dma("scalar", UWB[slot][:, :], D["uwb"][m], f"d_fu{slot}")
                gps = PS_A[:, slot * 1024: slot * 1024 + 512]
                ups = PS_A[:, slot * 1024 + 512: slot * 1024 + 1024]
                pb.wait("tensor", tgw)
                pb.wait("tensor", txl)
                pb.wait("tensor", tok_sg[slot])
                specs = [(gps, GWB[slot][:, kc * 128:(kc + 1) * 128],
                          XLT[xslot][:, kc * 512:(kc + 1) * 512],
                          kc == 0, kc == KC - 1) for kc in range(KC)]
                tg = mm_loop(specs)
                tok_g[slot] = tg
                pb.wait("scalar", tg)
                tsg = pb.op("scalar", lambda e, o=SG[slot][:, :], i=gps,
                            s1=GS[:, m:m + 1]:
                            e.activation(o, i, AF.Silu, scale=s1), inc="act")
                tok_sg[slot] = tsg
                pb.wait("tensor", tuw)
                pb.wait("tensor", tok_hd[slot])
                specs = [(ups, UWB[slot][:, kc * 128:(kc + 1) * 128],
                          XLT[xslot][:, kc * 512:(kc + 1) * 512],
                          kc == 0, kc == KC - 1) for kc in range(KC)]
                tu = mm_loop(specs)
                tok_u[slot] = tu
                pb.wait("vector", tu)
                pb.wait("vector", tsg)
                thd = pb.op("vector", lambda e, o=HDN[:, m * 512:(m + 1) * 512],
                            i=ups, s1=US[:, m:m + 1], b=SG[slot][:, :]:
                            e.scalar_tensor_tensor(o, i, s1, b, op0=ALU.mult,
                                                   op1=ALU.mult), inc="dve")
                tok_hd[slot] = thd
                prep_step(1)
            tok_xlt_use[xslot] = pb.tok("pe")
            tok_hd_all = pb.tok("dve")
            pe_sq = None
            sq_tok = None
            for f in range(KC):
                slot = f % 2
                pb.wait("scalar", tok_d[slot])
                tdw = pb.dma("scalar", DWB[slot][:, :], D["dwb"][f], f"d_fd{slot}")
                pvw = PS_A[:, slot * 512:(slot + 1) * 512]
                pb.wait("tensor", tdw)
                pb.wait("tensor", tok_hd_all)
                pb.wait("tensor", tok_ev[slot])
                specs = [(pvw, DWB[slot][:, m * 128:(m + 1) * 128],
                          HDN[:, m * 512:(m + 1) * 512], m == 0, m == 43)
                         for m in range(44)]
                td = mm_loop(specs)
                tok_d[slot] = td
                pb.wait("sync", tok_ev[slot])
                txp = pb.dma("sync", XPV[slot][:, :], D[xprev][f, :, t0:t0 + 512],
                             f"d_fxp{slot}")
                pb.wait("vector", td)
                pb.wait("vector", txp)
                pb.wait("vector", tok_store[slot])
                pb.wait("vector", tok_stats[slot])
                txn = pb.op("vector", lambda e, o=XNW[slot][:, :], i=pvw,
                            s1=DS[:, f:f + 1], b=XPV[slot][:, :]:
                            e.scalar_tensor_tensor(o, i, s1, b, op0=ALU.mult,
                                                   op1=ALU.add), inc="dve")
                tok_ev[slot] = txn
                pb.wait("sync", txn)
                tst = pb.dma("sync", D[xnext][f, :, t0:t0 + 512], XNW[slot][:, :],
                             f"d_fxn{slot}")
                tok_store[slot] = tst
                if xlnout is not None:
                    pb.wait("tensor", txn)
                    pb.op("tensor", lambda e, o=PS_B[0:1, 0:512], l=ONESM[:, :],
                          r=XNW[slot][:, :], st=(f == 0), sp=(f == KC - 1):
                          e.matmul(o, l, r, start=st, stop=sp, skip_group_check=True))
                    pb.wait("scalar", txn)
                    pb.wait("scalar", sq_tok)
                    ta = pb.op("scalar", lambda e, o=SQT[:, :], i=XNW[slot][:, :]:
                               e.activation(o, i, AF.Square), inc="act")
                    pb.wait("tensor", ta)
                    pe_sq = pb.op("tensor", lambda e, o=PS_B[0:1, 512:1024],
                                  l=ONESM[:, :], r=SQT[:, :],
                                  st=(f == 0), sp=(f == KC - 1):
                                  e.matmul(o, l, r, start=st, stop=sp,
                                           skip_group_check=True), inc="pe")
                    sq_tok = pe_sq
                    tok_stats[slot] = pe_sq
            if xlnout is not None:
                tfin, tfin_act = ln_finalize(PS_B[0:1, 0:512], PS_B[0:1, 512:1024],
                                             MBC, RBC, MRT, 512, pe_sq, 1024)
                tok_sg = [tfin_act, tfin_act]
                tok_hd = [tfin, tfin]
                tok_ev = list(tok_ev)
                pb.waitc("sync", "d_fxn0")
                pb.waitc("sync", "d_fxn1")
                for f in range(KC):
                    slot = f % 2
                    pb.wait("sync", tok_xp[slot])
                    tl = pb.dma("sync", XPV[slot][:, :],
                                D[xnext][f, :, t0:t0 + 512], f"d_fx2{slot}")
                    pb.wait("vector", tl)
                    pb.wait("vector", tfin)
                    pb.wait("vector", tok_xl)
                    pb.op("vector", lambda e, o=T1[:, :],
                          a=XPV[slot][:, :], b=RBC[:, :]:
                          e.tensor_tensor(o, a, b, ALU.mult), inc="dve")
                    t2 = pb.op("vector", lambda e, o=XLO[:, :], a=T1[:, :],
                               b=MRT[:, :]:
                               e.tensor_tensor(o, a, b, ALU.subtract), inc="dve")
                    tok_xp[slot] = t2
                    pb.wait("sync", t2)
                    tok_xl = pb.dma("sync", D[xlnout][f, :, t0:t0 + 512],
                                    XLO[:, :], "d_fxl")
            if xlnout is None:
                tok_sg = list(tok_ev)
                tok_hd = list(tok_ev)

    def attn_b_ccq(sc, pfx, s, CQ, CC, xln_src):
        scl = Scope()
        XLT = scl.sb(f"{pfx}_xlt", (128, KC * 512), BF16)
        WB = [scl.sb(f"{pfx}_wb{i}", (128, H), BF16) for i in range(2)]
        hbm_fence()
        tok_w = [None, None]
        tok_xlt = None
        tok_ev = [None, None]
        for tblk in range(4):
            g0 = s * S + tblk * 512
            pb.wait("sync", tok_xlt)
            pb.wait("sync", pb.tok("pe"))
            for kc in range(KC):
                txl = pb.dma("sync", XLT[:, kc * 512:(kc + 1) * 512],
                             D[xln_src][kc, :, g0:g0 + 512], "d_cbx")
            tok_xlt = txl
            for mb in range(16):
                is_c = mb >= 12
                wsrc = D["wkd"][mb - 12] if is_c else D["wqd"][mb]
                slot = mb % 2
                pb.wait("gpsimd", tok_w[slot])
                tw = pb.dma("gpsimd", WB[slot][:, :], wsrc, f"d_cb{slot}")
                prep_step(1)
                pb.wait("tensor", tw)
                pb.wait("tensor", txl)
                pb.wait("tensor", tok_ev[slot])
                specs = [(PS_A[:, slot * 512:(slot + 1) * 512],
                          WB[slot][:, kc * 128:(kc + 1) * 128],
                          XLT[:, kc * 512:(kc + 1) * 512], kc == 0, kc == KC - 1)
                         for kc in range(KC)]
                tq = mm_loop(specs)
                tok_w[slot] = tq
                if is_c:
                    dst = CC[:, (mb - 12) * S + tblk * 512: (mb - 12) * S + tblk * 512 + 512]
                else:
                    dst = CQ[:, mb * S + tblk * 512: mb * S + tblk * 512 + 512]
                pb.wait("scalar", tq)
                tev = pb.op("scalar", lambda e, o=dst,
                            i=PS_A[:, slot * 512:(slot + 1) * 512]:
                            e.activation(o, i, AF.Copy), inc="act")
                tok_ev[slot] = tev
        pb.waitc("tensor", "act")
        scl.close()

    pb.barrier()

    sc1 = Scope()
    XLN = sc1.sb("a_xln", (128, KC * S), BF16)
    bufs_a = dict(
        QF=sc1.sb("a_qf", (128, S), F32),
        QT=sc1.sb("a_qt", (128, S), BF16),
        KT=sc1.sb("a_kt", (128, S), BF16),
        VSB=sc1.sb("a_vsb", (128, KC * 256), BF16),
        FBC=sc1.sb("a_fbc", (128, S), F32),
        AO=sc1.sb("a_ao", (128, S), BF16),
        WQ=[sc1.sb(f"a_wq{i}", (128, H), BF16) for i in range(2)],
        WK=[sc1.sb(f"a_wk{i}", (128, H), BF16) for i in range(2)],
        WV=sc1.sb("a_wv", (128, KC * 256), BF16),
        BVBC=sc1.sb("a_bvbc", (128, H), F32),
        TCQ=sc1.sb("a_tcq", (128, S), F32),
        TSQ=sc1.sb("a_tsq", (128, S), F32),
        TCK=sc1.sb("a_tck", (128, S), F32),
        TSK=sc1.sb("a_tsk", (128, S), F32),
    )
    bufs_a["TRASH"] = bufs_a["QF"]
    for dst, srcn in [(bufs_a["TCQ"], "tbl_cq"), (bufs_a["TSQ"], "tbl_sq"),
                      (bufs_a["TCK"], "tbl_ck"), (bufs_a["TSK"], "tbl_sk")]:
        pb.dma("sync", dst[:, :], D[srcn][:, :], "d_const")
    tok_const = pb.tok("d_const")
    for e in ("tensor", "vector", "scalar"):
        pb.wait(e, tok_const)
    for n in range(4):
        pb.wait("tensor", tok_const)
        tb_ = pb.op("tensor", lambda e, o=PS_B[:, n * 512:(n + 1) * 512],
                    l=ONESK1[:, :], r=BVROW[0:1, n * 512:(n + 1) * 512]:
                    e.matmul(o, l, r, start=True, stop=True, skip_group_check=True),
                    inc="pe")
        pb.wait("scalar", tb_)
        pb.op("scalar", lambda e, o=bufs_a["BVBC"][:, n * 512:(n + 1) * 512],
              i=PS_B[:, n * 512:(n + 1) * 512]:
              e.activation(o, i, AF.Copy), inc="act")
    pb.waitc("tensor", "act")

    for s in range(NS):
        ln_prep(sc1, f"lp{s}", s, XLN, bufs_a["QF"], bufs_a["FBC"])
        pb.barrier()
        if DEBUG and s == 0:
            for kc in range(KC):
                pb.dma("sync", D["dbg_xln"][kc], XLN[:, kc * S:(kc + 1) * S], "d_dbg")
            pb.barrier()
        attn_heads("a", s, XLN, bufs_a)
        pb.barrier()
    if DEBUG:
        for hh in range(NH):
            pb.dma("sync", D["dbg_ao"][hh], D["ao"][0, hh], "d_dbg")
        pb.barrier()
    sc1.close()

    sc1d = Scope()
    for s in range(NS):
        dense_epilogue(sc1d, f"d1{s}", s, "wd", BD, "x0",
                       "xA" if SUBLAYERS > 1 else "out",
                       "lnA" if SUBLAYERS > 1 else None)
        pb.barrier()
    sc1d.close()

    if SUBLAYERS > 1:
        sc2 = Scope()
        prep_finish()
        pb.barrier()
        ffn(sc2, "f1", "lnA", "xA", "xB" if SUBLAYERS > 2 else "out",
            "lnB" if SUBLAYERS > 2 else None)
        pb.barrier()
        sc2.close()

    if SUBLAYERS > 2:
        sc3 = Scope()
        CQ = sc3.sb("b_cq", (128, 12 * S), BF16)
        CC = sc3.sb("b_cc", (128, 4 * S), BF16)
        bufs_b = dict(
            QT=sc3.sb("b_qt", (128, S), BF16),
            KT=sc3.sb("b_kt", (128, S), BF16),
            VSB=sc3.sb("b_vsb", (128, KC * 256), BF16),
            FBC=sc3.sb("b_fbc", (128, S), F32),
            RAMP=sc3.sb("b_ramp", (128, S), F32),
            AO=sc3.sb("b_ao", (128, S), BF16),
            WQ=[sc3.sb(f"b_wq{i}", (128, RQ), BF16) for i in range(2)],
            WK=[sc3.sb(f"b_wk{i}", (128, RKV), BF16) for i in range(2)],
            WV=sc3.sb("b_wv", (128, 4 * 256), BF16),
            RH=sc3.sb("b_rh", (128, S), F32),
            ALB=sc3.sb("b_alb", (128, 512), F32),
            DCONST=sc3.sb("b_dconst", (128, 512), F32),
            CQ=CQ, CC=CC,
        )
        bufs_b["TRASH"] = bufs_b["FBC"]
        pb.dma("sync", bufs_b["RAMP"][:, :], D["ramp"][:, :], "d_const")
        tok_r = pb.dma("sync", bufs_b["DCONST"][:, :], D["dconst"][:, :], "d_const")
        pb.wait("vector", tok_r)
        for s in range(NS):
            attn_b_ccq(sc3, f"cb{s}", s, CQ, CC, "lnB")
            pb.barrier()
            attn_heads("b", s, None, bufs_b)
            pb.barrier()
        sc3.close()
        sc3d = Scope()
        for s in range(NS):
            dense_epilogue(sc3d, f"d3{s}", s, "wo", None, "xB",
                           "xA" if SUBLAYERS > 3 else "out",
                           "lnA" if SUBLAYERS > 3 else None)
            pb.barrier()
        sc3d.close()

    if SUBLAYERS > 3:
        sc4 = Scope()
        ffn(sc4, "f2", "lnA", "xA", "out", None)
        pb.barrier()
        sc4.close()

    for sem in sorted(pb.cnt):
        if sem.startswith("d_") and pb.cnt[sem] > 0:
            pb.wait("sync", (sem, pb.cnt[sem]))

    with ExitStack() as st:
        sems = {}
        for name in pb.cnt:
            assert pb.cnt[name] < 60000, (name, pb.cnt[name])
            sems[name] = st.enter_context(nc.semaphore(name))
        block = st.enter_context(nc.Block())
        for eng in ENGS:
            def make(items):
                def f(e):
                    last = {}
                    for it in items:
                        if it[0] == "wait":
                            _, s_, v = it
                            if last.get(s_, -1) >= v:
                                continue
                            e.wait_ge(sems[s_], v)
                            last[s_] = v
                        else:
                            _, fn, inc, n = it
                            ins = fn(e)
                            if inc is not None:
                                ins.then_inc(sems[inc], n)
                return f
            getattr(block, eng)(make(pb.q[eng]))
    return nc


def kernel(**inputs):
    cache = {}
    in_maps = [_host_inputs(inputs, c, cache) for c in range(8)]
    nc = build_program()
    res = run_bass_kernel_spmd(nc, in_maps, core_ids=list(range(8)))
    out = np.empty((16, S, H), np.float32)
    for c in range(8):
        o = np.asarray(res.results[c]["out"])
        o = o.reshape(KC, 128, NS, S).transpose(2, 3, 0, 1).reshape(NS, S, H)
        out[2 * c: 2 * c + 2] = o
    return out
